# version 01
# baseline (speedup 1.0000x reference)
import contextlib
import dataclasses
import math
import numpy as np
import ml_dtypes

import concourse.bass as bass
import concourse.mybir as mybir
from concourse.bass_utils import run_bass_kernel_spmd

F32 = mybir.dt.float32
BF16 = mybir.dt.bfloat16
AF = mybir.ActivationFunctionType
ALU = mybir.AluOpType
NBF = ml_dtypes.bfloat16

B, S, D, H = 4, 4096, 1024, 16
HD = 64
SH = S // 2
LB = 1408
DMIN = -512
WT1 = LB + 1


def _rel_bucket(d):
    d = np.asarray(d)
    rp_f = np.maximum(d.astype(np.float64), 1.0)
    large = 16 + (np.log(rp_f / 16.0) / np.log(8.0) * 16.0).astype(np.int64)
    return np.where(d < 16, d, np.minimum(large, 31)).astype(np.int64)

def _host_consts():
    c = {}
    eye = np.eye(128, dtype=np.float32)
    c["ident32"] = eye
    c["identb"] = eye.astype(NBF)
    inv_freq = 1.0 / (10000.0 ** (np.arange(0, HD, 2, dtype=np.float64) / HD))
    freqs = np.outer(np.arange(S, dtype=np.float64), inv_freq)
    emb = np.concatenate([freqs, freqs], axis=-1)
    cos = np.cos(emb).T.astype(np.float32)
    sin = np.sin(emb).T.astype(np.float32)
    c["cosb"] = np.ascontiguousarray(np.concatenate([cos, cos], 0)).astype(NBF)
    c["sinb"] = np.ascontiguousarray(np.concatenate([sin, sin], 0)).astype(NBF)
    R = np.zeros((64, 64), np.float32)
    for i in range(32):
        R[i, 32 + i] = -1.0
        R[32 + i, i] = 1.0
    rotT = np.zeros((128, 128), np.float32)
    rotT[0:64, 0:64] = R.T
    rotT[64:128, 64:128] = R.T
    c["rotT"] = rotT.astype(NBF)
    ohcorr = np.zeros((32, LB), np.float32)
    m0 = np.zeros((16, LB), np.float32)
    for j in range(LB):
        d = j + DMIN
        if d >= 0:
            m0[:, j] = 1.0
        if 0 <= d <= 112:
            ohcorr[int(_rel_bucket(d)), j] += 1.0
            ohcorr[31, j] -= 1.0
    c["ohcorr"] = ohcorr
    c["m0rep"] = m0
    hrep = np.zeros((16, 2048), np.float32)
    for h in range(16):
        hrep[h, 128 * h:128 * (h + 1)] = 1.0
    c["hrep8"] = hrep
    c["lt64"] = np.triu(np.ones((64, 64), np.float32))
    ue = np.ones((128, 512), np.float32)
    tri = (np.arange(64)[:, None] <= np.arange(64)[None, :]).astype(np.float32)
    for blk in range(8):
        ue[0:64, 64 * blk:64 * blk + 64] = tri
    c["upperext"] = ue.astype(NBF)
    hs2 = np.zeros((128, 2), np.float32)
    hs2[0:64, 0] = 1.0
    hs2[64:128, 1] = 1.0
    c["hsum2"] = hs2.astype(NBF)
    bsel = np.zeros((2, 128), np.float32)
    bsel[0, 0:64] = 1.0
    bsel[1, 64:128] = 1.0
    c["blocksel"] = bsel
    e16 = np.zeros((16, 1024), np.float32)
    for h in range(16):
        e16[h, 64 * h:64 * h + 64] = 1.0
    c["e16"] = e16.astype(NBF)
    psel = np.zeros((16, 1024), np.float32)
    for p in range(8):
        psel[2 * p, 128 * p:128 * p + 64] = 1.0
        psel[2 * p + 1, 128 * p + 64:128 * p + 128] = 1.0
    c["pairsel"] = psel
    i64 = np.zeros((64, 1024), np.float32)
    for h in range(16):
        i64[:, 64 * h:64 * h + 64] = np.eye(64)
    c["i64rep"] = i64.astype(NBF)
    c["fconst"] = np.repeat(np.array([[0.0, 1.0, -1.0, 0.125, 1.0 / 64, 1.0 / 1024, 1e-5,
                                       1e-6, 1e-12, 0.25, 0.5, 2.0, math.sqrt(2.0), 0.0,
                                       0.0, 0.0]], np.float32), 128, axis=0)
    return c

def _tile_wT(w, mt, kt):
    wT = np.ascontiguousarray(w.T)
    return np.ascontiguousarray(wT.reshape(kt, 128, mt, 128).transpose(2, 0, 1, 3))

def _prep_inputs(inp):
    consts = _host_consts()
    x = np.ascontiguousarray(np.asarray(inp["L0_attn_a_x"], dtype=np.float32))
    Wq = np.asarray(inp["L0_attn_a_Wq"], np.float32)
    Wk = np.asarray(inp["L0_attn_a_Wk"], np.float32)
    Wv = np.asarray(inp["L0_attn_a_Wv"], np.float32)
    Wo = np.asarray(inp["L0_attn_a_Wo"], np.float32)
    rel = np.asarray(inp["L0_attn_a_rel_emb"], np.float32)
    sh = dict(consts)
    sh["wbq"] = np.asarray(inp["L0_attn_b_Wq"], np.float32)
    sh["wbk"] = np.asarray(inp["L0_attn_b_Wk"], np.float32)
    sh["wbv"] = np.asarray(inp["L0_attn_b_Wv"], np.float32)
    sh["wfb"] = np.ascontiguousarray(
        np.concatenate([np.asarray(inp["L0_attn_b_Wf"], np.float32),
                        np.asarray(inp["L0_attn_b_Wbet"], np.float32)], axis=1))
    sh["wbo"] = np.asarray(inp["L0_attn_b_Wo"], np.float32)
    sh["gw1"] = np.asarray(inp["L0_attn_b_gate_w1"], np.float32)
    sh["gw2"] = np.asarray(inp["L0_attn_b_gate_w2"], np.float32)
    sh["delta"] = np.asarray(inp["L0_attn_b_delta"], np.float32).reshape(1, 16)
    sh["lambp"] = np.ascontiguousarray(
        np.asarray(inp["L0_attn_b_lamb_params"], np.float32).reshape(8, 128).T)
    sh["nrmw"] = np.ascontiguousarray(
        np.asarray(inp["L0_attn_b_norm_weight"], np.float32).reshape(8, 128).T)
    sh["w1t"] = _tile_wT(np.asarray(inp["L0_ffn_w1_w"], np.float32), 32, 8)
    sh["w2t"] = _tile_wT(np.asarray(inp["L0_ffn_w2_w"], np.float32), 32, 8)
    sh["h1t"] = _tile_wT(np.asarray(inp["L0_ffn_h1_w"], np.float32), 32, 32)
    sh["h2t"] = _tile_wT(np.asarray(inp["L0_ffn_h2_w"], np.float32), 32, 32)
    sh["w3t"] = _tile_wT(np.asarray(inp["L0_ffn_w3_w"], np.float32), 8, 32)
    for nm, src in [("s1c", "L0_ffn_w1_s"), ("s2c", "L0_ffn_w2_s"),
                    ("sh1c", "L0_ffn_h1_s"), ("sh2c", "L0_ffn_h2_s")]:
        sh[nm] = np.ascontiguousarray(np.asarray(inp[src], np.float32).reshape(32, 128).T)
    sh["s3c"] = np.ascontiguousarray(np.asarray(inp["L0_ffn_w3_s"], np.float32).reshape(8, 128).T)
    maps = []
    for c in range(8):
        b, r = c // 2, c % 2
        m = dict(sh)
        m["x_full"] = x[b]
        m["x_own"] = np.ascontiguousarray(x[b, 2048 * r:2048 * (r + 1)])
        m["wqm"] = Wq
        m["wkm"] = Wk
        m["wvm"] = Wv
        m["wom"] = Wo
        m["rel16"] = rel
        m["gam"] = np.array([[1.0 - r, float(r)]], np.float32)
        gs = np.zeros((1, 4), np.float32)
        if r == 1:
            gs[0, b] = 1.0
        m["gsel"] = gs
        maps.append(m)
    return maps


ENGINES = ("sync", "vector", "scalar", "tensor", "gpsimd")
SY, VE, SC, TE, GP = ENGINES[0], ENGINES[1], ENGINES[2], ENGINES[3], ENGINES[4]

class Sched:
    def __init__(self, nc, es):
        self.nc = nc
        self.es = es
        self.ops = {e: [] for e in ENGINES}
        self.sems = {}
        self.counts = {}
        self.pending = {e: None for e in ENGINES}
        self.phase = ""
        self.serial = False
        self.last_tok = None
        self.bank_rd = [None] * 8

    def _sem(self, name):
        if name not in self.sems:
            self.sems[name] = self.es.enter_context(self.nc.semaphore(name))
            self.counts[name] = 0
        return self.sems[name]

    def do(self, eng, fn, waits=(), sig=None, n=1):
        import traceback
        _st = traceback.extract_stack()
        _lbl = ";".join(f"{f.lineno}" for f in _st[-4:-1])
        if sig is not None:
            if sig in ("ts", "vs", "ss", "ds", "dg", "gs"):
                sig = sig + self.phase
            self._sem(sig)
        ws = [w for w in waits if w is not None]
        ws = [(s, self.counts[s]) if s.startswith(("ds", "dg")) else (s, v) for (s, v) in ws]
        if self.pending[eng] is not None:
            ws.append(self.pending[eng])
            self.pending[eng] = None
        if self.serial and self.last_tok is not None:
            ws.append(self.last_tok)
        for w in ws:
            self._sem(w[0])
        best = {}
        for s, v in ws:
            best[s] = max(best.get(s, 0), v)
        tok = None
        if sig is not None:
            self.counts[sig] += n
            tok = (sig, self.counts[sig])
        self.ops[eng].append((tuple(best.items()), fn, sig, n, _lbl))
        if sig is not None:
            self.last_tok = tok
        return tok

    def barrier(self):
        self._sem("bar")
        tgt = self.counts["bar"] + len(ENGINES)
        for e in ENGINES:
            self.do(e, lambda en: en.nop(), sig="bar")
        tok = ("bar", tgt)
        for e in ENGINES:
            self.pending[e] = tok
        return tok

    def finalize(self):
        with self.nc.Block() as block:
            for e in ENGINES:
                ops = self.ops[e]
                def body(eng, ops=ops):
                    for waits, fn, sig, n, lbl in ops:
                        for (s, v) in waits:
                            eng.wait_ge(self.sems[s], v)
                        try:
                            ins = fn(eng)
                        except Exception:
                            print(f"EMIT FAIL at plan lines {lbl}")
                            raise
                        if sig is not None and ins is not None:
                            ins.then_inc(self.sems[sig], n)
                getattr(block, e)(body)


def build_program(debug_taps=(), upto=None, serial=False, serial_secs=("E",)):
    nc = bass.Bass("TRN2", target_bir_lowering=False, debug=False)
    es = contextlib.ExitStack()
    sch = Sched(nc, es)
    sch.serial = serial

    inn = {}
    def din(name, shape, dtype=F32):
        inn[name] = nc.dram_tensor(name, list(shape), dtype, kind="ExternalInput").ap()
        return inn[name]

    din("x_full", [S, D]); din("x_own", [SH, D])
    din("wqm", [D, D]); din("wkm", [D, D]); din("wvm", [D, D]); din("wom", [D, D])
    din("rel16", [32, 16])
    din("wbq", [D, D]); din("wbk", [D, D]); din("wbv", [D, D]); din("wfb", [D, 32])
    din("wbo", [D, D]); din("gw1", [D, 64]); din("gw2", [64, D])
    din("delta", [1, 16]); din("lambp", [128, 8]); din("nrmw", [128, 8])
    din("gam", [1, 2]); din("gsel", [1, 4])
    din("w1t", [32, 8, 128, 128]); din("w2t", [32, 8, 128, 128])
    din("h1t", [32, 32, 128, 128]); din("h2t", [32, 32, 128, 128]); din("w3t", [8, 32, 128, 128])
    din("s1c", [128, 32]); din("s2c", [128, 32]); din("sh1c", [128, 32]); din("sh2c", [128, 32])
    din("s3c", [128, 8])
    din("ident32", [128, 128]); din("identb", [128, 128], BF16)
    din("cosb", [128, S], BF16); din("sinb", [128, S], BF16); din("rotT", [128, 128], BF16)
    din("ohcorr", [32, LB]); din("m0rep", [16, LB]); din("hrep8", [16, 2048])
    din("lt64", [64, 64]); din("upperext", [128, 512], BF16); din("hsum2", [128, 2], BF16)
    din("blocksel", [2, 128]); din("e16", [16, 1024], BF16); din("pairsel", [16, 1024])
    din("i64rep", [64, 1024], BF16)
    din("fconst", [128, 16])

    out_own = nc.dram_tensor("out_own", [SH, D], F32, kind="ExternalOutput").ap()
    taps = {}
    for tname, tshape, tdt in debug_taps:
        taps[tname] = nc.dram_tensor(tname, list(tshape), tdt, kind="ExternalOutput").ap()

    toeh = [nc.dram_tensor(f"toeh{h}", [128 * WT1], BF16) for h in range(16)]
    attnout = nc.dram_tensor("attnout", [D, S], F32)
    xresA = nc.dram_tensor("xresA", [D, SH], F32)
    xresB = nc.dram_tensor("xresB", [D, SH], F32)
    wbf = nc.dram_tensor("wbf", [48 * 1024 * 1024], BF16)
    pkg_send = nc.dram_tensor("pkg_send", [128, 1024], F32)
    pkg_recv = nc.dram_tensor("pkg_recv", [1024, 1024], F32)
    woff = {}
    off = 0
    for nm, ntile in [("w1t", 32 * 8), ("w2t", 32 * 8), ("h1t", 32 * 32), ("h2t", 32 * 32),
                      ("w3t", 8 * 32)]:
        woff[nm] = off
        off += ntile * 128 * 128

    sb = {}
    def sball(name, shape, dtype=F32):
        sb[name] = nc.alloc_sbuf_tensor("s_" + name, list(shape), dtype)
        return sb[name]

    for nm, shp, dt in [("ident32", [128, 128], F32), ("identb", [128, 128], BF16),
                        ("rotT", [128, 128], BF16), ("lt64", [64, 64], F32),
                        ("upperext", [128, 512], BF16), ("hsum2", [128, 2], BF16),
                        ("blocksel", [2, 128], F32), ("e16", [16, 1024], BF16),
                        ("pairsel", [16, 1024], F32), ("fconst", [128, 16], F32),
                        ("delta", [1, 16], F32), ("gam", [1, 2], F32), ("gsel", [1, 4], F32),
                        ("lambp", [128, 8], F32), ("nrmw", [128, 8], F32),
                        ("s1c", [128, 32], F32), ("s2c", [128, 32], F32),
                        ("sh1c", [128, 32], F32), ("sh2c", [128, 32], F32),
                        ("s3c", [128, 8], F32), ("rel8s", [32, 16], F32)]:
        sball(nm, shp, dt)
    sball("onesb", [1, 128], BF16)
    sball("onesrowf", [1, 128], F32)
    sball("onescolf", [128, 1], F32)
    sball("lambcol", [128, 8], F32)
    sball("deltarep", [128, 16], F32)
    sball("gamrep", [128, 8], F32)
    sball("statc", [128, 16], F32)
    sball("rowf", [2, 1024], F32)

    fv = [0.0, 1.0, -1.0, 0.125, 1.0 / 64, 1.0 / 1024, 1e-5, 1e-6, 1e-12, 0.25, 0.5, 2.0,
          math.sqrt(2.0)]
    for i, v in enumerate(fv):
        nc.const_aps.register(F32, float(v), sb["fconst"][:, i:i + 1])

    pb = [nc.alloc_psum_tensor(f"pb{i}", [128, 512], F32) for i in range(7)]
    pbf = nc.alloc_psum_tensor("pbf", [128, 1024], BF16)

    ar1 = [nc.alloc_sbuf_tensor(f"ar1_{i}", [128, 4096], BF16) for i in range(8)]
    ar2 = [nc.alloc_sbuf_tensor(f"ar2_{i}", [128, 4096], BF16) for i in range(4)]
    ar3 = [nc.alloc_sbuf_tensor(f"ar3_{i}", [128, 4352], BF16) for i in range(4)]
    ar4 = [nc.alloc_sbuf_tensor(f"ar4_{i}", [128, 4096], BF16) for i in range(2)]
    qg = nc.alloc_sbuf_tensor("qg", [128, 1024], BF16)
    qg2 = nc.alloc_sbuf_tensor("qg2", [128, 512], BF16)
    ctxp = nc.alloc_sbuf_tensor("ctxp", [128, 4096], BF16)
    wstr = nc.alloc_sbuf_tensor("wstr", [128, 4096], BF16)
    wstr2 = nc.alloc_sbuf_tensor("wstr2", [128, 4096], BF16)
    tmpa = nc.alloc_sbuf_tensor("tmpa", [128, 1024], F32)
    tmpb = nc.alloc_sbuf_tensor("tmpb", [128, 1024], F32)
    tmpc = nc.alloc_sbuf_tensor("tmpc", [128, 1024], F32)
    tmpd = nc.alloc_sbuf_tensor("tmpd", [128, 1536], BF16)
    lnt = tmpc[:].bitcast(BF16)[:, 0:1024]
    pax0 = nc.alloc_sbuf_tensor("pax0", [128, 512], BF16)
    pax1 = nc.alloc_sbuf_tensor("pax1", [128, 512], BF16)
    costile = nc.alloc_sbuf_tensor("costile", [128, 512], BF16)
    sintile = nc.alloc_sbuf_tensor("sintile", [128, 512], BF16)

    ohc = ar3[0].ap().bitcast(F32)[0:32, 0:LB]
    m0s = ar3[1].ap().bitcast(F32)[0:16, 0:LB]
    hreps = ar3[2].ap().bitcast(F32)[0:16, 0:2048]
    bvals = ar3[3].ap().bitcast(F32)[0:16, 0:LB]

    def dma(queue, out, in_, waits=(), sig="ds", **kw):
        return sch.do(queue, lambda e: e.dma_start(out=out, in_=in_, **kw), waits, sig, 16)

    def mm(out, lhsT, rhs, start, stop, waits=(), sig="ts", tp=None, bank=None):
        w = list(waits)
        if bank is not None and start and sch.bank_rd[bank] is not None:
            w.append(sch.bank_rd[bank])
        return sch.do(TE, lambda e: e.matmul(out, lhsT, rhs, start=start, stop=stop,
                                             tile_position=tp), w, sig, 1)

    def rdbank(bank, tok):
        sch.bank_rd[bank] = tok

    def cp(eng, out, in_, waits=(), sig=None):
        if sig is None:
            sig = {"vector": "vs", "scalar": "ss", "gpsimd": "gs"}[eng]
        return sch.do(eng, lambda e: e.tensor_copy(out=out, in_=in_), waits, sig, 1)

    def act(out, in_, func, waits=(), sig="ss", bias=0.0, scale=1.0, accum=None):
        return sch.do(SC, lambda e: e.activation(out, in_, func, bias=bias, scale=scale,
                                                 accum_out=accum), waits, sig, 1)

    def tt(out, a, b_, op, waits=(), sig="vs"):
        return sch.do(VE, lambda e: e.tensor_tensor(out=out, in0=a, in1=b_, op=op),
                      waits, sig, 1)

    def ts_(out, a, s1, s2, op0, op1=ALU.bypass, waits=(), sig="vs", accum=None):
        return sch.do(VE, lambda e: e.tensor_scalar(out=out, in0=a, scalar1=s1, scalar2=s2,
                                                    op0=op0, op1=op1, accum_out=accum),
                      waits, sig, 1)

    def stt(out, in0, scalar, in1, op0, op1, waits=(), sig="vs", accum=None):
        return sch.do(VE, lambda e: e.scalar_tensor_tensor(out=out, in0=in0, scalar=scalar,
                                                           in1=in1, op0=op0, op1=op1,
                                                           accum_out=accum), waits, sig, 1)

    def transpose(psl, src, ident, waits=(), bank=None):
        w = list(waits)
        if bank is not None and sch.bank_rd[bank] is not None:
            w.append(sch.bank_rd[bank])
        return sch.do(TE, lambda e: e.transpose(psl, src, ident), w, "ts", 1)


    def _finish_early():
        for it in range(16):
            dma(SY, out_own[128 * it:128 * (it + 1), :], tmpa[:].bitcast(F32)[:, 0:1024]
                if False else tmpa[:], waits=())
        et = sch.barrier()
        sch.do(SY, lambda e: e.nop(), waits=(("ds", sch.counts.get("ds", 0)),
                                             ("dg", sch.counts.get("dg", 0)), et))
        sch.finalize()
        return nc
    for nm in ["ident32", "identb", "rotT", "lt64", "upperext", "hsum2", "blocksel", "e16",
               "pairsel", "fconst", "delta", "lambp", "nrmw", "s1c", "s2c", "sh1c", "sh2c",
               "s3c", "gam", "gsel", "rel8s"]:
        src = inn["rel16"] if nm == "rel8s" else inn[nm]
        tok_con = dma(SY, sb[nm][:], src)
    tok_oh = dma(SY, ohc, inn["ohcorr"])
    tok_m0 = dma(SY, m0s, inn["m0rep"])
    tok_hr = dma(SY, hreps, inn["hrep8"])

    t1 = sch.do(VE, lambda e: e.memset(sb["onesb"][:], 1.0), sig="vs")
    t2 = sch.do(VE, lambda e: e.memset(sb["onesrowf"][:], 1.0), sig="vs")
    sch.do(VE, lambda e: e.memset(sb["onescolf"][:], 1.0), sig="vs")
    td = mm(pb[6][:, 4:20], sb["onesrowf"][:], sb["delta"][:], True, True,
            waits=(tok_con, t2), bank=6)
    tdr = cp(VE, sb["deltarep"][:], pb[6][:, 4:20], waits=(td,))
    rdbank(6, tdr)
    lsc = qg2[:].bitcast(F32)
    ta1 = act(lsc[:, 0:8], sb["lambp"][:], AF.Abs, waits=(tok_con,))
    ta2 = act(lsc[:, 8:16], lsc[:, 0:8], AF.Exp, scale=-1.0, waits=(ta1,))
    ta3 = act(lsc[:, 16:24], lsc[:, 8:16], AF.Ln, bias=1.0, waits=(ta2,))
    ta4 = sch.do(VE, lambda e: e.tensor_relu(out=lsc[:, 24:32], in_=sb["lambp"][:]),
                 waits=(tok_con,), sig="vs")
    ta5 = tt(lsc[:, 32:40], lsc[:, 16:24], lsc[:, 24:32], ALU.add, waits=(ta3, ta4))
    tok_lamb = ts_(sb["lambcol"][:], lsc[:, 32:40], 0.25, None, ALU.add, waits=(ta5,))

    bv_last = None
    for o in range(0, LB, 512):
        n = min(512, LB - o)
        tmm = mm(pb[5][0:16, 0:n], sb["rel8s"][:], ohc[:, o:o + n], True, True,
                 waits=(tok_con, tok_oh), bank=5)
        tex = act(qg[:].bitcast(F32)[0:16, 0:n], pb[5][0:16, 0:n], AF.Exp, waits=(tmm,))
        rdbank(5, tex)
        bv_last = tt(bvals[0:16, o:o + n], qg[:].bitcast(F32)[0:16, 0:n],
                     m0s[0:16, o:o + n], ALU.mult, waits=(tex, tok_m0))
    if "t_bv" in taps:
        dma(SY, taps["t_bv"], bvals, waits=(bv_last,))
    toe_last_dma = None
    toe_mm_last = None
    for h in range(16):
        reps = []
        for o in range(0, LB, 512):
            n = min(512, LB - o)
            tmm = mm(pb[5][:, 0:n], hreps[:, 128 * h:128 * h + 128], bvals[0:16, o:o + n],
                     True, True, waits=(bv_last, tok_hr), bank=5)
            toe_mm_last = tmm
            tc = cp(VE, tmpd[:, o:o + n], pb[5][:, 0:n], waits=(tmm,))
            rdbank(5, tc)
            reps.append(tc)
        dst = dataclasses.replace(toeh[h].ap(), ap=[[WT1, 128], [1, LB]], offset=0)
        toe_last_dma = dma(GP, dst, tmpd[:, 0:LB], waits=tuple(reps), sig="dg")
        if h == 0 and "t_tmpd" in taps:
            dma(SY, taps["t_tmpd"], tmpd[:, 0:LB], waits=tuple(reps))
        if h == 0 and "t_toe0" in taps:
            srcq = dataclasses.replace(toeh[0].ap(), ap=[[1, 1], [1, 4096]], offset=0)
            dma(SY, taps["t_toe0"], srcq, waits=(toe_last_dma,))
        sch.pending[VE] = toe_last_dma

    if upto == "consts":
        return _finish_early()
    prev = {"iter": None, "tp": None}
    for it in range(32):
        buf = tmpa if it % 2 == 0 else tmpb
        w = [tok_con]
        if prev["iter"] is not None:
            w.append(prev["iter"])
        tl = dma(SY, buf[:], inn["x_full"][128 * it:128 * (it + 1), :], waits=tuple(w))
        tm = sch.do(VE, lambda e, b_=buf: e.reduce_sum(out=sb["statc"][:, 0:1], in_=b_[:],
                                                       axis=mybir.AxisListType.X),
                    waits=(tl,), sig="vs")
        tsq = act(lnt, buf[:], AF.Square, accum=sb["statc"][:, 1:2], waits=(tl,))
        tm2 = ts_(sb["statc"][:, 2:3], sb["statc"][:, 0:1], 1.0 / 1024, None, ALU.mult,
                  waits=(tm,))
        tv1 = ts_(sb["statc"][:, 3:4], sb["statc"][:, 2:3], sb["statc"][:, 2:3], None, ALU.mult,
                  waits=(tm2,))
        tv2 = ts_(sb["statc"][:, 4:5], sb["statc"][:, 1:2], 1.0 / 1024, None, ALU.mult,
                  waits=(tsq,))
        tv3 = tt(sb["statc"][:, 5:6], sb["statc"][:, 4:5], sb["statc"][:, 3:4], ALU.subtract,
                 waits=(tv1, tv2))
        tv4 = ts_(sb["statc"][:, 5:6], sb["statc"][:, 5:6], 1e-5, None, ALU.add, waits=(tv3,))
        tv5 = act(sb["statc"][:, 6:7], sb["statc"][:, 5:6], AF.Sqrt, waits=(tv4,))
        tv6 = sch.do(VE, lambda e: e.reciprocal(out=sb["statc"][:, 7:8],
                                                in_=sb["statc"][:, 6:7]),
                     waits=(tv5,), sig="vs")
        tn_ = ts_(lnt, buf[:], sb["statc"][:, 2:3], sb["statc"][:, 7:8],
                  ALU.subtract, ALU.mult, waits=(tm2, tv6))
        last_ev = None
        for dt_ in range(8):
            psl = pbf[:, 128 * (dt_ % 8):128 * (dt_ % 8) + 128]
            w2 = [tn_, tok_con]
            if sch.bank_rd[7] is not None:
                w2.append(sch.bank_rd[7])
            ttp = sch.do(TE, lambda e, ps=psl, d=dt_: e.transpose(
                ps, lnt[:, 128 * d:128 * (d + 1)], sb["identb"][:]), tuple(w2), "ts", 1)
            tev = cp(VE, ar1[dt_][:, 128 * it:128 * (it + 1)], psl, waits=(ttp,))
            rdbank(7, tev)
            last_ev = tev
        prev["iter"] = last_ev
    tok_ln1 = sch.barrier()

    if upto == "ln1":
        return _finish_early()
    def load_wcast(dst, name, din_, m, mlo=0, rlo=0, waits=()):
        k = din_ // 128
        src_ = inn[name][rlo:rlo + din_, :].rearrange("(k p) m -> p k m", p=128)[:, :, mlo:mlo + m]
        return dma(GP, dst[:, 0:k * m].rearrange("p (k m) -> p k m", k=k), src_,
                   waits=waits, sig="dg")

    DELTAS = list(range(-384, 385, 128))
    ffn_conv_chunks = []
    for nm in ["w1t", "w2t", "h1t", "h2t", "w3t"]:
        nel = int(np.prod(inn[nm].shape))
        flat_in = inn[nm].rearrange("a b c d -> (a b c d)")
        CH = 4 * 1024 * 1024
        for o in range(0, nel, CH):
            n = min(CH, nel - o)
            ffn_conv_chunks.append((wbf.ap()[woff[nm] + o: woff[nm] + o + n],
                                    flat_in[o:o + n]))
    conv_i = 0
    conv_toks = []
    prev_wo_dma = None
    ctxp_guard = None
    kT = ar2

    for hg in range(2):
        tok_wk = load_wcast(wstr, "wkm", 1024, 512, mlo=512 * hg, waits=(tok_ln1,))
        sch.serial = serial or ("E" in serial_secs)
        kcarry = {"rope": None, "rot": None}
        for pair in range(4):
            for nch in range(8):
                dst = pb[5][:, 0:512]
                tk = None
                for kt_ in range(8):
                    tk = mm(dst, wstr[:, 512 * kt_ + 128 * pair:512 * kt_ + 128 * pair + 128],
                            ar1[kt_][:, 512 * nch:512 * (nch + 1)], kt_ == 0, kt_ == 7,
                            waits=(tok_wk,) if kt_ == 0 else (), bank=5)
                tcs = dma(GP, costile[:], inn["cosb"][:, 512 * nch:512 * (nch + 1)],
                          waits=(kcarry["rope"],), sig="dg")
                tsn = dma(GP, sintile[:], inn["sinb"][:, 512 * nch:512 * (nch + 1)],
                          waits=(kcarry["rope"],), sig="dg")
                w_e1 = [tk, kcarry["rot"], kcarry["rope"]]
                te1 = cp(VE, tmpd[:, 0:512], dst, waits=tuple(w_e1))
                rdbank(5, te1)
                trm = mm(pb[4][:, 0:512], sb["rotT"][:], tmpd[:, 0:512], True, True,
                         waits=(te1,), bank=4)
                tt1 = tt(tmpa[:, 0:512], tmpd[:, 0:512], costile[:], ALU.mult,
                         waits=(te1, tcs))
                te2 = act(tmpd[:, 512:1024], pb[4][:, 0:512], AF.Copy, waits=(trm,))
                rdbank(4, te2)
                kcarry["rot"] = te2
                tt2 = tt(tmpb[:, 0:512], tmpd[:, 512:1024], sintile[:], ALU.mult,
                         waits=(te2, tsn))
                kcarry["rope"] = tt(kT[pair][:, 512 * nch:512 * (nch + 1)], tmpa[:, 0:512],
                                    tmpb[:, 0:512], ALU.add, waits=(tt1, tt2))
        sch.serial = serial
        tok_kT = sch.barrier()
        if "F" in serial_secs:
            sch.serial = True

        tok_wv = load_wcast(wstr, "wvm", 1024, 512, mlo=512 * hg, waits=(tok_kT,))
        vt = []
        for i in range(32):
            a = ar3[i // 8]
            o = (i % 8) * 544
            vt.append(a[:, o:o + 520])
        first_v = (toe_mm_last, tok_kT) if hg == 0 else (tok_kT,)
        for ti in range(32):
            dst = pb[5][:, 0:512]
            tk = None
            for kt_ in range(8):
                w = (tok_wv,) + first_v if kt_ == 0 else ()
                tk = mm(dst, ar1[kt_][:, 128 * ti:128 * (ti + 1)],
                        wstr[:, 512 * kt_:512 * (kt_ + 1)], kt_ == 0, kt_ == 7,
                        waits=w, bank=5)
            dstv = dataclasses.replace(vt[ti], ap=[vt[ti].ap[0], [65, 8], [1, 64]])
            tv = sch.do(VE, lambda e, d=dstv, s=dst: e.tensor_copy(
                out=d, in_=s.rearrange("p (h x) -> p h x", h=8)), waits=(tk,), sig="vs")
            rdbank(5, tv)
            onesv = dataclasses.replace(vt[ti], ap=[vt[ti].ap[0], [65, 8]],
                                        offset=vt[ti].offset + 64)
            sch.do(VE, lambda e, d=onesv: e.memset(d, 1.0), waits=(tv,), sig="vs")
        sch.serial = serial
        tok_v = sch.barrier()

        tok_wq = load_wcast(wstr, "wqm", 1024, 512, mlo=512 * hg, waits=(tok_v,))
        tok_wo = load_wcast(wstr2, "wom", 512, 1024, rlo=512 * hg, waits=(tok_v,))
        for pair in range(4):
            bc = [ar4[0], ar4[1]]
            btoks = []
            for di, dl in enumerate(DELTAS):
                c0 = dl + 512
                for hh in range(2):
                    h = 8 * hg + 2 * pair + hh
                    srcb = dataclasses.replace(toeh[h].ap(), ap=[[LB, 128], [1, 512]],
                                               offset=c0)
                    btoks.append(dma(GP, bc[hh][:, 512 * di:512 * (di + 1)], srcb,
                                     waits=(toe_last_dma, tok_v), sig="dg"))
            tok_b = btoks[-1]
            if hg == 0 and pair == 0 and "t_bc" in taps:
                dma(SY, taps["t_bc"], bc[0][:, 0:3584], waits=(tok_b,))
            ctx_last = None
            last_S = None
            prev_tqr = None
            for g in range(8):
                sch.serial = serial or ("C" in serial_secs)
                qdst = pb[5][:, 0:512]
                tk = None
                for kt_ in range(8):
                    tk = mm(qdst, wstr[:, 512 * kt_ + 128 * pair:512 * kt_ + 128 * pair + 128],
                            ar1[kt_][:, 512 * g:512 * (g + 1)], kt_ == 0, kt_ == 7,
                            waits=(tok_wq,) if kt_ == 0 else (), bank=5)
                tcs = dma(GP, costile[:], inn["cosb"][:, 512 * g:512 * (g + 1)],
                          waits=(prev_tqr,), sig="dg")
                tsn = dma(GP, sintile[:], inn["sinb"][:, 512 * g:512 * (g + 1)],
                          waits=(prev_tqr,), sig="dg")
                te1 = act(qg[:, 0:512], qdst, AF.Copy, scale=0.125,
                          waits=(tk, last_S, prev_tqr))
                rdbank(5, te1)
                trm = mm(pb[4][:, 0:512], sb["rotT"][:], qg[:, 0:512], True, True,
                         waits=(te1,), bank=4)
                te2 = act(qg[:, 512:1024], pb[4][:, 0:512], AF.Copy, waits=(trm, prev_tqr))
                rdbank(4, te2)
                tt1 = tt(tmpa[:, 0:512], qg[:, 0:512], costile[:], ALU.mult,
                         waits=(te1, tcs))
                tt2 = tt(tmpb[:, 0:512], qg[:, 512:1024], sintile[:], ALU.mult,
                         waits=(te2, tsn))
                tqr = tt(qg2[:, 0:512], tmpa[:, 0:512], tmpb[:, 0:512], ALU.add,
                         waits=(tt1, tt2, last_S))
                prev_tqr = tqr
                sch.serial = serial or ("A" in serial_secs)
                nkt = 4 * (g + 1)
                pa = [tmpd[:, 0:512], tmpd[:, 512:1024]]
                olast = [None, None]
                last_tP = [None, None]
                for kt_ in range(nkt):
                    dl = 512 * g - 128 * kt_
                    near = -384 <= dl <= 384
                    for hh in range(2):
                        srow = slice(64 * hh, 64 * hh + 64)
                        sps = pb[hh][:, 0:512]
                        tS = mm(sps, kT[pair][srow, 128 * kt_:128 * (kt_ + 1)],
                                qg2[srow, 0:512], True, True, waits=(tqr,), sig="ts",
                                tp=(64 * hh, 0), bank=hh)
                        last_S = tS
                        tE = act(pa[hh], sps, AF.Exp, waits=(tS, last_tP[hh], olast[hh]))
                        rdbank(hh, tE)
                        tP = tE
                        avsrc = pa[hh]
                        if near:
                            di = DELTAS.index(dl)
                            pax = pax0 if hh == 0 else pax1
                            tP = tt(pax[:, 0:512], pa[hh], bc[hh][:, 512 * di:512 * (di + 1)],
                                    ALU.mult, waits=(tE, tok_b, last_tP[hh]))
                            last_tP[hh] = tP
                            avsrc = pax[:, 0:512]
                        hl = 2 * pair + hh
                        tAV = mm(pb[2 + hh][0:65, 0:512], vt[kt_][:, 65 * hl:65 * hl + 65],
                                 avsrc, kt_ == 0, kt_ == nkt - 1, waits=(tP,),
                                 bank=(2 + hh) if kt_ == 0 else None)
                        olast[hh] = tAV
                sch.serial = serial or ("B" in serial_secs)
                for hh in range(2):
                    ob = pb[2 + hh]
                    trc = sch.do(VE, lambda e, ob_=ob: e.reciprocal(
                        out=sb["rowf"][0:1, 0:512], in_=ob_[64:65, 0:512]),
                        waits=(olast[hh], ctx_last), sig="vs")
                    tlr = mm(pb[4][0:64, 0:512], sb["onesrowf"][0:1, 0:64],
                             sb["rowf"][0:1, 0:512], True, True, waits=(trc,), bank=4)
                    tos = act(tmpc[0:64, 512 * hh:512 * hh + 512], ob[0:64, 0:512], AF.Copy,
                              waits=(olast[hh], trc, prev_wo_dma))
                    tcx = tt(ctxp[64 * hh:64 * hh + 64, 512 * g:512 * (g + 1)],
                             tmpc[0:64, 512 * hh:512 * hh + 512], pb[4][0:64, 0:512],
                             ALU.mult, waits=(tlr, tos, ctxp_guard))
                    rdbank(2 + hh, tcx)
                    rdbank(4, tcx)
                    ctx_last = tcx
                sch.serial = serial
            if "t_ctx8" in taps:
                ctxp_guard = dma(SY,
                                 taps["t_ctx8"][128 * (4 * hg + pair):128 * (4 * hg + pair) + 128, :],
                                 ctxp[:], waits=(ctx_last,))
            sch.serial = serial or ("D" in serial_secs)
            evac_carry = [None, None]
            wo_dma_last = None
            for mt in range(8):
                for nch in range(8):
                    dst = pb[6][:, 0:512]
                    lhs = wstr2[:, 1024 * pair + 128 * mt:1024 * pair + 128 * mt + 128]
                    tmm = mm(dst, lhs, ctxp[:, 512 * nch:512 * (nch + 1)], True, True,
                             waits=(tok_wo, ctx_last, prev_wo_dma), bank=6)
                    sl = (mt * 8 + nch) % 2
                    tev = cp(VE, tmpc[:, 512 * sl:512 * sl + 512], dst,
                             waits=(tmm, evac_carry[sl]))
                    rdbank(6, tev)
                    dstp = attnout.ap()[128 * mt:128 * (mt + 1), 512 * nch:512 * (nch + 1)]
                    kw = {} if (hg == 0 and pair == 0) else {"accum_op": ALU.add}
                    tdm = dma(GP, dstp, tmpc[:, 512 * sl:512 * sl + 512], waits=(tev,),
                              sig="dg", **kw)
                    evac_carry[sl] = tdm
                    wo_dma_last = tdm
            prev_wo_dma = wo_dma_last
            sch.serial = serial
            n_emit = (len(ffn_conv_chunks) + 7) // 8
            for _ in range(n_emit):
                if conv_i < len(ffn_conv_chunks):
                    dsty, srcy = ffn_conv_chunks[conv_i]
                    conv_toks.append(dma(GP, dsty, srcy, sig="dg"))
                    conv_i += 1
            sch.barrier()
    while conv_i < len(ffn_conv_chunks):
        dsty, srcy = ffn_conv_chunks[conv_i]
        conv_toks.append(dma(GP, dsty, srcy, sig="dg"))
        conv_i += 1
    tok_conv = conv_toks[-1]
    tok_attn = sch.barrier()
    if upto == "attn":
        return _finish_early()

    tgm = mm(pb[6][:, 0:2], sb["onesrowf"][:], sb["gam"][:], True, True,
             waits=(tok_attn,), bank=6)
    tgm2 = mm(pb[6][:, 2:6], sb["onesrowf"][:], sb["gsel"][:], True, True,
              waits=(tok_attn,), bank=None)
    tok_gam = cp(VE, sb["gamrep"][:, 0:6], pb[6][:, 0:6], waits=(tgm, tgm2))
    rdbank(6, tok_gam)

    carry = {"a": None, "b": None, "l": None, "c": None}
    for it in range(16):
        tl = dma(SY, tmpa[:], inn["x_own"][128 * it:128 * (it + 1), :],
                 waits=(tok_attn, carry["l"]))
        for dt_ in range(8):
            bk = dt_ % 2
            psl = pb[bk][:, 0:128]
            ttp = transpose(psl, tmpa[:, 128 * dt_:128 * (dt_ + 1)], sb["ident32"][:],
                            waits=(tl, tok_con), bank=bk)
            tr0 = dma(GP, tmpb[:, 0:128],
                      attnout.ap()[128 * dt_:128 * (dt_ + 1), 128 * it:128 * it + 128],
                      waits=(tok_attn, prev_wo_dma, carry["a"]), sig="dg")
            tr1 = dma(GP, tmpb[:, 128:256],
                      attnout.ap()[128 * dt_:128 * (dt_ + 1),
                                   2048 + 128 * it:2048 + 128 * it + 128],
                      waits=(tok_attn, prev_wo_dma, carry["a"]), sig="dg")
            tg1 = ts_(tmpb[:, 256:384], tmpb[:, 0:128], sb["gamrep"][:, 0:1], None,
                      ALU.mult, waits=(tr0, tok_gam, carry["c"]))
            tg2 = stt(tmpb[:, 384:512], tmpb[:, 128:256], sb["gamrep"][:, 1:2],
                      tmpb[:, 256:384], ALU.mult, ALU.add, waits=(tr1, tg1))
            tad = tt(tmpc[:, 0:128], psl, tmpb[:, 384:512], ALU.add,
                     waits=(ttp, tg2, carry["b"]))
            rdbank(bk, tad)
            carry["a"] = tg2
            carry["c"] = tad
            tdm = dma(GP, xresA.ap()[128 * dt_:128 * (dt_ + 1), 128 * it:128 * it + 128],
                      tmpc[:, 0:128], waits=(tad,), sig="dg")
            carry["b"] = tdm
            if dt_ == 7:
                carry["l"] = tad
    tok_x2 = sch.barrier()

    if "t_ln1" in taps:
        for dt_ in range(8):
            dma(SY, taps["t_ln1"][128 * dt_:128 * (dt_ + 1), :], ar1[dt_][:],
                waits=(tok_ln1,))
    if "t_ao" in taps:
        dma(SY, taps["t_ao"], attnout.ap(), waits=(tok_attn, prev_wo_dma))
    if "t_toe" in taps:
        dma(SY, taps["t_toe"], toeh[0].ap()[0:8192].rearrange("(a b) -> a b", a=2),
            waits=(tok_attn,))
    if "t_btile" in taps:
        for di, dl in enumerate([0, -128, 384]):
            srcb = dataclasses.replace(toeh[0].ap(), ap=[[LB, 128], [1, 512]],
                                       offset=dl + 512)
            dma(SY, taps["t_btile"][128 * di:128 * (di + 1), :], srcb, waits=(tok_attn,))
    if "t_x2" in taps:
        for dt_ in range(8):
            dma(SY, taps["t_x2"][128 * dt_:128 * (dt_ + 1), :],
                xresA.ap()[128 * dt_:128 * (dt_ + 1), :], waits=(tok_x2,))

    sch.serial = True

    def ffn_pass(xin, xout):
        TB = 512
        xb = lambda dt: ar1[dt // 4][:].bitcast(F32)[:, 512 * (dt % 4):512 * (dt % 4) + TB]
        lnb = lambda dt: ar1[2][:, TB * dt:TB * dt + TB]
        hbuf = lambda k: ar2[k // 8][:, TB * (k % 8):TB * (k % 8) + TB]
        gbuf = lambda k: ar3[k // 8][:, TB * (k % 8):TB * (k % 8) + TB]
        slab = [wstr, wstr2]
        for tb in range(4):
            col0 = TB * tb
            for dt in range(8):
                dma(GP, xb(dt), xin.ap()[128 * dt:128 * (dt + 1), col0:col0 + TB], sig="dg")
            for dt in range(8):
                mm(pb[4][0:1, 0:TB], sb["onescolf"][:], xb(dt), dt == 0, dt == 7,
                   bank=4 if dt == 0 else None)
            for dt in range(8):
                tq = act(tmpa[:, 0:TB], xb(dt), AF.Square)
                mm(pb[5][0:1, 0:TB], sb["onescolf"][:], tmpa[:, 0:TB], dt == 0, dt == 7,
                   waits=(tq,), bank=5 if dt == 0 else None)
            tm = ts_(sb["rowf"][0:1, 0:TB], pb[4][0:1, 0:TB], 1.0 / 1024, None, ALU.mult)
            rdbank(4, tm)
            te = ts_(tmpa[0:1, 0:TB], pb[5][0:1, 0:TB], 1.0 / 1024, None, ALU.mult)
            rdbank(5, te)
            tv = tt(tmpa[0:1, TB:2 * TB], sb["rowf"][0:1, 0:TB], sb["rowf"][0:1, 0:TB],
                    ALU.mult)
            tv2 = tt(tmpa[0:1, TB:2 * TB], tmpa[0:1, 0:TB], tmpa[0:1, TB:2 * TB],
                     ALU.subtract)
            tv3 = ts_(tmpa[0:1, TB:2 * TB], tmpa[0:1, TB:2 * TB], 1e-5, None, ALU.add)
            tv4 = act(tmpc[0:1, 0:TB], tmpa[0:1, TB:2 * TB], AF.Sqrt)
            tv5 = sch.do(VE, lambda e: e.reciprocal(out=tmpa[0:1, TB:2 * TB],
                                                    in_=tmpc[0:1, 0:TB]), sig="vs")
            mm(pb[4][:, 0:TB], sb["onesrowf"][:], sb["rowf"][0:1, 0:TB], True, True, bank=4)
            mm(pb[5][:, 0:TB], sb["onesrowf"][:], tmpa[0:1, TB:2 * TB], True, True, bank=5)
            for dt in range(8):
                t1 = tt(tmpb[:, 0:TB], xb(dt), pb[4][:, 0:TB], ALU.subtract)
                t2 = tt(lnb(dt), tmpb[:, 0:TB], pb[5][:, 0:TB], ALU.mult)
                if dt == 7:
                    rdbank(4, t2)
                    rdbank(5, t2)
            for mt in range(32):
                sl = slab[mt % 2]
                d1 = dma(GP, sl[:, 0:1024].rearrange("p (k m) -> p k m", k=8),
                         inn["w1t"][mt].rearrange("k a b -> a k b"), sig="dg")
                d2 = dma(GP, sl[:, 1024:2048].rearrange("p (k m) -> p k m", k=8),
                         inn["w2t"][mt].rearrange("k a b -> a k b"), sig="dg")
                for k in range(8):
                    mm(pb[0][:, 0:TB], sl[:, 128 * k:128 * k + 128], lnb(k), k == 0, k == 7,
                       waits=(d1,) if k == 0 else (), bank=0)
                for k in range(8):
                    mm(pb[1][:, 0:TB], sl[:, 1024 + 128 * k:1024 + 128 * k + 128], lnb(k),
                       k == 0, k == 7, waits=(d2,) if k == 0 else (), bank=1)
                a1 = act(tmpd[:, 0:TB], pb[0][:, 0:TB], AF.Silu,
                         scale=sb["s1c"][:, mt:mt + 1])
                rdbank(0, a1)
                a2 = act(tmpd[:, TB:2 * TB], pb[1][:, 0:TB], AF.Copy,
                         scale=sb["s2c"][:, mt:mt + 1])
                rdbank(1, a2)
                tt(hbuf(mt), tmpd[:, 0:TB], tmpd[:, TB:2 * TB], ALU.mult, waits=(a1, a2))
            for layer, wnm, scnm, srcf, dstf in (("h1", "h1t", "sh1c", hbuf, gbuf),
                                                 ("h2", "h2t", "sh2c", gbuf, hbuf)):
                for mt in range(32):
                    sl = slab[mt % 2]
                    d1 = dma(GP, sl[:, 0:4096].rearrange("p (k m) -> p k m", k=32),
                             inn[wnm][mt].rearrange("k a b -> a k b"), sig="dg")
                    for k in range(32):
                        mm(pb[0][:, 0:TB], sl[:, 128 * k:128 * k + 128], srcf(k),
                           k == 0, k == 31, waits=(d1,) if k == 0 else (), bank=0)
                    a1 = act(dstf(mt), pb[0][:, 0:TB], AF.Gelu,
                             scale=sb[scnm][:, mt:mt + 1])
                    rdbank(0, a1)
            for mo in range(8):
                sl = slab[mo % 2]
                d1 = dma(GP, sl[:, 0:4096].rearrange("p (k m) -> p k m", k=32),
                         inn["w3t"][mo].rearrange("k a b -> a k b"), sig="dg")
                for k in range(32):
                    mm(pb[0][:, 0:TB], sl[:, 128 * k:128 * k + 128], hbuf(k), k == 0,
                       k == 31, waits=(d1,) if k == 0 else (), bank=0)
                s1 = stt(tmpc[:, 0:TB], pb[0][:, 0:TB], sb["s3c"][:, mo:mo + 1], xb(mo),
                         ALU.mult, ALU.add)
                rdbank(0, s1)
                dma(GP, xout.ap()[128 * mo:128 * (mo + 1), col0:col0 + TB], tmpc[:, 0:TB],
                    waits=(s1,), sig="dg")
        return sch.barrier()

    sch.phase = "f1"
    tok_ffn1 = ffn_pass(xresA, xresB)
    if upto == "ffn1":
        tap_x3 = taps.get("t_x3")
        if tap_x3 is not None:
            for dt_ in range(8):
                dma(GP, tap_x3[128 * dt_:128 * (dt_ + 1), :],
                    xresB.ap()[128 * dt_:128 * (dt_ + 1), :], sig="dg")
        return _finish_early()

    lnb3 = lambda dt: ar1[dt // 4][:, 1024 * (dt % 4):1024 * (dt % 4) + 1024]
    kTb = lambda p: ar1[2 + p // 4][:, 1024 * (p % 4):1024 * (p % 4) + 1024]
    qTb = lambda p: ar1[4 + p // 4][:, 1024 * (p % 4):1024 * (p % 4) + 1024]
    oTb = lambda p: ar1[6 + p // 4][:, 1024 * (p % 4):1024 * (p % 4) + 1024]
    ktok = lambda t: ar2[t // 4][:, 1024 * (t % 4):1024 * (t % 4) + 1024]
    vtok = lambda t: ar2[2 + (t // 4)][:, 1024 * (t % 4):1024 * (t % 4) + 1024]
    gateb = lambda p: ar3[p // 4][:, 1024 * (p % 4):1024 * (p % 4) + 1024]
    fmisc = ar3[2][:].bitcast(F32)
    ftok = lambda t: fmisc[:, 32 * t:32 * t + 16]
    lbtok = lambda t: fmisc[:, 32 * t + 16:32 * t + 32]
    lnbT = fmisc[0:16, 256:1280]
    Fch = fmisc[0:64, 1280:1296]
    FTch = fmisc[0:16, 1296:1360]
    GTch = fmisc[0:16, 1360:1424]
    wendT = ar3[2][0:16, 2848:2912]
    dTcol = fmisc[0:16, 1488:1489]
    EGch = ar3[2][0:16, 3008:3072]
    eFtok = fmisc[0:64, 1568:1584]
    eFT2 = fmisc[0:2, 1584:1648]
    sss = ar4[0][:].bitcast(F32)
    kwb = ar4[1][0:64, 0:1024]
    vwb = ar4[1][0:64, 1024:2048]
    ktil = ar4[1][0:64, 2048:3072]
    lambBIG = ar4[1][:, 3584:4096]
    cfBIG = ar3[3][:, 0:512]
    vtil = ar3[3][0:64, 512:1536]
    zqs = ar3[3][:, 1536:2048]
    kch = ar3[3][0:64, 2048:3072]
    vch = ar3[3][0:64, 3072:4096]
    fch0 = fmisc[0:64, 1600:1616]
    lhcg = ctxp[:, 0:1024]
    lhO = ctxp[:, 1024:2048]
    lhK = ctxp[:, 2048:3072]
    lhV = ctxp[:, 3072:4096]
    cgAp = tmpd[:, 1024:1536]
    cgX = pax0[:, 0:512]
    cgR = pax1[:, 0:512]
    cgP = qg2[:, 0:512]
    Gext = qg[:, 0:512]
    gscr = qg[:, 512:1024]
    zsq = tmpa[:].bitcast(BF16)[:, 0:512]
    tmp1 = tmpa[:].bitcast(BF16)[:, 512:1024]
    tmp2 = tmpa[:].bitcast(BF16)[:, 1024:1536]
    tmp3 = tmpa[:].bitcast(BF16)[:, 1536:2048]
    rwrs = tmpb[0:2, 0:512]
    rwpap = tmpb[0:2, 512:1024]
    rwa = tmpc[0:2, 0:512]
    rwb = tmpc[0:2, 512:1024]
    rwt = fmisc[0:2, 1616:2128]

    def tp_pe(dst, srcap, f32=False):
        P, Fd = srcap.shape[0], srcap.shape[1]
        if f32:
            tq = transpose(pb[6][0:Fd, 0:P], srcap, sb["ident32"][0:P, 0:P], bank=6)
            tc = cp(VE, dst, pb[6][0:Fd, 0:P], waits=(tq,))
            rdbank(6, tc)
        else:
            tq = transpose(pbf[0:Fd, 0:P], srcap, sb["identb"][0:P, 0:P], bank=7)
            tc = cp(VE, dst, pbf[0:Fd, 0:P], waits=(tq,))
            rdbank(7, tc)
        return tc

    sch.do(VE, lambda e: e.memset(sss[:, 0:1024], 0.0), sig="vs")
    tob = sch.do(VE, lambda e: e.memset(tmpd[:, 0:64], 1.0), sig="vs")
    for p in range(8):
        ts_(lambBIG[:, 64 * p:64 * p + 64], tmpd[:, 0:64], sb["lambcol"][:, p:p + 1],
            None, ALU.mult, waits=(tob,))
    dma(GP, lhK[64:128, :], inn["i64rep"], sig="dg")
    dma(GP, lhV[64:128, :], inn["i64rep"], sig="dg")
    sch.do(VE, lambda e: e.memset(tmpd[64:128, 0:512], 1.0), sig="vs")

    def ln3_half(half):
        for sub in range(2):
            col0 = 1024 * half + 512 * sub
            for dt in range(8):
                tl = dma(GP, tmpa[:, 0:512],
                         xresB.ap()[128 * dt:128 * (dt + 1), col0:col0 + 512], sig="dg")
                mm(pb[4][0:1, 0:512], sb["onescolf"][:], tmpa[:, 0:512], dt == 0, dt == 7,
                   waits=(tl,), bank=4 if dt == 0 else None)
                tq = act(tmpc[:, 0:512], tmpa[:, 0:512], AF.Square)
                mm(pb[5][0:1, 0:512], sb["onescolf"][:], tmpc[:, 0:512], dt == 0, dt == 7,
                   waits=(tq,), bank=5 if dt == 0 else None)
            tm = ts_(sb["rowf"][0:1, 0:512], pb[4][0:1, 0:512], 1.0 / 1024, None, ALU.mult)
            rdbank(4, tm)
            te = ts_(tmpc[0:1, 0:512], pb[5][0:1, 0:512], 1.0 / 1024, None, ALU.mult)
            rdbank(5, te)
            tt(tmpc[0:1, 512:1024], sb["rowf"][0:1, 0:512], sb["rowf"][0:1, 0:512],
               ALU.mult)
            tt(tmpc[0:1, 512:1024], tmpc[0:1, 0:512], tmpc[0:1, 512:1024], ALU.subtract)
            ts_(tmpc[0:1, 512:1024], tmpc[0:1, 512:1024], 1e-5, None, ALU.add)
            act(tmpc[0:1, 0:512], tmpc[0:1, 512:1024], AF.Sqrt)
            sch.do(VE, lambda e: e.reciprocal(out=tmpc[0:1, 512:1024],
                                              in_=tmpc[0:1, 0:512]), sig="vs")
            mm(pb[4][:, 0:512], sb["onesrowf"][:], sb["rowf"][0:1, 0:512], True, True,
               bank=4)
            mm(pb[5][:, 0:512], sb["onesrowf"][:], tmpc[0:1, 512:1024], True, True, bank=5)
            for dt in range(8):
                tl = dma(GP, tmpa[:, 0:512],
                         xresB.ap()[128 * dt:128 * (dt + 1), col0:col0 + 512], sig="dg")
                t1 = tt(tmpc[:, 0:512], tmpa[:, 0:512], pb[4][:, 0:512], ALU.subtract,
                        waits=(tl,))
                t2 = tt(lnb3(dt)[:, 512 * sub:512 * sub + 512], tmpc[:, 0:512],
                        pb[5][:, 0:512], ALU.mult)
                if dt == 7:
                    rdbank(4, t2)
                    rdbank(5, t2)

    def wload(wname):
        da = dma(GP, wstr[:, 0:4096].rearrange("p (k m) -> p k m", k=8),
                 inn[wname].rearrange("(k p) m -> p k m", p=128)[:, :, 0:512], sig="dg")
        db = dma(GP, wstr2[:, 0:4096].rearrange("p (k m) -> p k m", k=8),
                 inn[wname].rearrange("(k p) m -> p k m", p=128)[:, :, 512:1024], sig="dg")
        return da, db

    def proj_feat(wname, dst_fn, l2n):
        da, db = wload(wname)
        for p in range(8):
            wv = wstr if p < 4 else wstr2
            mc = 128 * (p % 4)
            for nch in range(2):
                for k in range(8):
                    mm(pb[0][:, 0:512], wv[:, 512 * k + mc:512 * k + mc + 128],
                       lnb3(k)[:, 512 * nch:512 * nch + 512], k == 0, k == 7,
                       waits=(da, db) if k == 0 else (), bank=0)
                if l2n:
                    tq = act(tmp1, pb[0][:, 0:512], AF.Square)
                    tn = mm(pb[1][0:2, 0:512], sb["hsum2"][:], tmp1, True, True,
                            waits=(tq,), bank=1)
                    t1 = ts_(sb["rowf"][0:2, 0:512], pb[1][0:2, 0:512], 1e-6, None, ALU.add)
                    rdbank(1, t1)
                    t2 = act(tmpc[0:2, 0:512], sb["rowf"][0:2, 0:512], AF.Sqrt)
                    t3 = sch.do(VE, lambda e: e.reciprocal(
                        out=sb["rowf"][0:2, 0:512], in_=tmpc[0:2, 0:512]), sig="vs")
                    t4 = mm(pb[2][:, 0:512], sb["blocksel"][:], sb["rowf"][0:2, 0:512],
                            True, True, bank=2)
                    t5 = act(tmp1, pb[0][:, 0:512], AF.Copy)
                    rdbank(0, t5)
                    t6 = tt(dst_fn(p)[:, 512 * nch:512 * nch + 512], tmp1, pb[2][:, 0:512],
                            ALU.mult)
                    rdbank(2, t6)
                else:
                    t5 = act(dst_fn(p)[:, 512 * nch:512 * nch + 512], pb[0][:, 0:512],
                             AF.Copy)
                    rdbank(0, t5)

    def attnb_half_setup(half):
        ln3_half(half)
        proj_feat("wbk", kTb, True)
        proj_feat("wbq", qTb, True)
        da, db = wload("wbv")
        for t in range(8):
            for nh, wv in ((0, wstr), (1, wstr2)):
                for k in range(8):
                    mm(pb[0][:, 0:512], lnb3(k)[:, 128 * t:128 * t + 128],
                       wv[:, 512 * k:512 * k + 512], k == 0, k == 7,
                       waits=(da, db) if k == 0 else (), bank=0)
                t5 = act(vtok(t)[:, 512 * nh:512 * nh + 512], pb[0][:, 0:512], AF.Copy)
                rdbank(0, t5)
        for p in range(8):
            for j in range(8):
                tp_pe(ktok(j)[:, 128 * p:128 * p + 128], kTb(p)[:, 128 * j:128 * j + 128])
        dwf = dma(GP, wstr[:, 0:256].rearrange("p (k m) -> p k m", k=8),
                  inn["wfb"].rearrange("(k p) m -> p k m", p=128), sig="dg")
        for t in range(8):
            for k in range(8):
                mm(pb[0][:, 0:32], lnb3(k)[:, 128 * t:128 * t + 128],
                   wstr[:, 32 * k:32 * k + 32], k == 0, k == 7,
                   waits=(dwf,) if k == 0 else (), bank=0)
            tz1 = tt(tmpc[:, 0:16], pb[0][:, 0:16], sb["deltarep"][:], ALU.add)
            tz2 = act(tmpc[:, 16:32], tmpc[:, 0:16], AF.Sigmoid)
            tz3 = act(ftok(t), tmpc[:, 16:32], AF.Ln)
            tz4 = act(tmpc[:, 32:48], pb[0][:, 16:32], AF.Sigmoid)
            rdbank(0, tz4)
            tz5 = act(lbtok(t), tmpc[:, 32:48], AF.Ln)
        for t in range(8):
            tp_pe(lnbT[0:16, 128 * t:128 * t + 128], lbtok(t), f32=True)

    def chunk_F(ci):
        t8, r0 = ci // 2, 64 * (ci % 2)
        ccol = slice(64 * ci, 64 * ci + 64)
        cp(VE, kch[:, 0:1024], ktok(t8)[r0:r0 + 64, 0:1024])
        cp(VE, vch[:, 0:1024], vtok(t8)[r0:r0 + 64, 0:1024])
        tfc = cp(VE, fch0[:, 0:16], ftok(t8)[r0:r0 + 64, 0:16])
        tF = mm(pb[3][0:64, 0:16], sb["lt64"][:], fch0[:, 0:16], True, True,
                waits=(tfc,), bank=3)
        tFe = cp(VE, Fch, pb[3][0:64, 0:16], waits=(tF,))
        rdbank(3, tFe)
        tFT = tp_pe(FTch, Fch, f32=True)
        tGT = tt(GTch, lnbT[0:16, ccol], FTch, ALU.subtract, waits=(tFT,))
        twe = act(wendT, GTch, AF.Exp, bias=FTch[0:16, 63:64], waits=(tGT,))
        tdc = act(dTcol, FTch[0:16, 63:64], AF.Exp, waits=(tFT,))
        teg = act(EGch, GTch, AF.Exp, waits=(tGT,))
        tef = act(eFtok, Fch, AF.Exp, waits=(tFe,))
        tw1 = mm(pb[3][0:64, 0:512], wendT, sb["e16"][0:16, 0:512], True, True,
                 waits=(twe,), bank=3)
        tw2 = mm(pb[2][0:64, 0:512], wendT, sb["e16"][0:16, 512:1024], True, True,
                 waits=(twe,), bank=2)
        tk1 = tt(kwb[:, 0:512], kch[:, 0:512], pb[3][0:64, 0:512], ALU.mult,
                 waits=(tw1,))
        tk2 = tt(kwb[:, 512:1024], kch[:, 512:1024], pb[2][0:64, 0:512],
                 ALU.mult, waits=(tw2,))
        tv1 = tt(vwb[:, 0:512], vch[:, 0:512], pb[3][0:64, 0:512], ALU.mult,
                 waits=(tw1,))
        tv2 = tt(vwb[:, 512:1024], vch[:, 512:1024], pb[2][0:64, 0:512],
                 ALU.mult, waits=(tw2,))
        rdbank(3, tv1)
        rdbank(2, tv2)
        return t8, r0, (tdc, tef, teg, tk2, tv2)

    def chunk_state(t8, r0, toks):
        tdc = toks[0]
        for p in range(8):
            mm(pb[4][:, p:p + 1], sb["pairsel"][:, 128 * p:128 * p + 128], dTcol,
               p == 0, p == 7, waits=(tdc,), bank=4 if p == 0 else None)
        tdr = cp(VE, sb["statc"][:, 0:8], pb[4][:, 0:8])
        rdbank(4, tdr)
        for p in range(8):
            pbank, bk = (pb[5], 5) if p < 4 else (pb[6], 6)
            pcol = 128 * (p % 4)
            for hh in range(2):
                h = 2 * p + hh
                col = slice(64 * h, 64 * h + 64)
                mm(pbank[64 * hh:64 * hh + 64, pcol:pcol + 64],
                   kch[:, col], kwb[:, col],
                   True, True, tp=(0, 64 * hh), bank=bk if (p % 4 == 0 and hh == 0) else None)
                mm(pbank[64 * hh:64 * hh + 64, pcol + 64:pcol + 128],
                   kch[:, col], vwb[:, col],
                   True, True, tp=(0, 64 * hh))
        tup = None
        for p in range(8):
            pbank = pb[5] if p < 4 else pb[6]
            pcol = 128 * (p % 4)
            tup = stt(sss[:, 128 * p:128 * p + 128], sss[:, 128 * p:128 * p + 128],
                      sb["statc"][:, p:p + 1], pbank[:, pcol:pcol + 128], ALU.mult, ALU.add,
                      waits=(tdr,))
        rdbank(5, tup)
        rdbank(6, tup)
        return tup

    def chunk_cg_o(ci, t8, r0, toks):
        ccol = slice(64 * ci, 64 * ci + 64)
        tef, teg = toks[1], toks[2]
        te1 = mm(pb[3][0:64, 0:512], EGch, sb["e16"][0:16, 0:512], True, True,
                 waits=(teg,), bank=3)
        te2 = mm(pb[2][0:64, 0:512], EGch, sb["e16"][0:16, 512:1024], True, True,
                 waits=(teg,), bank=2)
        tt(ktil[:, 0:512], kch[:, 0:512], pb[3][0:64, 0:512], ALU.mult)
        tk = tt(ktil[:, 512:1024], kch[:, 512:1024], pb[2][0:64, 0:512],
                ALU.mult)
        tt(vtil[:, 0:512], vch[:, 0:512], pb[3][0:64, 0:512], ALU.mult)
        tv = tt(vtil[:, 512:1024], vch[:, 512:1024], pb[2][0:64, 0:512],
                ALU.mult)
        rdbank(3, tv)
        rdbank(2, tv)
        for p in range(8):
            tpe = tp_pe(eFT2, eFtok[0:64, 2 * p:2 * p + 2], f32=True)
            tcf = mm(pb[4][:, 64 * p:64 * p + 64], sb["blocksel"][:], eFT2, True, True,
                     waits=(tpe, tef), bank=4 if p == 0 else None)
        tcfe = cp(VE, cfBIG, pb[4][:, 0:512])
        rdbank(4, tcfe)
        a1 = cp(VE, lhK[0:64, :], ktil[:, 0:1024], waits=(tk,))
        a2 = cp(VE, lhV[0:64, :], vtil[:, 0:1024], waits=(tv,))
        for p in range(8):
            cp(VE, lhcg[:, 128 * p:128 * p + 64], kTb(p)[:, ccol])
            cp(VE, lhcg[:, 128 * p + 64:128 * p + 128], sss[:, 128 * p:128 * p + 64])
            cp(VE, lhO[:, 128 * p:128 * p + 64], kTb(p)[:, ccol])
            a3 = cp(VE, lhO[:, 128 * p + 64:128 * p + 128],
                    sss[:, 128 * p + 64:128 * p + 128])
        sch.do(VE, lambda e: e.memset(cgX[:], 0.0), sig="vs")
        for p in range(8):
            cp(VE, cgR[:, 64 * p:64 * p + 64], qTb(p)[:, ccol])
        ti = cp(VE, cgP[:], cgR[:])
        tq = act(zsq, cgR[:], AF.Square, waits=(ti,))
        tn = mm(pb[3][0:2, 0:512], sb["hsum2"][:], zsq, True, True, waits=(tq,), bank=3)
        tr0 = cp(VE, rwrs, pb[3][0:2, 0:512], waits=(tn,))
        rdbank(3, tr0)

        def Amul(pvec, dst_psumA, dst_psumB, dst_ap, lhs1, lhsk):
            for p in range(8):
                mm(dst_psumA[:, 64 * p:64 * p + 64], lhs1[0:64, 128 * p:128 * p + 128],
                   pvec[0:64, 64 * p:64 * p + 64], True, True, tp=(0, 0),
                   bank=0 if p == 0 else None)
                mm(dst_psumB[:, 64 * p:64 * p + 64], lhs1[64:128, 128 * p:128 * p + 128],
                   pvec[64:128, 64 * p:64 * p + 64], True, True, tp=(64, 0),
                   bank=1 if p == 0 else None)
            g1 = tt(Gext, sb["upperext"][:], dst_psumA[:, 0:512], ALU.mult)
            rdbank(0, g1)
            g2 = tt(gscr, sb["upperext"][:], dst_psumB[:, 0:512], ALU.mult)
            rdbank(1, g2)
            for p in range(8):
                for hh in range(2):
                    h = 2 * p + hh
                    gsrc = Gext if hh == 0 else gscr
                    mm(pb[2][64 * hh:64 * hh + 64, 64 * p:64 * p + 64],
                       lhsk[:, 64 * h:64 * h + 64],
                       gsrc[:, 64 * p:64 * p + 64], True, True, tp=(0, 64 * hh),
                       bank=2 if (p == 0 and hh == 0) else None)
            m1 = tt(tmp1, cfBIG, pb[2][:, 0:512], ALU.mult)
            rdbank(2, m1)
            m2 = tt(tmp2, pvec[:], lambBIG, ALU.mult)
            return tt(dst_ap, tmp1, tmp2, ALU.add, waits=(m1, m2))

        for it in range(30):
            tap_ = Amul(cgP, pb[0], pb[1], cgAp, lhcg, lhK)
            z1 = tt(zsq, cgP[:], cgAp, ALU.mult, waits=(tap_,))
            tn1 = mm(pb[3][0:2, 0:512], sb["hsum2"][:], zsq, True, True, waits=(z1,),
                     bank=3)
            tp1 = ts_(rwpap, pb[3][0:2, 0:512], 1e-12, None, ALU.add)
            rdbank(3, tp1)
            ta0 = sch.do(VE, lambda e: e.reciprocal(out=rwt, in_=rwpap),
                         waits=(tp1,), sig="vs")
            ta1 = tt(rwa, rwrs, rwt, ALU.mult, waits=(ta0,))
            tb1 = mm(pb[4][:, 0:512], sb["blocksel"][:], rwa, True, True, waits=(ta1,),
                     bank=4)
            u1 = tt(tmp1, cgP[:], pb[4][:, 0:512], ALU.mult)
            u2 = tt(cgX[:], cgX[:], tmp1, ALU.add, waits=(u1,))
            u3 = tt(tmp2, cgAp, pb[4][:, 0:512], ALU.mult)
            rdbank(4, u3)
            u4 = tt(cgR[:], cgR[:], tmp2, ALU.subtract, waits=(u3,))
            q1 = act(zsq, cgR[:], AF.Square, waits=(u4,))
            tn2 = mm(pb[3][0:2, 0:512], sb["hsum2"][:], zsq, True, True, waits=(q1,),
                     bank=3)
            tp2 = ts_(rwt, rwrs, 1e-12, None, ALU.add)
            tz2 = sch.do(VE, lambda e: e.reciprocal(out=rwa, in_=rwt),
                         waits=(tp2, tb1), sig="vs")
            tb2 = tt(rwb, pb[3][0:2, 0:512], rwa, ALU.mult, waits=(tz2, tn2))
            tr2 = cp(VE, rwrs, pb[3][0:2, 0:512], waits=(tb2,))
            rdbank(3, tr2)
            tb3 = mm(pb[4][:, 0:512], sb["blocksel"][:], rwb, True, True, waits=(tb2,),
                     bank=4)
            u5 = tt(tmp3, cgP[:], pb[4][:, 0:512], ALU.mult)
            rdbank(4, u5)
            u6 = tt(cgP[:], cgR[:], tmp3, ALU.add, waits=(u5,))
        to = Amul(cgX, pb[0], pb[1], cgAp, lhO, lhV)
        o1 = tt(zsq, cfBIG, pb[2][:, 0:512], ALU.mult)
        for p in range(8):
            oc = cp(VE, oTb(p)[:, ccol], zsq[:, 64 * p:64 * p + 64], waits=(o1,))
        return oc

    def attnb_gate_out(half):
        dg1 = dma(GP, wstr[:, 0:512].rearrange("p (k m) -> p k m", k=8),
                  inn["gw1"].rearrange("(k p) m -> p k m", p=128), sig="dg")
        for nch in range(2):
            for k in range(8):
                mm(pb[0][0:64, 0:512], wstr[:, 64 * k:64 * k + 64],
                   lnb3(k)[:, 512 * nch:512 * nch + 512], k == 0, k == 7,
                   waits=(dg1,) if k == 0 else (), bank=0)
            tg = act(tmpd[0:64, 0:512], pb[0][0:64, 0:512], AF.Copy)
            rdbank(0, tg)
            dg2 = dma(GP, wstr2[0:64, 0:1024].rearrange("p (k m) -> p k m", k=1),
                      inn["gw2"].rearrange("(k p) m -> p k m", p=64), sig="dg")
            for p in range(8):
                mm(pb[1][:, 0:512], wstr2[0:64, 128 * p:128 * p + 128],
                   tmpd[0:64, 0:512], True, True, waits=(dg2, tg), bank=1)
                tsg = act(gateb(p)[:, 512 * nch:512 * nch + 512], pb[1][:, 0:512],
                          AF.Sigmoid)
                rdbank(1, tsg)
        for p in range(8):
            for nch in range(2):
                sl = slice(512 * nch, 512 * nch + 512)
                t1 = tt(tmp1, oTb(p)[:, sl], gateb(p)[:, sl], ALU.mult)
                tq = act(tmp2, tmp1, AF.Square, waits=(t1,))
                tn = mm(pb[1][0:2, 0:512], sb["hsum2"][:], tmp2, True, True, waits=(tq,),
                        bank=1)
                t2 = ts_(sb["rowf"][0:2, 0:512], pb[1][0:2, 0:512], 1.0 / 64, 1e-5,
                         ALU.mult, ALU.add)
                rdbank(1, t2)
                t3 = act(tmpc[0:2, 0:512], sb["rowf"][0:2, 0:512], AF.Sqrt)
                t4 = sch.do(VE, lambda e: e.reciprocal(out=sb["rowf"][0:2, 0:512],
                                                       in_=tmpc[0:2, 0:512]), sig="vs")
                t5 = mm(pb[2][:, 0:512], sb["blocksel"][:], sb["rowf"][0:2, 0:512],
                        True, True, waits=(t4,), bank=2)
                t6 = tt(tmp3, tmp1, pb[2][:, 0:512], ALU.mult)
                rdbank(2, t6)
                t7 = ts_(oTb(p)[:, sl], tmp3, sb["nrmw"][:, p:p + 1], None, ALU.mult,
                         waits=(t6,))
        da, db = wload("wbo")
        for mo in range(8):
            for nch in range(2):
                for k in range(8):
                    wv = wstr if mo < 4 else wstr2
                    mc = 128 * (mo % 4)
                    mm(pb[0][:, 0:512], wv[:, 512 * k + mc:512 * k + mc + 128],
                       oTb(k)[:, 512 * nch:512 * nch + 512], k == 0, k == 7,
                       waits=(da, db) if k == 0 else (), bank=0)
                col0 = 1024 * half + 512 * nch
                tl = dma(GP, tmpa[:, 0:512],
                         xresB.ap()[128 * mo:128 * (mo + 1), col0:col0 + 512], sig="dg")
                t1 = tt(tmpc[:, 0:512], pb[0][:, 0:512], tmpa[:, 0:512], ALU.add,
                        waits=(tl,))
                rdbank(0, t1)
                dma(GP, xresA.ap()[128 * mo:128 * (mo + 1), col0:col0 + 512],
                    tmpc[:, 0:512], waits=(t1,), sig="dg")

    for half in range(2):
        sch.phase = f"p3a{half}"
        attnb_half_setup(half)
        for ci in range(16):
            t8, r0, toks = chunk_F(ci)
            chunk_state(t8, r0, toks)
        if half == 1:
            cpv = cp(VE, tmpc[:, 0:1024], sss[:, 0:1024])
            dsend = dma(GP, pkg_send.ap(), tmpc[:, 0:1024], waits=(cpv,), sig="dg")
            def cc_ag(e):
                return e.collective_compute("AllGather", ALU.bypass,
                                            replica_groups=[list(range(8))],
                                            ins=[pkg_send.ap()], outs=[pkg_recv.ap()])
            tcc = sch.do(GP, cc_ag, waits=(dsend,), sig="cc", n=1)
            tz = sch.do(VE, lambda e: e.memset(sss[:, 0:1024], 0.0), waits=(tcc,), sig="vs")
            for j in range(4):
                tl = dma(GP, tmpc[:, 0:1024],
                         pkg_recv.ap()[128 * 2 * j:128 * 2 * j + 128, :],
                         waits=(tcc,), sig="dg")
                tz = stt(sss[:, 0:1024], tmpc[:, 0:1024], sb["gamrep"][:, 2 + j:3 + j],
                         sss[:, 0:1024], ALU.mult, ALU.add, waits=(tl, tz))
    for half in range(2):
        sch.phase = f"p3b{half}"
        attnb_half_setup(half)
        for ci in range(16):
            t8, r0, toks = chunk_F(ci)
            chunk_cg_o(ci, t8, r0, toks)
            chunk_state(t8, r0, toks)
        attnb_gate_out(half)
    tok_ab = sch.barrier()
    if upto == "attnb":
        tap_x4 = taps.get("t_x4")
        if tap_x4 is not None:
            for dt_ in range(8):
                dma(GP, tap_x4[128 * dt_:128 * (dt_ + 1), :],
                    xresA.ap()[128 * dt_:128 * (dt_ + 1), :], sig="dg")
        return _finish_early()

    sch.phase = "f2"
    tok_ffn2 = ffn_pass(xresA, xresB)

    sch.phase = "o"
    carry = {"l": None, "e": None}
    for it in range(16):
        for dt_ in range(8):
            bk = 2 + (dt_ % 2)
            tl = dma(GP, tmpa[:, 0:128],
                     xresB.ap()[128 * dt_:128 * (dt_ + 1), 128 * it:128 * it + 128],
                     waits=(carry["l"],), sig="dg")
            psl = pb[bk][:, 0:128]
            ttp = transpose(psl, tmpa[:, 0:128], sb["ident32"][:], waits=(tl,), bank=bk)
            carry["l"] = ttp
            tev = cp(VE, tmpb[:, 0:128], psl, waits=(ttp, carry["e"]))
            rdbank(bk, tev)
            tdm = dma(GP, out_own[128 * it:128 * (it + 1), 128 * dt_:128 * (dt_ + 1)],
                      tmpb[:, 0:128], waits=(tev,), sig="dg")
            carry["e"] = tdm

    end_tok = sch.barrier()
    final_waits = [(k, v) for k, v in sch.counts.items()
                   if k.startswith(("ds", "dg"))] + [end_tok]
    sch.do(SY, lambda e: e.nop(), waits=tuple(final_waits))
    print('SEM COUNTS:', {k: v for k, v in sch.counts.items()})
    sch.finalize()
    return nc


def kernel(**inputs):
    maps = _prep_inputs(inputs)
    nc = build_program()
    res = run_bass_kernel_spmd(nc, maps, core_ids=list(range(8)))
    out = np.zeros((B, S, D), np.float32)
    for c in range(8):
        b, r = c // 2, c % 2
        out[b, 2048 * r:2048 * (r + 1)] = np.asarray(res.results[c]["out_own"])
    return out


# version 02
# speedup vs baseline: 1.0994x; 1.0994x over previous
import contextlib
import dataclasses
import math
import numpy as np
import ml_dtypes

import concourse.bass as bass
import concourse.mybir as mybir
from concourse.bass_utils import run_bass_kernel_spmd

F32 = mybir.dt.float32
BF16 = mybir.dt.bfloat16
AF = mybir.ActivationFunctionType
ALU = mybir.AluOpType
NBF = ml_dtypes.bfloat16

B, S, D, H = 4, 4096, 1024, 16
HD = 64
SH = S // 2
LB = 1408
DMIN = -512
WT1 = LB + 1


def _rel_bucket(d):
    d = np.asarray(d)
    rp_f = np.maximum(d.astype(np.float64), 1.0)
    large = 16 + (np.log(rp_f / 16.0) / np.log(8.0) * 16.0).astype(np.int64)
    return np.where(d < 16, d, np.minimum(large, 31)).astype(np.int64)

def _host_consts():
    c = {}
    eye = np.eye(128, dtype=np.float32)
    c["ident32"] = eye
    c["identb"] = eye.astype(NBF)
    inv_freq = 1.0 / (10000.0 ** (np.arange(0, HD, 2, dtype=np.float64) / HD))
    freqs = np.outer(np.arange(S, dtype=np.float64), inv_freq)
    emb = np.concatenate([freqs, freqs], axis=-1)
    cos = np.cos(emb).T.astype(np.float32)
    sin = np.sin(emb).T.astype(np.float32)
    c["cosb"] = np.ascontiguousarray(np.concatenate([cos, cos], 0)).astype(NBF)
    c["sinb"] = np.ascontiguousarray(np.concatenate([sin, sin], 0)).astype(NBF)
    R = np.zeros((64, 64), np.float32)
    for i in range(32):
        R[i, 32 + i] = -1.0
        R[32 + i, i] = 1.0
    rotT = np.zeros((128, 128), np.float32)
    rotT[0:64, 0:64] = R.T
    rotT[64:128, 64:128] = R.T
    c["rotT"] = rotT.astype(NBF)
    ohcorr = np.zeros((32, LB), np.float32)
    m0 = np.zeros((16, LB), np.float32)
    for j in range(LB):
        d = j + DMIN
        if d >= 0:
            m0[:, j] = 1.0
        if 0 <= d <= 112:
            ohcorr[int(_rel_bucket(d)), j] += 1.0
            ohcorr[31, j] -= 1.0
    c["ohcorr"] = ohcorr
    c["m0rep"] = m0
    hrep = np.zeros((16, 2048), np.float32)
    for h in range(16):
        hrep[h, 128 * h:128 * (h + 1)] = 1.0
    c["hrep8"] = hrep
    c["lt64"] = np.triu(np.ones((64, 64), np.float32))
    ue = np.ones((128, 512), np.float32)
    tri = (np.arange(64)[:, None] <= np.arange(64)[None, :]).astype(np.float32)
    for blk in range(8):
        ue[0:64, 64 * blk:64 * blk + 64] = tri
    c["upperext"] = ue.astype(NBF)
    hs2 = np.zeros((128, 2), np.float32)
    hs2[0:64, 0] = 1.0
    hs2[64:128, 1] = 1.0
    c["hsum2"] = hs2.astype(NBF)
    bsel = np.zeros((2, 128), np.float32)
    bsel[0, 0:64] = 1.0
    bsel[1, 64:128] = 1.0
    c["blocksel"] = bsel
    e16 = np.zeros((16, 1024), np.float32)
    for h in range(16):
        e16[h, 64 * h:64 * h + 64] = 1.0
    c["e16"] = e16.astype(NBF)
    psel = np.zeros((16, 1024), np.float32)
    for p in range(8):
        psel[2 * p, 128 * p:128 * p + 64] = 1.0
        psel[2 * p + 1, 128 * p + 64:128 * p + 128] = 1.0
    c["pairsel"] = psel
    i64 = np.zeros((64, 1024), np.float32)
    for h in range(16):
        i64[:, 64 * h:64 * h + 64] = np.eye(64)
    c["i64rep"] = i64.astype(NBF)
    c["fconst"] = np.repeat(np.array([[0.0, 1.0, -1.0, 0.125, 1.0 / 64, 1.0 / 1024, 1e-5,
                                       1e-6, 1e-12, 0.25, 0.5, 2.0, math.sqrt(2.0), 0.0,
                                       0.0, 0.0]], np.float32), 128, axis=0)
    return c

def _tile_wT(w, mt, kt):
    wT = np.ascontiguousarray(w.T)
    return np.ascontiguousarray(wT.reshape(kt, 128, mt, 128).transpose(2, 0, 1, 3))

def _prep_inputs(inp):
    consts = _host_consts()
    x = np.ascontiguousarray(np.asarray(inp["L0_attn_a_x"], dtype=np.float32))
    Wq = np.asarray(inp["L0_attn_a_Wq"], np.float32)
    Wk = np.asarray(inp["L0_attn_a_Wk"], np.float32)
    Wv = np.asarray(inp["L0_attn_a_Wv"], np.float32)
    Wo = np.asarray(inp["L0_attn_a_Wo"], np.float32)
    rel = np.asarray(inp["L0_attn_a_rel_emb"], np.float32)
    sh = dict(consts)
    sh["wbq"] = np.asarray(inp["L0_attn_b_Wq"], np.float32)
    sh["wbk"] = np.asarray(inp["L0_attn_b_Wk"], np.float32)
    sh["wbv"] = np.asarray(inp["L0_attn_b_Wv"], np.float32)
    sh["wfb"] = np.ascontiguousarray(
        np.concatenate([np.asarray(inp["L0_attn_b_Wf"], np.float32),
                        np.asarray(inp["L0_attn_b_Wbet"], np.float32)], axis=1))
    sh["wbo"] = np.asarray(inp["L0_attn_b_Wo"], np.float32)
    sh["gw1"] = np.asarray(inp["L0_attn_b_gate_w1"], np.float32)
    sh["gw2"] = np.asarray(inp["L0_attn_b_gate_w2"], np.float32)
    sh["delta"] = np.asarray(inp["L0_attn_b_delta"], np.float32).reshape(1, 16)
    sh["lambp"] = np.ascontiguousarray(
        np.asarray(inp["L0_attn_b_lamb_params"], np.float32).reshape(8, 128).T)
    sh["nrmw"] = np.ascontiguousarray(
        np.asarray(inp["L0_attn_b_norm_weight"], np.float32).reshape(8, 128).T)
    sh["w1t"] = _tile_wT(np.asarray(inp["L0_ffn_w1_w"], np.float32), 32, 8)
    sh["w2t"] = _tile_wT(np.asarray(inp["L0_ffn_w2_w"], np.float32), 32, 8)
    sh["h1t"] = _tile_wT(np.asarray(inp["L0_ffn_h1_w"], np.float32), 32, 32)
    sh["h2t"] = _tile_wT(np.asarray(inp["L0_ffn_h2_w"], np.float32), 32, 32)
    sh["w3t"] = _tile_wT(np.asarray(inp["L0_ffn_w3_w"], np.float32), 8, 32)
    for nm, src in [("s1c", "L0_ffn_w1_s"), ("s2c", "L0_ffn_w2_s"),
                    ("sh1c", "L0_ffn_h1_s"), ("sh2c", "L0_ffn_h2_s")]:
        sh[nm] = np.ascontiguousarray(np.asarray(inp[src], np.float32).reshape(32, 128).T)
    sh["s3c"] = np.ascontiguousarray(np.asarray(inp["L0_ffn_w3_s"], np.float32).reshape(8, 128).T)
    maps = []
    for c in range(8):
        b, r = c // 2, c % 2
        m = dict(sh)
        m["x_full"] = x[b]
        m["x_own"] = np.ascontiguousarray(x[b, 2048 * r:2048 * (r + 1)])
        m["wqm"] = Wq
        m["wkm"] = Wk
        m["wvm"] = Wv
        m["wom"] = Wo
        m["rel16"] = rel
        m["gam"] = np.array([[1.0 - r, float(r)]], np.float32)
        gs = np.zeros((1, 4), np.float32)
        if r == 1:
            gs[0, b] = 1.0
        m["gsel"] = gs
        maps.append(m)
    return maps


ENGINES = ("sync", "vector", "scalar", "tensor", "gpsimd")
SY, VE, SC, TE, GP = ENGINES[0], ENGINES[1], ENGINES[2], ENGINES[3], ENGINES[4]

class Sched:
    def __init__(self, nc, es):
        self.nc = nc
        self.es = es
        self.ops = {e: [] for e in ENGINES}
        self.sems = {}
        self.counts = {}
        self.pending = {e: None for e in ENGINES}
        self.phase = ""
        self.serial = False
        self.last_tok = None
        self.bank_rd = [None] * 8

    def _sem(self, name):
        if name not in self.sems:
            self.sems[name] = self.es.enter_context(self.nc.semaphore(name))
            self.counts[name] = 0
        return self.sems[name]

    def do(self, eng, fn, waits=(), sig=None, n=1):
        import traceback
        _st = traceback.extract_stack()
        _lbl = ";".join(f"{f.lineno}" for f in _st[-4:-1])
        if sig is not None:
            if sig in ("ts", "vs", "ss", "ds", "dg", "gs"):
                sig = sig + self.phase
            self._sem(sig)
        ws = [w for w in waits if w is not None]
        ws = [(s, self.counts[s]) if s.startswith(("ds", "dg")) else (s, v) for (s, v) in ws]
        if self.pending[eng] is not None:
            ws.append(self.pending[eng])
            self.pending[eng] = None
        if self.serial and self.last_tok is not None:
            ws.append(self.last_tok)
        for w in ws:
            self._sem(w[0])
        best = {}
        for s, v in ws:
            best[s] = max(best.get(s, 0), v)
        tok = None
        if sig is not None:
            self.counts[sig] += n
            tok = (sig, self.counts[sig])
        self.ops[eng].append((tuple(best.items()), fn, sig, n, _lbl))
        if sig is not None:
            self.last_tok = tok
        return tok

    def barrier(self):
        self._sem("bar")
        tgt = self.counts["bar"] + len(ENGINES)
        for e in ENGINES:
            self.do(e, lambda en: en.nop(), sig="bar")
        tok = ("bar", tgt)
        for e in ENGINES:
            self.pending[e] = tok
        return tok

    def finalize(self):
        with self.nc.Block() as block:
            for e in ENGINES:
                ops = self.ops[e]
                def body(eng, ops=ops):
                    for waits, fn, sig, n, lbl in ops:
                        for (s, v) in waits:
                            eng.wait_ge(self.sems[s], v)
                        try:
                            ins = fn(eng)
                        except Exception:
                            print(f"EMIT FAIL at plan lines {lbl}")
                            raise
                        if sig is not None and ins is not None:
                            ins.then_inc(self.sems[sig], n)
                getattr(block, e)(body)


def build_program(debug_taps=(), upto=None, serial=False, serial_secs=("E",)):
    nc = bass.Bass("TRN2", target_bir_lowering=False, debug=False)
    es = contextlib.ExitStack()
    sch = Sched(nc, es)
    sch.serial = serial

    inn = {}
    def din(name, shape, dtype=F32):
        inn[name] = nc.dram_tensor(name, list(shape), dtype, kind="ExternalInput").ap()
        return inn[name]

    din("x_full", [S, D]); din("x_own", [SH, D])
    din("wqm", [D, D]); din("wkm", [D, D]); din("wvm", [D, D]); din("wom", [D, D])
    din("rel16", [32, 16])
    din("wbq", [D, D]); din("wbk", [D, D]); din("wbv", [D, D]); din("wfb", [D, 32])
    din("wbo", [D, D]); din("gw1", [D, 64]); din("gw2", [64, D])
    din("delta", [1, 16]); din("lambp", [128, 8]); din("nrmw", [128, 8])
    din("gam", [1, 2]); din("gsel", [1, 4])
    din("w1t", [32, 8, 128, 128]); din("w2t", [32, 8, 128, 128])
    din("h1t", [32, 32, 128, 128]); din("h2t", [32, 32, 128, 128]); din("w3t", [8, 32, 128, 128])
    din("s1c", [128, 32]); din("s2c", [128, 32]); din("sh1c", [128, 32]); din("sh2c", [128, 32])
    din("s3c", [128, 8])
    din("ident32", [128, 128]); din("identb", [128, 128], BF16)
    din("cosb", [128, S], BF16); din("sinb", [128, S], BF16); din("rotT", [128, 128], BF16)
    din("ohcorr", [32, LB]); din("m0rep", [16, LB]); din("hrep8", [16, 2048])
    din("lt64", [64, 64]); din("upperext", [128, 512], BF16); din("hsum2", [128, 2], BF16)
    din("blocksel", [2, 128]); din("e16", [16, 1024], BF16); din("pairsel", [16, 1024])
    din("i64rep", [64, 1024], BF16)
    din("fconst", [128, 16])

    out_own = nc.dram_tensor("out_own", [SH, D], F32, kind="ExternalOutput").ap()
    taps = {}
    for tname, tshape, tdt in debug_taps:
        taps[tname] = nc.dram_tensor(tname, list(tshape), tdt, kind="ExternalOutput").ap()

    toeh = [nc.dram_tensor(f"toeh{h}", [128 * WT1], BF16) for h in range(16)]
    attnout = nc.dram_tensor("attnout", [D, S], F32)
    xresA = nc.dram_tensor("xresA", [D, SH], F32)
    xresB = nc.dram_tensor("xresB", [D, SH], F32)
    wbf = nc.dram_tensor("wbf", [48 * 1024 * 1024], BF16)
    pkg_send = nc.dram_tensor("pkg_send", [128, 1024], F32)
    pkg_recv = nc.dram_tensor("pkg_recv", [1024, 1024], F32)
    woff = {}
    off = 0
    for nm, ntile in [("w1t", 32 * 8), ("w2t", 32 * 8), ("h1t", 32 * 32), ("h2t", 32 * 32),
                      ("w3t", 8 * 32)]:
        woff[nm] = off
        off += ntile * 128 * 128

    sb = {}
    def sball(name, shape, dtype=F32):
        sb[name] = nc.alloc_sbuf_tensor("s_" + name, list(shape), dtype)
        return sb[name]

    for nm, shp, dt in [("ident32", [128, 128], F32), ("identb", [128, 128], BF16),
                        ("rotT", [128, 128], BF16), ("lt64", [64, 64], F32),
                        ("upperext", [128, 512], BF16), ("hsum2", [128, 2], BF16),
                        ("blocksel", [2, 128], F32), ("e16", [16, 1024], BF16),
                        ("pairsel", [16, 1024], F32), ("fconst", [128, 16], F32),
                        ("delta", [1, 16], F32), ("gam", [1, 2], F32), ("gsel", [1, 4], F32),
                        ("lambp", [128, 8], F32), ("nrmw", [128, 8], F32),
                        ("s1c", [128, 32], F32), ("s2c", [128, 32], F32),
                        ("sh1c", [128, 32], F32), ("sh2c", [128, 32], F32),
                        ("s3c", [128, 8], F32), ("rel8s", [32, 16], F32)]:
        sball(nm, shp, dt)
    sball("onesb", [1, 128], BF16)
    sball("onesrowf", [1, 128], F32)
    sball("onescolf", [128, 1], F32)
    sball("lambcol", [128, 8], F32)
    sball("deltarep", [128, 16], F32)
    sball("gamrep", [128, 8], F32)
    sball("statc", [128, 16], F32)
    sball("rowf", [2, 1024], F32)

    fv = [0.0, 1.0, -1.0, 0.125, 1.0 / 64, 1.0 / 1024, 1e-5, 1e-6, 1e-12, 0.25, 0.5, 2.0,
          math.sqrt(2.0)]
    for i, v in enumerate(fv):
        nc.const_aps.register(F32, float(v), sb["fconst"][:, i:i + 1])

    pb = [nc.alloc_psum_tensor(f"pb{i}", [128, 512], F32) for i in range(7)]
    pbf = nc.alloc_psum_tensor("pbf", [128, 1024], BF16)

    ar1 = [nc.alloc_sbuf_tensor(f"ar1_{i}", [128, 4096], BF16) for i in range(8)]
    ar2 = [nc.alloc_sbuf_tensor(f"ar2_{i}", [128, 4096], BF16) for i in range(4)]
    ar3 = [nc.alloc_sbuf_tensor(f"ar3_{i}", [128, 4352], BF16) for i in range(4)]
    ar4 = [nc.alloc_sbuf_tensor(f"ar4_{i}", [128, 4096], BF16) for i in range(2)]
    qg = nc.alloc_sbuf_tensor("qg", [128, 1024], BF16)
    qg2 = nc.alloc_sbuf_tensor("qg2", [128, 512], BF16)
    ctxp = nc.alloc_sbuf_tensor("ctxp", [128, 4096], BF16)
    wstr = nc.alloc_sbuf_tensor("wstr", [128, 4096], BF16)
    wstr2 = nc.alloc_sbuf_tensor("wstr2", [128, 4096], BF16)
    tmpa = nc.alloc_sbuf_tensor("tmpa", [128, 1024], F32)
    tmpb = nc.alloc_sbuf_tensor("tmpb", [128, 1024], F32)
    tmpc = nc.alloc_sbuf_tensor("tmpc", [128, 1024], F32)
    tmpd = nc.alloc_sbuf_tensor("tmpd", [128, 1536], BF16)
    lnt = tmpc[:].bitcast(BF16)[:, 0:1024]
    pax0 = nc.alloc_sbuf_tensor("pax0", [128, 512], BF16)
    pax1 = nc.alloc_sbuf_tensor("pax1", [128, 512], BF16)
    costile = nc.alloc_sbuf_tensor("costile", [128, 512], BF16)
    sintile = nc.alloc_sbuf_tensor("sintile", [128, 512], BF16)

    ohc = ar3[0].ap().bitcast(F32)[0:32, 0:LB]
    m0s = ar3[1].ap().bitcast(F32)[0:16, 0:LB]
    hreps = ar3[2].ap().bitcast(F32)[0:16, 0:2048]
    bvals = ar3[3].ap().bitcast(F32)[0:16, 0:LB]

    def dma(queue, out, in_, waits=(), sig="ds", **kw):
        return sch.do(queue, lambda e: e.dma_start(out=out, in_=in_, **kw), waits, sig, 16)

    def mm(out, lhsT, rhs, start, stop, waits=(), sig="ts", tp=None, bank=None):
        w = list(waits)
        if bank is not None and start and sch.bank_rd[bank] is not None:
            w.append(sch.bank_rd[bank])
        return sch.do(TE, lambda e: e.matmul(out, lhsT, rhs, start=start, stop=stop,
                                             tile_position=tp), w, sig, 1)

    def rdbank(bank, tok):
        sch.bank_rd[bank] = tok

    def cp(eng, out, in_, waits=(), sig=None):
        if sig is None:
            sig = {"vector": "vs", "scalar": "ss", "gpsimd": "gs"}[eng]
        return sch.do(eng, lambda e: e.tensor_copy(out=out, in_=in_), waits, sig, 1)

    def act(out, in_, func, waits=(), sig="ss", bias=0.0, scale=1.0, accum=None):
        return sch.do(SC, lambda e: e.activation(out, in_, func, bias=bias, scale=scale,
                                                 accum_out=accum), waits, sig, 1)

    def tt(out, a, b_, op, waits=(), sig="vs"):
        return sch.do(VE, lambda e: e.tensor_tensor(out=out, in0=a, in1=b_, op=op),
                      waits, sig, 1)

    def ts_(out, a, s1, s2, op0, op1=ALU.bypass, waits=(), sig="vs", accum=None):
        return sch.do(VE, lambda e: e.tensor_scalar(out=out, in0=a, scalar1=s1, scalar2=s2,
                                                    op0=op0, op1=op1, accum_out=accum),
                      waits, sig, 1)

    def stt(out, in0, scalar, in1, op0, op1, waits=(), sig="vs", accum=None):
        return sch.do(VE, lambda e: e.scalar_tensor_tensor(out=out, in0=in0, scalar=scalar,
                                                           in1=in1, op0=op0, op1=op1,
                                                           accum_out=accum), waits, sig, 1)

    def transpose(psl, src, ident, waits=(), bank=None):
        w = list(waits)
        if bank is not None and sch.bank_rd[bank] is not None:
            w.append(sch.bank_rd[bank])
        return sch.do(TE, lambda e: e.transpose(psl, src, ident), w, "ts", 1)


    def _finish_early():
        for it in range(16):
            dma(SY, out_own[128 * it:128 * (it + 1), :], tmpa[:].bitcast(F32)[:, 0:1024]
                if False else tmpa[:], waits=())
        et = sch.barrier()
        sch.do(SY, lambda e: e.nop(), waits=(("ds", sch.counts.get("ds", 0)),
                                             ("dg", sch.counts.get("dg", 0)), et))
        sch.finalize()
        return nc
    for nm in ["ident32", "identb", "rotT", "lt64", "upperext", "hsum2", "blocksel", "e16",
               "pairsel", "fconst", "delta", "lambp", "nrmw", "s1c", "s2c", "sh1c", "sh2c",
               "s3c", "gam", "gsel", "rel8s"]:
        src = inn["rel16"] if nm == "rel8s" else inn[nm]
        tok_con = dma(SY, sb[nm][:], src)
    tok_oh = dma(SY, ohc, inn["ohcorr"])
    tok_m0 = dma(SY, m0s, inn["m0rep"])
    tok_hr = dma(SY, hreps, inn["hrep8"])

    t1 = sch.do(VE, lambda e: e.memset(sb["onesb"][:], 1.0), sig="vs")
    t2 = sch.do(VE, lambda e: e.memset(sb["onesrowf"][:], 1.0), sig="vs")
    sch.do(VE, lambda e: e.memset(sb["onescolf"][:], 1.0), sig="vs")
    td = mm(pb[6][:, 4:20], sb["onesrowf"][:], sb["delta"][:], True, True,
            waits=(tok_con, t2), bank=6)
    tdr = cp(VE, sb["deltarep"][:], pb[6][:, 4:20], waits=(td,))
    rdbank(6, tdr)
    lsc = qg2[:].bitcast(F32)
    ta1 = act(lsc[:, 0:8], sb["lambp"][:], AF.Abs, waits=(tok_con,))
    ta2 = act(lsc[:, 8:16], lsc[:, 0:8], AF.Exp, scale=-1.0, waits=(ta1,))
    ta3 = act(lsc[:, 16:24], lsc[:, 8:16], AF.Ln, bias=1.0, waits=(ta2,))
    ta4 = sch.do(VE, lambda e: e.tensor_relu(out=lsc[:, 24:32], in_=sb["lambp"][:]),
                 waits=(tok_con,), sig="vs")
    ta5 = tt(lsc[:, 32:40], lsc[:, 16:24], lsc[:, 24:32], ALU.add, waits=(ta3, ta4))
    tok_lamb = ts_(sb["lambcol"][:], lsc[:, 32:40], 0.25, None, ALU.add, waits=(ta5,))

    bv_last = None
    for o in range(0, LB, 512):
        n = min(512, LB - o)
        tmm = mm(pb[5][0:16, 0:n], sb["rel8s"][:], ohc[:, o:o + n], True, True,
                 waits=(tok_con, tok_oh), bank=5)
        tex = act(qg[:].bitcast(F32)[0:16, 0:n], pb[5][0:16, 0:n], AF.Exp, waits=(tmm,))
        rdbank(5, tex)
        bv_last = tt(bvals[0:16, o:o + n], qg[:].bitcast(F32)[0:16, 0:n],
                     m0s[0:16, o:o + n], ALU.mult, waits=(tex, tok_m0))
    if "t_bv" in taps:
        dma(SY, taps["t_bv"], bvals, waits=(bv_last,))
    toe_last_dma = None
    toe_mm_last = None
    for h in range(16):
        reps = []
        for o in range(0, LB, 512):
            n = min(512, LB - o)
            tmm = mm(pb[5][:, 0:n], hreps[:, 128 * h:128 * h + 128], bvals[0:16, o:o + n],
                     True, True, waits=(bv_last, tok_hr), bank=5)
            toe_mm_last = tmm
            tc = cp(VE, tmpd[:, o:o + n], pb[5][:, 0:n], waits=(tmm,))
            rdbank(5, tc)
            reps.append(tc)
        dst = dataclasses.replace(toeh[h].ap(), ap=[[WT1, 128], [1, LB]], offset=0)
        toe_last_dma = dma(GP, dst, tmpd[:, 0:LB], waits=tuple(reps), sig="dg")
        if h == 0 and "t_tmpd" in taps:
            dma(SY, taps["t_tmpd"], tmpd[:, 0:LB], waits=tuple(reps))
        if h == 0 and "t_toe0" in taps:
            srcq = dataclasses.replace(toeh[0].ap(), ap=[[1, 1], [1, 4096]], offset=0)
            dma(SY, taps["t_toe0"], srcq, waits=(toe_last_dma,))
        sch.pending[VE] = toe_last_dma

    if upto == "consts":
        return _finish_early()
    prev = {"iter": None, "tp": None}
    for it in range(32):
        buf = tmpa if it % 2 == 0 else tmpb
        w = [tok_con]
        if prev["iter"] is not None:
            w.append(prev["iter"])
        tl = dma(SY, buf[:], inn["x_full"][128 * it:128 * (it + 1), :], waits=tuple(w))
        tm = sch.do(VE, lambda e, b_=buf: e.reduce_sum(out=sb["statc"][:, 0:1], in_=b_[:],
                                                       axis=mybir.AxisListType.X),
                    waits=(tl,), sig="vs")
        tsq = act(lnt, buf[:], AF.Square, accum=sb["statc"][:, 1:2], waits=(tl,))
        tm2 = ts_(sb["statc"][:, 2:3], sb["statc"][:, 0:1], 1.0 / 1024, None, ALU.mult,
                  waits=(tm,))
        tv1 = ts_(sb["statc"][:, 3:4], sb["statc"][:, 2:3], sb["statc"][:, 2:3], None, ALU.mult,
                  waits=(tm2,))
        tv2 = ts_(sb["statc"][:, 4:5], sb["statc"][:, 1:2], 1.0 / 1024, None, ALU.mult,
                  waits=(tsq,))
        tv3 = tt(sb["statc"][:, 5:6], sb["statc"][:, 4:5], sb["statc"][:, 3:4], ALU.subtract,
                 waits=(tv1, tv2))
        tv4 = ts_(sb["statc"][:, 5:6], sb["statc"][:, 5:6], 1e-5, None, ALU.add, waits=(tv3,))
        tv5 = act(sb["statc"][:, 6:7], sb["statc"][:, 5:6], AF.Sqrt, waits=(tv4,))
        tv6 = sch.do(VE, lambda e: e.reciprocal(out=sb["statc"][:, 7:8],
                                                in_=sb["statc"][:, 6:7]),
                     waits=(tv5,), sig="vs")
        tn_ = ts_(lnt, buf[:], sb["statc"][:, 2:3], sb["statc"][:, 7:8],
                  ALU.subtract, ALU.mult, waits=(tm2, tv6))
        last_ev = None
        for dt_ in range(8):
            psl = pbf[:, 128 * (dt_ % 8):128 * (dt_ % 8) + 128]
            w2 = [tn_, tok_con]
            if sch.bank_rd[7] is not None:
                w2.append(sch.bank_rd[7])
            ttp = sch.do(TE, lambda e, ps=psl, d=dt_: e.transpose(
                ps, lnt[:, 128 * d:128 * (d + 1)], sb["identb"][:]), tuple(w2), "ts", 1)
            tev = cp(VE, ar1[dt_][:, 128 * it:128 * (it + 1)], psl, waits=(ttp,))
            rdbank(7, tev)
            last_ev = tev
        prev["iter"] = last_ev
    tok_ln1 = sch.barrier()

    if upto == "ln1":
        return _finish_early()
    def load_wcast(dst, name, din_, m, mlo=0, rlo=0, waits=()):
        k = din_ // 128
        src_ = inn[name][rlo:rlo + din_, :].rearrange("(k p) m -> p k m", p=128)[:, :, mlo:mlo + m]
        return dma(GP, dst[:, 0:k * m].rearrange("p (k m) -> p k m", k=k), src_,
                   waits=waits, sig="dg")

    DELTAS = list(range(-384, 385, 128))
    ffn_conv_chunks = []
    for nm in ["w1t", "w2t", "h1t", "h2t", "w3t"]:
        nel = int(np.prod(inn[nm].shape))
        flat_in = inn[nm].rearrange("a b c d -> (a b c d)")
        CH = 4 * 1024 * 1024
        for o in range(0, nel, CH):
            n = min(CH, nel - o)
            ffn_conv_chunks.append((wbf.ap()[woff[nm] + o: woff[nm] + o + n],
                                    flat_in[o:o + n]))
    conv_i = 0
    conv_toks = []
    prev_wo_dma = None
    ctxp_guard = None
    kT = ar2

    for hg in range(2):
        tok_wk = load_wcast(wstr, "wkm", 1024, 512, mlo=512 * hg, waits=(tok_ln1,))
        sch.serial = serial or ("E" in serial_secs)
        kcarry = {"rope": None, "rot": None}
        for pair in range(4):
            for nch in range(8):
                dst = pb[5][:, 0:512]
                tk = None
                for kt_ in range(8):
                    tk = mm(dst, wstr[:, 512 * kt_ + 128 * pair:512 * kt_ + 128 * pair + 128],
                            ar1[kt_][:, 512 * nch:512 * (nch + 1)], kt_ == 0, kt_ == 7,
                            waits=(tok_wk,) if kt_ == 0 else (), bank=5)
                tcs = dma(GP, costile[:], inn["cosb"][:, 512 * nch:512 * (nch + 1)],
                          waits=(kcarry["rope"],), sig="dg")
                tsn = dma(GP, sintile[:], inn["sinb"][:, 512 * nch:512 * (nch + 1)],
                          waits=(kcarry["rope"],), sig="dg")
                w_e1 = [tk, kcarry["rot"], kcarry["rope"]]
                te1 = cp(VE, tmpd[:, 0:512], dst, waits=tuple(w_e1))
                rdbank(5, te1)
                trm = mm(pb[4][:, 0:512], sb["rotT"][:], tmpd[:, 0:512], True, True,
                         waits=(te1,), bank=4)
                tt1 = tt(tmpa[:, 0:512], tmpd[:, 0:512], costile[:], ALU.mult,
                         waits=(te1, tcs))
                te2 = act(tmpd[:, 512:1024], pb[4][:, 0:512], AF.Copy, waits=(trm,))
                rdbank(4, te2)
                kcarry["rot"] = te2
                tt2 = tt(tmpb[:, 0:512], tmpd[:, 512:1024], sintile[:], ALU.mult,
                         waits=(te2, tsn))
                kcarry["rope"] = tt(kT[pair][:, 512 * nch:512 * (nch + 1)], tmpa[:, 0:512],
                                    tmpb[:, 0:512], ALU.add, waits=(tt1, tt2))
        sch.serial = serial
        tok_kT = sch.barrier()
        if "F" in serial_secs:
            sch.serial = True

        tok_wv = load_wcast(wstr, "wvm", 1024, 512, mlo=512 * hg, waits=(tok_kT,))
        vt = []
        for i in range(32):
            a = ar3[i // 8]
            o = (i % 8) * 544
            vt.append(a[:, o:o + 520])
        first_v = (toe_mm_last, tok_kT) if hg == 0 else (tok_kT,)
        for ti in range(32):
            dst = pb[5][:, 0:512]
            tk = None
            for kt_ in range(8):
                w = (tok_wv,) + first_v if kt_ == 0 else ()
                tk = mm(dst, ar1[kt_][:, 128 * ti:128 * (ti + 1)],
                        wstr[:, 512 * kt_:512 * (kt_ + 1)], kt_ == 0, kt_ == 7,
                        waits=w, bank=5)
            dstv = dataclasses.replace(vt[ti], ap=[vt[ti].ap[0], [65, 8], [1, 64]])
            tv = sch.do(VE, lambda e, d=dstv, s=dst: e.tensor_copy(
                out=d, in_=s.rearrange("p (h x) -> p h x", h=8)), waits=(tk,), sig="vs")
            rdbank(5, tv)
            onesv = dataclasses.replace(vt[ti], ap=[vt[ti].ap[0], [65, 8]],
                                        offset=vt[ti].offset + 64)
            sch.do(VE, lambda e, d=onesv: e.memset(d, 1.0), waits=(tv,), sig="vs")
        sch.serial = serial
        tok_v = sch.barrier()

        tok_wq = load_wcast(wstr, "wqm", 1024, 512, mlo=512 * hg, waits=(tok_v,))
        tok_wo = load_wcast(wstr2, "wom", 512, 1024, rlo=512 * hg, waits=(tok_v,))
        for pair in range(4):
            bc = [ar4[0], ar4[1]]
            btoks = []
            for di, dl in enumerate(DELTAS):
                c0 = dl + 512
                for hh in range(2):
                    h = 8 * hg + 2 * pair + hh
                    srcb = dataclasses.replace(toeh[h].ap(), ap=[[LB, 128], [1, 512]],
                                               offset=c0)
                    btoks.append(dma(GP, bc[hh][:, 512 * di:512 * (di + 1)], srcb,
                                     waits=(toe_last_dma, tok_v), sig="dg"))
            tok_b = btoks[-1]
            if hg == 0 and pair == 0 and "t_bc" in taps:
                dma(SY, taps["t_bc"], bc[0][:, 0:3584], waits=(tok_b,))
            ctx_last = None
            last_S = None
            prev_tqr = None
            for g in range(8):
                sch.serial = serial or ("C" in serial_secs)
                qdst = pb[5][:, 0:512]
                tk = None
                for kt_ in range(8):
                    tk = mm(qdst, wstr[:, 512 * kt_ + 128 * pair:512 * kt_ + 128 * pair + 128],
                            ar1[kt_][:, 512 * g:512 * (g + 1)], kt_ == 0, kt_ == 7,
                            waits=(tok_wq,) if kt_ == 0 else (), bank=5)
                tcs = dma(GP, costile[:], inn["cosb"][:, 512 * g:512 * (g + 1)],
                          waits=(prev_tqr,), sig="dg")
                tsn = dma(GP, sintile[:], inn["sinb"][:, 512 * g:512 * (g + 1)],
                          waits=(prev_tqr,), sig="dg")
                te1 = act(qg[:, 0:512], qdst, AF.Copy, scale=0.125,
                          waits=(tk, last_S, prev_tqr))
                rdbank(5, te1)
                trm = mm(pb[4][:, 0:512], sb["rotT"][:], qg[:, 0:512], True, True,
                         waits=(te1,), bank=4)
                te2 = act(qg[:, 512:1024], pb[4][:, 0:512], AF.Copy, waits=(trm, prev_tqr))
                rdbank(4, te2)
                tt1 = tt(tmpa[:, 0:512], qg[:, 0:512], costile[:], ALU.mult,
                         waits=(te1, tcs))
                tt2 = tt(tmpb[:, 0:512], qg[:, 512:1024], sintile[:], ALU.mult,
                         waits=(te2, tsn))
                tqr = tt(qg2[:, 0:512], tmpa[:, 0:512], tmpb[:, 0:512], ALU.add,
                         waits=(tt1, tt2, last_S))
                prev_tqr = tqr
                sch.serial = serial or ("A" in serial_secs)
                nkt = 4 * (g + 1)
                pa = [tmpd[:, 0:512], tmpd[:, 512:1024]]
                olast = [None, None]
                last_tP = [None, None]
                for kt_ in range(nkt):
                    dl = 512 * g - 128 * kt_
                    near = -384 <= dl <= 384
                    for hh in range(2):
                        srow = slice(64 * hh, 64 * hh + 64)
                        sps = pb[hh][:, 0:512]
                        tS = mm(sps, kT[pair][srow, 128 * kt_:128 * (kt_ + 1)],
                                qg2[srow, 0:512], True, True, waits=(tqr,), sig="ts",
                                tp=(64 * hh, 0), bank=hh)
                        last_S = tS
                        tE = act(pa[hh], sps, AF.Exp, waits=(tS, last_tP[hh], olast[hh]))
                        rdbank(hh, tE)
                        tP = tE
                        avsrc = pa[hh]
                        if near:
                            di = DELTAS.index(dl)
                            pax = pax0 if hh == 0 else pax1
                            tP = tt(pax[:, 0:512], pa[hh], bc[hh][:, 512 * di:512 * (di + 1)],
                                    ALU.mult, waits=(tE, tok_b, last_tP[hh]))
                            last_tP[hh] = tP
                            avsrc = pax[:, 0:512]
                        hl = 2 * pair + hh
                        tAV = mm(pb[2 + hh][0:65, 0:512], vt[kt_][:, 65 * hl:65 * hl + 65],
                                 avsrc, kt_ == 0, kt_ == nkt - 1, waits=(tP,),
                                 bank=(2 + hh) if kt_ == 0 else None)
                        olast[hh] = tAV
                sch.serial = serial or ("B" in serial_secs)
                for hh in range(2):
                    ob = pb[2 + hh]
                    trc = sch.do(VE, lambda e, ob_=ob: e.reciprocal(
                        out=sb["rowf"][0:1, 0:512], in_=ob_[64:65, 0:512]),
                        waits=(olast[hh], ctx_last), sig="vs")
                    tlr = mm(pb[4][0:64, 0:512], sb["onesrowf"][0:1, 0:64],
                             sb["rowf"][0:1, 0:512], True, True, waits=(trc,), bank=4)
                    tos = act(tmpc[0:64, 512 * hh:512 * hh + 512], ob[0:64, 0:512], AF.Copy,
                              waits=(olast[hh], trc, prev_wo_dma))
                    tcx = tt(ctxp[64 * hh:64 * hh + 64, 512 * g:512 * (g + 1)],
                             tmpc[0:64, 512 * hh:512 * hh + 512], pb[4][0:64, 0:512],
                             ALU.mult, waits=(tlr, tos, ctxp_guard))
                    rdbank(2 + hh, tcx)
                    rdbank(4, tcx)
                    ctx_last = tcx
                sch.serial = serial
            if "t_ctx8" in taps:
                ctxp_guard = dma(SY,
                                 taps["t_ctx8"][128 * (4 * hg + pair):128 * (4 * hg + pair) + 128, :],
                                 ctxp[:], waits=(ctx_last,))
            sch.serial = serial or ("D" in serial_secs)
            evac_carry = [None, None]
            wo_dma_last = None
            for mt in range(8):
                for nch in range(8):
                    dst = pb[6][:, 0:512]
                    lhs = wstr2[:, 1024 * pair + 128 * mt:1024 * pair + 128 * mt + 128]
                    tmm = mm(dst, lhs, ctxp[:, 512 * nch:512 * (nch + 1)], True, True,
                             waits=(tok_wo, ctx_last, prev_wo_dma), bank=6)
                    sl = (mt * 8 + nch) % 2
                    tev = cp(VE, tmpc[:, 512 * sl:512 * sl + 512], dst,
                             waits=(tmm, evac_carry[sl]))
                    rdbank(6, tev)
                    dstp = attnout.ap()[128 * mt:128 * (mt + 1), 512 * nch:512 * (nch + 1)]
                    kw = {} if (hg == 0 and pair == 0) else {"accum_op": ALU.add}
                    tdm = dma(GP, dstp, tmpc[:, 512 * sl:512 * sl + 512], waits=(tev,),
                              sig="dg", **kw)
                    evac_carry[sl] = tdm
                    wo_dma_last = tdm
            prev_wo_dma = wo_dma_last
            sch.serial = serial
            n_emit = (len(ffn_conv_chunks) + 7) // 8
            for _ in range(n_emit):
                if conv_i < len(ffn_conv_chunks):
                    dsty, srcy = ffn_conv_chunks[conv_i]
                    conv_toks.append(dma(GP, dsty, srcy, sig="dg"))
                    conv_i += 1
            sch.barrier()
    while conv_i < len(ffn_conv_chunks):
        dsty, srcy = ffn_conv_chunks[conv_i]
        conv_toks.append(dma(GP, dsty, srcy, sig="dg"))
        conv_i += 1
    tok_conv = conv_toks[-1]
    tok_attn = sch.barrier()
    if upto == "attn":
        return _finish_early()

    tgm = mm(pb[6][:, 0:2], sb["onesrowf"][:], sb["gam"][:], True, True,
             waits=(tok_attn,), bank=6)
    tgm2 = mm(pb[6][:, 2:6], sb["onesrowf"][:], sb["gsel"][:], True, True,
              waits=(tok_attn,), bank=None)
    tok_gam = cp(VE, sb["gamrep"][:, 0:6], pb[6][:, 0:6], waits=(tgm, tgm2))
    rdbank(6, tok_gam)

    carry = {"a": None, "b": None, "l": None, "c": None}
    for it in range(16):
        tl = dma(SY, tmpa[:], inn["x_own"][128 * it:128 * (it + 1), :],
                 waits=(tok_attn, carry["l"]))
        for dt_ in range(8):
            bk = dt_ % 2
            psl = pb[bk][:, 0:128]
            ttp = transpose(psl, tmpa[:, 128 * dt_:128 * (dt_ + 1)], sb["ident32"][:],
                            waits=(tl, tok_con), bank=bk)
            tr0 = dma(GP, tmpb[:, 0:128],
                      attnout.ap()[128 * dt_:128 * (dt_ + 1), 128 * it:128 * it + 128],
                      waits=(tok_attn, prev_wo_dma, carry["a"]), sig="dg")
            tr1 = dma(GP, tmpb[:, 128:256],
                      attnout.ap()[128 * dt_:128 * (dt_ + 1),
                                   2048 + 128 * it:2048 + 128 * it + 128],
                      waits=(tok_attn, prev_wo_dma, carry["a"]), sig="dg")
            tg1 = ts_(tmpb[:, 256:384], tmpb[:, 0:128], sb["gamrep"][:, 0:1], None,
                      ALU.mult, waits=(tr0, tok_gam, carry["c"]))
            tg2 = stt(tmpb[:, 384:512], tmpb[:, 128:256], sb["gamrep"][:, 1:2],
                      tmpb[:, 256:384], ALU.mult, ALU.add, waits=(tr1, tg1))
            tad = tt(tmpc[:, 0:128], psl, tmpb[:, 384:512], ALU.add,
                     waits=(ttp, tg2, carry["b"]))
            rdbank(bk, tad)
            carry["a"] = tg2
            carry["c"] = tad
            tdm = dma(GP, xresA.ap()[128 * dt_:128 * (dt_ + 1), 128 * it:128 * it + 128],
                      tmpc[:, 0:128], waits=(tad,), sig="dg")
            carry["b"] = tdm
            if dt_ == 7:
                carry["l"] = tad
    tok_x2 = sch.barrier()

    if "t_ln1" in taps:
        for dt_ in range(8):
            dma(SY, taps["t_ln1"][128 * dt_:128 * (dt_ + 1), :], ar1[dt_][:],
                waits=(tok_ln1,))
    if "t_ao" in taps:
        dma(SY, taps["t_ao"], attnout.ap(), waits=(tok_attn, prev_wo_dma))
    if "t_toe" in taps:
        dma(SY, taps["t_toe"], toeh[0].ap()[0:8192].rearrange("(a b) -> a b", a=2),
            waits=(tok_attn,))
    if "t_btile" in taps:
        for di, dl in enumerate([0, -128, 384]):
            srcb = dataclasses.replace(toeh[0].ap(), ap=[[LB, 128], [1, 512]],
                                       offset=dl + 512)
            dma(SY, taps["t_btile"][128 * di:128 * (di + 1), :], srcb, waits=(tok_attn,))
    if "t_x2" in taps:
        for dt_ in range(8):
            dma(SY, taps["t_x2"][128 * dt_:128 * (dt_ + 1), :],
                xresA.ap()[128 * dt_:128 * (dt_ + 1), :], waits=(tok_x2,))

    def phase_fence():
        ws = []
        for pre in ("ds", "dg"):
            nm = pre + sch.phase
            if sch.counts.get(nm):
                ws.append((nm, sch.counts[nm]))
        if sch.last_tok is not None:
            ws.append(sch.last_tok)
        sch.do(SY, lambda e: e.nop(), waits=tuple(ws))
        return sch.barrier()

    phase_fence()
    sch.serial = serial

    def ffn_pass(xin, xout):
        TB = 512
        xb = lambda dt: ar1[dt // 4][:].bitcast(F32)[:, 512 * (dt % 4):512 * (dt % 4) + TB]
        lnb = lambda dt: ar1[2][:, TB * dt:TB * dt + TB]
        hbuf = lambda k: ar2[k // 8][:, TB * (k % 8):TB * (k % 8) + TB]
        gbuf = lambda k: ar3[k // 8][:, TB * (k % 8):TB * (k % 8) + TB]
        slab = [wstr, wstr2]
        for tb in range(4):
            if tb:
                sch.barrier()
            slab_rd = [None, None]
            col0 = TB * tb
            dx = None
            for dt in range(8):
                dx = dma(GP if dt < 4 else SY, xb(dt),
                         xin.ap()[128 * dt:128 * (dt + 1), col0:col0 + TB],
                         sig="dg" if dt < 4 else "ds")
            m4last = None
            for dt in range(8):
                m4last = mm(pb[4][0:1, 0:TB], sb["onescolf"][:], xb(dt), dt == 0, dt == 7,
                            waits=(dx,) if dt == 0 else (), bank=4 if dt == 0 else None)
            m5last = None
            for dt in range(8):
                tq = act(tmpa[:, 0:TB], xb(dt), AF.Square,
                         waits=((dx,) if dt == 0 else ()) +
                               ((m5last,) if m5last is not None else ()))
                m5last = mm(pb[5][0:1, 0:TB], sb["onescolf"][:], tmpa[:, 0:TB],
                            dt == 0, dt == 7, waits=(tq,), bank=5 if dt == 0 else None)
            tm = ts_(sb["rowf"][0:1, 0:TB], pb[4][0:1, 0:TB], 1.0 / 1024, None, ALU.mult,
                     waits=(m4last,))
            rdbank(4, tm)
            te = ts_(tmpa[0:1, 0:TB], pb[5][0:1, 0:TB], 1.0 / 1024, None, ALU.mult,
                     waits=(m5last,))
            rdbank(5, te)
            tv = tt(tmpa[0:1, TB:2 * TB], sb["rowf"][0:1, 0:TB], sb["rowf"][0:1, 0:TB],
                    ALU.mult)
            tv2 = tt(tmpa[0:1, TB:2 * TB], tmpa[0:1, 0:TB], tmpa[0:1, TB:2 * TB],
                     ALU.subtract)
            tv3 = ts_(tmpa[0:1, TB:2 * TB], tmpa[0:1, TB:2 * TB], 1e-5, None, ALU.add)
            tv4 = act(tmpc[0:1, 0:TB], tmpa[0:1, TB:2 * TB], AF.Sqrt, waits=(tv3,))
            tv5 = sch.do(VE, lambda e: e.reciprocal(out=tmpa[0:1, TB:2 * TB],
                                                    in_=tmpc[0:1, 0:TB]),
                         waits=(tv4,), sig="vs")
            tmm4 = mm(pb[4][:, 0:TB], sb["onesrowf"][:], sb["rowf"][0:1, 0:TB], True, True,
                      waits=(tm,), bank=4)
            tmm5 = mm(pb[5][:, 0:TB], sb["onesrowf"][:], tmpa[0:1, TB:2 * TB], True, True,
                      waits=(tv5,), bank=5)
            t2last = None
            for dt in range(8):
                t1 = tt(tmpb[:, 0:TB], xb(dt), pb[4][:, 0:TB], ALU.subtract,
                        waits=(tmm4, tmm5) if dt == 0 else ())
                t2last = tt(lnb(dt), tmpb[:, 0:TB], pb[5][:, 0:TB], ALU.mult)
                if dt == 7:
                    rdbank(4, t2last)
                    rdbank(5, t2last)
            tt_prev = None
            for mt in range(32):
                si = mt % 2
                sl = slab[si]
                war = (slab_rd[si],) if slab_rd[si] is not None else ()
                d1 = dma(GP, sl[:, 0:1024].rearrange("p (k m) -> p k m", k=8),
                         inn["w1t"][mt].rearrange("k a b -> a k b"), waits=war, sig="dg")
                d2 = dma(GP, sl[:, 1024:2048].rearrange("p (k m) -> p k m", k=8),
                         inn["w2t"][mt].rearrange("k a b -> a k b"), waits=war, sig="dg")
                m1 = None
                for k in range(8):
                    m1 = mm(pb[0][:, 0:TB], sl[:, 128 * k:128 * k + 128], lnb(k),
                            k == 0, k == 7,
                            waits=((d1, d2, t2last) if mt == 0 else (d1, d2))
                            if k == 0 else (), bank=0)
                m2 = None
                for k in range(8):
                    m2 = mm(pb[1][:, 0:TB], sl[:, 1024 + 128 * k:1024 + 128 * k + 128],
                            lnb(k), k == 0, k == 7, bank=1)
                slab_rd[si] = m2
                a1 = act(tmpd[:, 0:TB], pb[0][:, 0:TB], AF.Silu,
                         scale=sb["s1c"][:, mt:mt + 1],
                         waits=(m1,) + ((tt_prev,) if tt_prev is not None else ()))
                rdbank(0, a1)
                a2 = act(tmpd[:, TB:2 * TB], pb[1][:, 0:TB], AF.Copy,
                         scale=sb["s2c"][:, mt:mt + 1], waits=(m2,))
                rdbank(1, a2)
                tt_prev = tt(hbuf(mt), tmpd[:, 0:TB], tmpd[:, TB:2 * TB], ALU.mult,
                             waits=(a1, a2))
            src_ready = tt_prev
            for layer, wnm, scnm, srcf, dstf in (("h1", "h1t", "sh1c", hbuf, gbuf),
                                                 ("h2", "h2t", "sh2c", gbuf, hbuf)):
                last_act = None
                for mt in range(32):
                    si = mt % 2
                    sl = slab[si]
                    war = (slab_rd[si],) if slab_rd[si] is not None else ()
                    d1 = dma(GP, sl[:, 0:4096].rearrange("p (k m) -> p k m", k=32),
                             inn[wnm][mt].rearrange("k a b -> a k b"), waits=war,
                             sig="dg")
                    m1 = None
                    for k in range(32):
                        m1 = mm(pb[0][:, 0:TB], sl[:, 128 * k:128 * k + 128], srcf(k),
                                k == 0, k == 31,
                                waits=((d1, src_ready) if mt == 0 else (d1,))
                                if k == 0 else (), bank=0)
                    slab_rd[si] = m1
                    last_act = act(dstf(mt), pb[0][:, 0:TB], AF.Gelu,
                                   scale=sb[scnm][:, mt:mt + 1], waits=(m1,))
                    rdbank(0, last_act)
                src_ready = last_act
            prev_odma = None
            for mo in range(8):
                si = mo % 2
                sl = slab[si]
                war = (slab_rd[si],) if slab_rd[si] is not None else ()
                d1 = dma(GP, sl[:, 0:4096].rearrange("p (k m) -> p k m", k=32),
                         inn["w3t"][mo].rearrange("k a b -> a k b"), waits=war, sig="dg")
                m1 = None
                for k in range(32):
                    m1 = mm(pb[0][:, 0:TB], sl[:, 128 * k:128 * k + 128], hbuf(k),
                            k == 0, k == 31,
                            waits=((d1, src_ready) if mo == 0 else (d1,))
                            if k == 0 else (), bank=0)
                slab_rd[si] = m1
                s1 = stt(tmpc[:, 0:TB], pb[0][:, 0:TB], sb["s3c"][:, mo:mo + 1], xb(mo),
                         ALU.mult, ALU.add,
                         waits=(m1,) + ((prev_odma,) if prev_odma is not None else ()))
                rdbank(0, s1)
                prev_odma = dma(SY, xout.ap()[128 * mo:128 * (mo + 1), col0:col0 + TB],
                                tmpc[:, 0:TB], waits=(s1,), sig="ds")
        return phase_fence()

    sch.phase = "f1"
    tok_ffn1 = ffn_pass(xresA, xresB)
    sch.serial = True
    if upto == "ffn1":
        tap_x3 = taps.get("t_x3")
        if tap_x3 is not None:
            for dt_ in range(8):
                dma(GP, tap_x3[128 * dt_:128 * (dt_ + 1), :],
                    xresB.ap()[128 * dt_:128 * (dt_ + 1), :], sig="dg")
        return _finish_early()

    lnb3 = lambda dt: ar1[dt // 4][:, 1024 * (dt % 4):1024 * (dt % 4) + 1024]
    kTb = lambda p: ar1[2 + p // 4][:, 1024 * (p % 4):1024 * (p % 4) + 1024]
    qTb = lambda p: ar1[4 + p // 4][:, 1024 * (p % 4):1024 * (p % 4) + 1024]
    oTb = lambda p: ar1[6 + p // 4][:, 1024 * (p % 4):1024 * (p % 4) + 1024]
    ktok = lambda t: ar2[t // 4][:, 1024 * (t % 4):1024 * (t % 4) + 1024]
    vtok = lambda t: ar2[2 + (t // 4)][:, 1024 * (t % 4):1024 * (t % 4) + 1024]
    gateb = lambda p: ar3[p // 4][:, 1024 * (p % 4):1024 * (p % 4) + 1024]
    fmisc = ar3[2][:].bitcast(F32)
    ftok = lambda t: fmisc[:, 32 * t:32 * t + 16]
    lbtok = lambda t: fmisc[:, 32 * t + 16:32 * t + 32]
    lnbT = fmisc[0:16, 256:1280]
    Fch = fmisc[0:64, 1280:1296]
    FTch = fmisc[0:16, 1296:1360]
    GTch = fmisc[0:16, 1360:1424]
    wendT = ar3[2][0:16, 2848:2912]
    dTcol = fmisc[0:16, 1488:1489]
    EGch = ar3[2][0:16, 3008:3072]
    eFtok = fmisc[0:64, 1568:1584]
    eFT2 = fmisc[0:2, 1584:1648]
    sss = ar4[0][:].bitcast(F32)
    kwb = ar4[1][0:64, 0:1024]
    vwb = ar4[1][0:64, 1024:2048]
    ktil = ar4[1][0:64, 2048:3072]
    lambBIG = ar4[1][:, 3584:4096]
    cfBIG = ar3[3][:, 0:512]
    vtil = ar3[3][0:64, 512:1536]
    zqs = ar3[3][:, 1536:2048]
    kch = ar3[3][0:64, 2048:3072]
    vch = ar3[3][0:64, 3072:4096]
    fch0 = fmisc[0:64, 1600:1616]
    lhcg = ctxp[:, 0:1024]
    lhO = ctxp[:, 1024:2048]
    lhK = ctxp[:, 2048:3072]
    lhV = ctxp[:, 3072:4096]
    cgAp = tmpd[:, 1024:1536]
    cgX = pax0[:, 0:512]
    cgR = pax1[:, 0:512]
    cgP = qg2[:, 0:512]
    Gext = qg[:, 0:512]
    gscr = qg[:, 512:1024]
    zsq = tmpa[:].bitcast(BF16)[:, 0:512]
    tmp1 = tmpa[:].bitcast(BF16)[:, 512:1024]
    tmp2 = tmpa[:].bitcast(BF16)[:, 1024:1536]
    tmp3 = tmpa[:].bitcast(BF16)[:, 1536:2048]
    rwrs = tmpb[0:2, 0:512]
    rwpap = tmpb[0:2, 512:1024]
    rwa = tmpc[0:2, 0:512]
    rwb = tmpc[0:2, 512:1024]
    rwt = fmisc[0:2, 1616:2128]

    def tp_pe(dst, srcap, f32=False):
        P, Fd = srcap.shape[0], srcap.shape[1]
        if f32:
            tq = transpose(pb[6][0:Fd, 0:P], srcap, sb["ident32"][0:P, 0:P], bank=6)
            tc = cp(VE, dst, pb[6][0:Fd, 0:P], waits=(tq,))
            rdbank(6, tc)
        else:
            tq = transpose(pbf[0:Fd, 0:P], srcap, sb["identb"][0:P, 0:P], bank=7)
            tc = cp(VE, dst, pbf[0:Fd, 0:P], waits=(tq,))
            rdbank(7, tc)
        return tc

    sch.do(VE, lambda e: e.memset(sss[:, 0:1024], 0.0), sig="vs")
    tob = sch.do(VE, lambda e: e.memset(tmpd[:, 0:64], 1.0), sig="vs")
    for p in range(8):
        ts_(lambBIG[:, 64 * p:64 * p + 64], tmpd[:, 0:64], sb["lambcol"][:, p:p + 1],
            None, ALU.mult, waits=(tob,))
    dma(GP, lhK[64:128, :], inn["i64rep"], sig="dg")
    dma(GP, lhV[64:128, :], inn["i64rep"], sig="dg")
    sch.do(VE, lambda e: e.memset(tmpd[64:128, 0:512], 1.0), sig="vs")

    def ln3_half(half):
        for sub in range(2):
            col0 = 1024 * half + 512 * sub
            for dt in range(8):
                tl = dma(GP, tmpa[:, 0:512],
                         xresB.ap()[128 * dt:128 * (dt + 1), col0:col0 + 512], sig="dg")
                mm(pb[4][0:1, 0:512], sb["onescolf"][:], tmpa[:, 0:512], dt == 0, dt == 7,
                   waits=(tl,), bank=4 if dt == 0 else None)
                tq = act(tmpc[:, 0:512], tmpa[:, 0:512], AF.Square)
                mm(pb[5][0:1, 0:512], sb["onescolf"][:], tmpc[:, 0:512], dt == 0, dt == 7,
                   waits=(tq,), bank=5 if dt == 0 else None)
            tm = ts_(sb["rowf"][0:1, 0:512], pb[4][0:1, 0:512], 1.0 / 1024, None, ALU.mult)
            rdbank(4, tm)
            te = ts_(tmpc[0:1, 0:512], pb[5][0:1, 0:512], 1.0 / 1024, None, ALU.mult)
            rdbank(5, te)
            tt(tmpc[0:1, 512:1024], sb["rowf"][0:1, 0:512], sb["rowf"][0:1, 0:512],
               ALU.mult)
            tt(tmpc[0:1, 512:1024], tmpc[0:1, 0:512], tmpc[0:1, 512:1024], ALU.subtract)
            ts_(tmpc[0:1, 512:1024], tmpc[0:1, 512:1024], 1e-5, None, ALU.add)
            act(tmpc[0:1, 0:512], tmpc[0:1, 512:1024], AF.Sqrt)
            sch.do(VE, lambda e: e.reciprocal(out=tmpc[0:1, 512:1024],
                                              in_=tmpc[0:1, 0:512]), sig="vs")
            mm(pb[4][:, 0:512], sb["onesrowf"][:], sb["rowf"][0:1, 0:512], True, True,
               bank=4)
            mm(pb[5][:, 0:512], sb["onesrowf"][:], tmpc[0:1, 512:1024], True, True, bank=5)
            for dt in range(8):
                tl = dma(GP, tmpa[:, 0:512],
                         xresB.ap()[128 * dt:128 * (dt + 1), col0:col0 + 512], sig="dg")
                t1 = tt(tmpc[:, 0:512], tmpa[:, 0:512], pb[4][:, 0:512], ALU.subtract,
                        waits=(tl,))
                t2 = tt(lnb3(dt)[:, 512 * sub:512 * sub + 512], tmpc[:, 0:512],
                        pb[5][:, 0:512], ALU.mult)
                if dt == 7:
                    rdbank(4, t2)
                    rdbank(5, t2)

    def wload(wname):
        da = dma(GP, wstr[:, 0:4096].rearrange("p (k m) -> p k m", k=8),
                 inn[wname].rearrange("(k p) m -> p k m", p=128)[:, :, 0:512], sig="dg")
        db = dma(GP, wstr2[:, 0:4096].rearrange("p (k m) -> p k m", k=8),
                 inn[wname].rearrange("(k p) m -> p k m", p=128)[:, :, 512:1024], sig="dg")
        return da, db

    def proj_feat(wname, dst_fn, l2n):
        da, db = wload(wname)
        for p in range(8):
            wv = wstr if p < 4 else wstr2
            mc = 128 * (p % 4)
            for nch in range(2):
                for k in range(8):
                    mm(pb[0][:, 0:512], wv[:, 512 * k + mc:512 * k + mc + 128],
                       lnb3(k)[:, 512 * nch:512 * nch + 512], k == 0, k == 7,
                       waits=(da, db) if k == 0 else (), bank=0)
                if l2n:
                    tq = act(tmp1, pb[0][:, 0:512], AF.Square)
                    tn = mm(pb[1][0:2, 0:512], sb["hsum2"][:], tmp1, True, True,
                            waits=(tq,), bank=1)
                    t1 = ts_(sb["rowf"][0:2, 0:512], pb[1][0:2, 0:512], 1e-6, None, ALU.add)
                    rdbank(1, t1)
                    t2 = act(tmpc[0:2, 0:512], sb["rowf"][0:2, 0:512], AF.Sqrt)
                    t3 = sch.do(VE, lambda e: e.reciprocal(
                        out=sb["rowf"][0:2, 0:512], in_=tmpc[0:2, 0:512]), sig="vs")
                    t4 = mm(pb[2][:, 0:512], sb["blocksel"][:], sb["rowf"][0:2, 0:512],
                            True, True, bank=2)
                    t5 = act(tmp1, pb[0][:, 0:512], AF.Copy)
                    rdbank(0, t5)
                    t6 = tt(dst_fn(p)[:, 512 * nch:512 * nch + 512], tmp1, pb[2][:, 0:512],
                            ALU.mult)
                    rdbank(2, t6)
                else:
                    t5 = act(dst_fn(p)[:, 512 * nch:512 * nch + 512], pb[0][:, 0:512],
                             AF.Copy)
                    rdbank(0, t5)

    def attnb_half_setup(half):
        ln3_half(half)
        proj_feat("wbk", kTb, True)
        proj_feat("wbq", qTb, True)
        da, db = wload("wbv")
        for t in range(8):
            for nh, wv in ((0, wstr), (1, wstr2)):
                for k in range(8):
                    mm(pb[0][:, 0:512], lnb3(k)[:, 128 * t:128 * t + 128],
                       wv[:, 512 * k:512 * k + 512], k == 0, k == 7,
                       waits=(da, db) if k == 0 else (), bank=0)
                t5 = act(vtok(t)[:, 512 * nh:512 * nh + 512], pb[0][:, 0:512], AF.Copy)
                rdbank(0, t5)
        for p in range(8):
            for j in range(8):
                tp_pe(ktok(j)[:, 128 * p:128 * p + 128], kTb(p)[:, 128 * j:128 * j + 128])
        dwf = dma(GP, wstr[:, 0:256].rearrange("p (k m) -> p k m", k=8),
                  inn["wfb"].rearrange("(k p) m -> p k m", p=128), sig="dg")
        for t in range(8):
            for k in range(8):
                mm(pb[0][:, 0:32], lnb3(k)[:, 128 * t:128 * t + 128],
                   wstr[:, 32 * k:32 * k + 32], k == 0, k == 7,
                   waits=(dwf,) if k == 0 else (), bank=0)
            tz1 = tt(tmpc[:, 0:16], pb[0][:, 0:16], sb["deltarep"][:], ALU.add)
            tz2 = act(tmpc[:, 16:32], tmpc[:, 0:16], AF.Sigmoid)
            tz3 = act(ftok(t), tmpc[:, 16:32], AF.Ln)
            tz4 = act(tmpc[:, 32:48], pb[0][:, 16:32], AF.Sigmoid)
            rdbank(0, tz4)
            tz5 = act(lbtok(t), tmpc[:, 32:48], AF.Ln)
        for t in range(8):
            tp_pe(lnbT[0:16, 128 * t:128 * t + 128], lbtok(t), f32=True)

    def chunk_F(ci):
        t8, r0 = ci // 2, 64 * (ci % 2)
        ccol = slice(64 * ci, 64 * ci + 64)
        cp(VE, kch[:, 0:1024], ktok(t8)[r0:r0 + 64, 0:1024])
        cp(VE, vch[:, 0:1024], vtok(t8)[r0:r0 + 64, 0:1024])
        tfc = cp(VE, fch0[:, 0:16], ftok(t8)[r0:r0 + 64, 0:16])
        tF = mm(pb[3][0:64, 0:16], sb["lt64"][:], fch0[:, 0:16], True, True,
                waits=(tfc,), bank=3)
        tFe = cp(VE, Fch, pb[3][0:64, 0:16], waits=(tF,))
        rdbank(3, tFe)
        tFT = tp_pe(FTch, Fch, f32=True)
        tGT = tt(GTch, lnbT[0:16, ccol], FTch, ALU.subtract, waits=(tFT,))
        twe = act(wendT, GTch, AF.Exp, bias=FTch[0:16, 63:64], waits=(tGT,))
        tdc = act(dTcol, FTch[0:16, 63:64], AF.Exp, waits=(tFT,))
        teg = act(EGch, GTch, AF.Exp, waits=(tGT,))
        tef = act(eFtok, Fch, AF.Exp, waits=(tFe,))
        tw1 = mm(pb[3][0:64, 0:512], wendT, sb["e16"][0:16, 0:512], True, True,
                 waits=(twe,), bank=3)
        tw2 = mm(pb[2][0:64, 0:512], wendT, sb["e16"][0:16, 512:1024], True, True,
                 waits=(twe,), bank=2)
        tk1 = tt(kwb[:, 0:512], kch[:, 0:512], pb[3][0:64, 0:512], ALU.mult,
                 waits=(tw1,))
        tk2 = tt(kwb[:, 512:1024], kch[:, 512:1024], pb[2][0:64, 0:512],
                 ALU.mult, waits=(tw2,))
        tv1 = tt(vwb[:, 0:512], vch[:, 0:512], pb[3][0:64, 0:512], ALU.mult,
                 waits=(tw1,))
        tv2 = tt(vwb[:, 512:1024], vch[:, 512:1024], pb[2][0:64, 0:512],
                 ALU.mult, waits=(tw2,))
        rdbank(3, tv1)
        rdbank(2, tv2)
        return t8, r0, (tdc, tef, teg, tk2, tv2)

    def chunk_state(t8, r0, toks):
        tdc = toks[0]
        for p in range(8):
            mm(pb[4][:, p:p + 1], sb["pairsel"][:, 128 * p:128 * p + 128], dTcol,
               p == 0, p == 7, waits=(tdc,), bank=4 if p == 0 else None)
        tdr = cp(VE, sb["statc"][:, 0:8], pb[4][:, 0:8])
        rdbank(4, tdr)
        for p in range(8):
            pbank, bk = (pb[5], 5) if p < 4 else (pb[6], 6)
            pcol = 128 * (p % 4)
            for hh in range(2):
                h = 2 * p + hh
                col = slice(64 * h, 64 * h + 64)
                mm(pbank[64 * hh:64 * hh + 64, pcol:pcol + 64],
                   kch[:, col], kwb[:, col],
                   True, True, tp=(0, 64 * hh), bank=bk if (p % 4 == 0 and hh == 0) else None)
                mm(pbank[64 * hh:64 * hh + 64, pcol + 64:pcol + 128],
                   kch[:, col], vwb[:, col],
                   True, True, tp=(0, 64 * hh))
        tup = None
        for p in range(8):
            pbank = pb[5] if p < 4 else pb[6]
            pcol = 128 * (p % 4)
            tup = stt(sss[:, 128 * p:128 * p + 128], sss[:, 128 * p:128 * p + 128],
                      sb["statc"][:, p:p + 1], pbank[:, pcol:pcol + 128], ALU.mult, ALU.add,
                      waits=(tdr,))
        rdbank(5, tup)
        rdbank(6, tup)
        return tup

    def chunk_cg_o(ci, t8, r0, toks):
        ccol = slice(64 * ci, 64 * ci + 64)
        tef, teg = toks[1], toks[2]
        te1 = mm(pb[3][0:64, 0:512], EGch, sb["e16"][0:16, 0:512], True, True,
                 waits=(teg,), bank=3)
        te2 = mm(pb[2][0:64, 0:512], EGch, sb["e16"][0:16, 512:1024], True, True,
                 waits=(teg,), bank=2)
        tt(ktil[:, 0:512], kch[:, 0:512], pb[3][0:64, 0:512], ALU.mult)
        tk = tt(ktil[:, 512:1024], kch[:, 512:1024], pb[2][0:64, 0:512],
                ALU.mult)
        tt(vtil[:, 0:512], vch[:, 0:512], pb[3][0:64, 0:512], ALU.mult)
        tv = tt(vtil[:, 512:1024], vch[:, 512:1024], pb[2][0:64, 0:512],
                ALU.mult)
        rdbank(3, tv)
        rdbank(2, tv)
        for p in range(8):
            tpe = tp_pe(eFT2, eFtok[0:64, 2 * p:2 * p + 2], f32=True)
            tcf = mm(pb[4][:, 64 * p:64 * p + 64], sb["blocksel"][:], eFT2, True, True,
                     waits=(tpe, tef), bank=4 if p == 0 else None)
        tcfe = cp(VE, cfBIG, pb[4][:, 0:512])
        rdbank(4, tcfe)
        a1 = cp(VE, lhK[0:64, :], ktil[:, 0:1024], waits=(tk,))
        a2 = cp(VE, lhV[0:64, :], vtil[:, 0:1024], waits=(tv,))
        for p in range(8):
            cp(VE, lhcg[:, 128 * p:128 * p + 64], kTb(p)[:, ccol])
            cp(VE, lhcg[:, 128 * p + 64:128 * p + 128], sss[:, 128 * p:128 * p + 64])
            cp(VE, lhO[:, 128 * p:128 * p + 64], kTb(p)[:, ccol])
            a3 = cp(VE, lhO[:, 128 * p + 64:128 * p + 128],
                    sss[:, 128 * p + 64:128 * p + 128])
        sch.do(VE, lambda e: e.memset(cgX[:], 0.0), sig="vs")
        for p in range(8):
            cp(VE, cgR[:, 64 * p:64 * p + 64], qTb(p)[:, ccol])
        ti = cp(VE, cgP[:], cgR[:])
        tq = act(zsq, cgR[:], AF.Square, waits=(ti,))
        tn = mm(pb[3][0:2, 0:512], sb["hsum2"][:], zsq, True, True, waits=(tq,), bank=3)
        tr0 = cp(VE, rwrs, pb[3][0:2, 0:512], waits=(tn,))
        rdbank(3, tr0)

        def Amul(pvec, dst_psumA, dst_psumB, dst_ap, lhs1, lhsk):
            for p in range(8):
                mm(dst_psumA[:, 64 * p:64 * p + 64], lhs1[0:64, 128 * p:128 * p + 128],
                   pvec[0:64, 64 * p:64 * p + 64], True, True, tp=(0, 0),
                   bank=0 if p == 0 else None)
                mm(dst_psumB[:, 64 * p:64 * p + 64], lhs1[64:128, 128 * p:128 * p + 128],
                   pvec[64:128, 64 * p:64 * p + 64], True, True, tp=(64, 0),
                   bank=1 if p == 0 else None)
            g1 = tt(Gext, sb["upperext"][:], dst_psumA[:, 0:512], ALU.mult)
            rdbank(0, g1)
            g2 = tt(gscr, sb["upperext"][:], dst_psumB[:, 0:512], ALU.mult)
            rdbank(1, g2)
            for p in range(8):
                for hh in range(2):
                    h = 2 * p + hh
                    gsrc = Gext if hh == 0 else gscr
                    mm(pb[2][64 * hh:64 * hh + 64, 64 * p:64 * p + 64],
                       lhsk[:, 64 * h:64 * h + 64],
                       gsrc[:, 64 * p:64 * p + 64], True, True, tp=(0, 64 * hh),
                       bank=2 if (p == 0 and hh == 0) else None)
            m1 = tt(tmp1, cfBIG, pb[2][:, 0:512], ALU.mult)
            rdbank(2, m1)
            m2 = tt(tmp2, pvec[:], lambBIG, ALU.mult)
            return tt(dst_ap, tmp1, tmp2, ALU.add, waits=(m1, m2))

        for it in range(30):
            tap_ = Amul(cgP, pb[0], pb[1], cgAp, lhcg, lhK)
            z1 = tt(zsq, cgP[:], cgAp, ALU.mult, waits=(tap_,))
            tn1 = mm(pb[3][0:2, 0:512], sb["hsum2"][:], zsq, True, True, waits=(z1,),
                     bank=3)
            tp1 = ts_(rwpap, pb[3][0:2, 0:512], 1e-12, None, ALU.add)
            rdbank(3, tp1)
            ta0 = sch.do(VE, lambda e: e.reciprocal(out=rwt, in_=rwpap),
                         waits=(tp1,), sig="vs")
            ta1 = tt(rwa, rwrs, rwt, ALU.mult, waits=(ta0,))
            tb1 = mm(pb[4][:, 0:512], sb["blocksel"][:], rwa, True, True, waits=(ta1,),
                     bank=4)
            u1 = tt(tmp1, cgP[:], pb[4][:, 0:512], ALU.mult)
            u2 = tt(cgX[:], cgX[:], tmp1, ALU.add, waits=(u1,))
            u3 = tt(tmp2, cgAp, pb[4][:, 0:512], ALU.mult)
            rdbank(4, u3)
            u4 = tt(cgR[:], cgR[:], tmp2, ALU.subtract, waits=(u3,))
            q1 = act(zsq, cgR[:], AF.Square, waits=(u4,))
            tn2 = mm(pb[3][0:2, 0:512], sb["hsum2"][:], zsq, True, True, waits=(q1,),
                     bank=3)
            tp2 = ts_(rwt, rwrs, 1e-12, None, ALU.add)
            tz2 = sch.do(VE, lambda e: e.reciprocal(out=rwa, in_=rwt),
                         waits=(tp2, tb1), sig="vs")
            tb2 = tt(rwb, pb[3][0:2, 0:512], rwa, ALU.mult, waits=(tz2, tn2))
            tr2 = cp(VE, rwrs, pb[3][0:2, 0:512], waits=(tb2,))
            rdbank(3, tr2)
            tb3 = mm(pb[4][:, 0:512], sb["blocksel"][:], rwb, True, True, waits=(tb2,),
                     bank=4)
            u5 = tt(tmp3, cgP[:], pb[4][:, 0:512], ALU.mult)
            rdbank(4, u5)
            u6 = tt(cgP[:], cgR[:], tmp3, ALU.add, waits=(u5,))
        to = Amul(cgX, pb[0], pb[1], cgAp, lhO, lhV)
        o1 = tt(zsq, cfBIG, pb[2][:, 0:512], ALU.mult)
        for p in range(8):
            oc = cp(VE, oTb(p)[:, ccol], zsq[:, 64 * p:64 * p + 64], waits=(o1,))
        return oc

    def attnb_gate_out(half):
        dg1 = dma(GP, wstr[:, 0:512].rearrange("p (k m) -> p k m", k=8),
                  inn["gw1"].rearrange("(k p) m -> p k m", p=128), sig="dg")
        for nch in range(2):
            for k in range(8):
                mm(pb[0][0:64, 0:512], wstr[:, 64 * k:64 * k + 64],
                   lnb3(k)[:, 512 * nch:512 * nch + 512], k == 0, k == 7,
                   waits=(dg1,) if k == 0 else (), bank=0)
            tg = act(tmpd[0:64, 0:512], pb[0][0:64, 0:512], AF.Copy)
            rdbank(0, tg)
            dg2 = dma(GP, wstr2[0:64, 0:1024].rearrange("p (k m) -> p k m", k=1),
                      inn["gw2"].rearrange("(k p) m -> p k m", p=64), sig="dg")
            for p in range(8):
                mm(pb[1][:, 0:512], wstr2[0:64, 128 * p:128 * p + 128],
                   tmpd[0:64, 0:512], True, True, waits=(dg2, tg), bank=1)
                tsg = act(gateb(p)[:, 512 * nch:512 * nch + 512], pb[1][:, 0:512],
                          AF.Sigmoid)
                rdbank(1, tsg)
        for p in range(8):
            for nch in range(2):
                sl = slice(512 * nch, 512 * nch + 512)
                t1 = tt(tmp1, oTb(p)[:, sl], gateb(p)[:, sl], ALU.mult)
                tq = act(tmp2, tmp1, AF.Square, waits=(t1,))
                tn = mm(pb[1][0:2, 0:512], sb["hsum2"][:], tmp2, True, True, waits=(tq,),
                        bank=1)
                t2 = ts_(sb["rowf"][0:2, 0:512], pb[1][0:2, 0:512], 1.0 / 64, 1e-5,
                         ALU.mult, ALU.add)
                rdbank(1, t2)
                t3 = act(tmpc[0:2, 0:512], sb["rowf"][0:2, 0:512], AF.Sqrt)
                t4 = sch.do(VE, lambda e: e.reciprocal(out=sb["rowf"][0:2, 0:512],
                                                       in_=tmpc[0:2, 0:512]), sig="vs")
                t5 = mm(pb[2][:, 0:512], sb["blocksel"][:], sb["rowf"][0:2, 0:512],
                        True, True, waits=(t4,), bank=2)
                t6 = tt(tmp3, tmp1, pb[2][:, 0:512], ALU.mult)
                rdbank(2, t6)
                t7 = ts_(oTb(p)[:, sl], tmp3, sb["nrmw"][:, p:p + 1], None, ALU.mult,
                         waits=(t6,))
        da, db = wload("wbo")
        for mo in range(8):
            for nch in range(2):
                for k in range(8):
                    wv = wstr if mo < 4 else wstr2
                    mc = 128 * (mo % 4)
                    mm(pb[0][:, 0:512], wv[:, 512 * k + mc:512 * k + mc + 128],
                       oTb(k)[:, 512 * nch:512 * nch + 512], k == 0, k == 7,
                       waits=(da, db) if k == 0 else (), bank=0)
                col0 = 1024 * half + 512 * nch
                tl = dma(GP, tmpa[:, 0:512],
                         xresB.ap()[128 * mo:128 * (mo + 1), col0:col0 + 512], sig="dg")
                t1 = tt(tmpc[:, 0:512], pb[0][:, 0:512], tmpa[:, 0:512], ALU.add,
                        waits=(tl,))
                rdbank(0, t1)
                dma(GP, xresA.ap()[128 * mo:128 * (mo + 1), col0:col0 + 512],
                    tmpc[:, 0:512], waits=(t1,), sig="dg")

    for half in range(2):
        sch.phase = f"p3a{half}"
        attnb_half_setup(half)
        for ci in range(16):
            t8, r0, toks = chunk_F(ci)
            chunk_state(t8, r0, toks)
        if half == 1:
            cpv = cp(VE, tmpc[:, 0:1024], sss[:, 0:1024])
            dsend = dma(GP, pkg_send.ap(), tmpc[:, 0:1024], waits=(cpv,), sig="dg")
            def cc_ag(e):
                return e.collective_compute("AllGather", ALU.bypass,
                                            replica_groups=[list(range(8))],
                                            ins=[pkg_send.ap()], outs=[pkg_recv.ap()])
            tcc = sch.do(GP, cc_ag, waits=(dsend,), sig="cc", n=1)
            tz = sch.do(VE, lambda e: e.memset(sss[:, 0:1024], 0.0), waits=(tcc,), sig="vs")
            for j in range(4):
                tl = dma(GP, tmpc[:, 0:1024],
                         pkg_recv.ap()[128 * 2 * j:128 * 2 * j + 128, :],
                         waits=(tcc,), sig="dg")
                tz = stt(sss[:, 0:1024], tmpc[:, 0:1024], sb["gamrep"][:, 2 + j:3 + j],
                         sss[:, 0:1024], ALU.mult, ALU.add, waits=(tl, tz))
    for half in range(2):
        sch.phase = f"p3b{half}"
        attnb_half_setup(half)
        for ci in range(16):
            t8, r0, toks = chunk_F(ci)
            chunk_cg_o(ci, t8, r0, toks)
            chunk_state(t8, r0, toks)
        attnb_gate_out(half)
    tok_ab = sch.barrier()
    if upto == "attnb":
        tap_x4 = taps.get("t_x4")
        if tap_x4 is not None:
            for dt_ in range(8):
                dma(GP, tap_x4[128 * dt_:128 * (dt_ + 1), :],
                    xresA.ap()[128 * dt_:128 * (dt_ + 1), :], sig="dg")
        return _finish_early()

    phase_fence()
    sch.serial = serial
    sch.phase = "f2"
    tok_ffn2 = ffn_pass(xresA, xresB)

    sch.phase = "o"
    carry = {"l": None, "e": None}
    for it in range(16):
        for dt_ in range(8):
            bk = 2 + (dt_ % 2)
            tl = dma(GP, tmpa[:, 0:128],
                     xresB.ap()[128 * dt_:128 * (dt_ + 1), 128 * it:128 * it + 128],
                     waits=(carry["l"],), sig="dg")
            psl = pb[bk][:, 0:128]
            ttp = transpose(psl, tmpa[:, 0:128], sb["ident32"][:], waits=(tl,), bank=bk)
            carry["l"] = ttp
            tev = cp(VE, tmpb[:, 0:128], psl, waits=(ttp, carry["e"]))
            rdbank(bk, tev)
            tdm = dma(GP, out_own[128 * it:128 * (it + 1), 128 * dt_:128 * (dt_ + 1)],
                      tmpb[:, 0:128], waits=(tev,), sig="dg")
            carry["e"] = tdm

    end_tok = sch.barrier()
    final_waits = [(k, v) for k, v in sch.counts.items()
                   if k.startswith(("ds", "dg"))] + [end_tok]
    sch.do(SY, lambda e: e.nop(), waits=tuple(final_waits))
    print('SEM COUNTS:', {k: v for k, v in sch.counts.items()})
    sch.finalize()
    return nc


def kernel(**inputs):
    maps = _prep_inputs(inputs)
    nc = build_program()
    res = run_bass_kernel_spmd(nc, maps, core_ids=list(range(8)))
    out = np.zeros((B, S, D), np.float32)
    for c in range(8):
        b, r = c // 2, c % 2
        out[b, 2048 * r:2048 * (r + 1)] = np.asarray(res.results[c]["out_own"])
    return out


# version 03
# speedup vs baseline: 1.5746x; 1.4322x over previous
import contextlib
import dataclasses
import math
import numpy as np
import ml_dtypes

import concourse.bass as bass
import concourse.mybir as mybir
from concourse.bass_utils import run_bass_kernel_spmd

F32 = mybir.dt.float32
BF16 = mybir.dt.bfloat16
AF = mybir.ActivationFunctionType
ALU = mybir.AluOpType
NBF = ml_dtypes.bfloat16

B, S, D, H = 4, 4096, 1024, 16
HD = 64
SH = S // 2
LB = 1408
DMIN = -512
WT1 = LB + 1


def _rel_bucket(d):
    d = np.asarray(d)
    rp_f = np.maximum(d.astype(np.float64), 1.0)
    large = 16 + (np.log(rp_f / 16.0) / np.log(8.0) * 16.0).astype(np.int64)
    return np.where(d < 16, d, np.minimum(large, 31)).astype(np.int64)

def _host_consts():
    c = {}
    eye = np.eye(128, dtype=np.float32)
    c["ident32"] = eye
    c["identb"] = eye.astype(NBF)
    inv_freq = 1.0 / (10000.0 ** (np.arange(0, HD, 2, dtype=np.float64) / HD))
    freqs = np.outer(np.arange(S, dtype=np.float64), inv_freq)
    emb = np.concatenate([freqs, freqs], axis=-1)
    cos = np.cos(emb).T.astype(np.float32)
    sin = np.sin(emb).T.astype(np.float32)
    c["cosb"] = np.ascontiguousarray(np.concatenate([cos, cos], 0)).astype(NBF)
    c["sinb"] = np.ascontiguousarray(np.concatenate([sin, sin], 0)).astype(NBF)
    R = np.zeros((64, 64), np.float32)
    for i in range(32):
        R[i, 32 + i] = -1.0
        R[32 + i, i] = 1.0
    rotT = np.zeros((128, 128), np.float32)
    rotT[0:64, 0:64] = R.T
    rotT[64:128, 64:128] = R.T
    c["rotT"] = rotT.astype(NBF)
    ohcorr = np.zeros((32, LB), np.float32)
    m0 = np.zeros((16, LB), np.float32)
    for j in range(LB):
        d = j + DMIN
        if d >= 0:
            m0[:, j] = 1.0
        if 0 <= d <= 112:
            ohcorr[int(_rel_bucket(d)), j] += 1.0
            ohcorr[31, j] -= 1.0
    c["ohcorr"] = ohcorr
    c["m0rep"] = m0
    hrep = np.zeros((16, 2048), np.float32)
    for h in range(16):
        hrep[h, 128 * h:128 * (h + 1)] = 1.0
    c["hrep8"] = hrep
    c["lt64"] = np.triu(np.ones((64, 64), np.float32))
    ue = np.ones((128, 512), np.float32)
    tri = (np.arange(64)[:, None] <= np.arange(64)[None, :]).astype(np.float32)
    for blk in range(8):
        ue[0:64, 64 * blk:64 * blk + 64] = tri
    c["upperext"] = ue.astype(NBF)
    hs2 = np.zeros((128, 2), np.float32)
    hs2[0:64, 0] = 1.0
    hs2[64:128, 1] = 1.0
    c["hsum2"] = hs2.astype(NBF)
    bsel = np.zeros((2, 128), np.float32)
    bsel[0, 0:64] = 1.0
    bsel[1, 64:128] = 1.0
    c["blocksel"] = bsel
    e16 = np.zeros((16, 1024), np.float32)
    for h in range(16):
        e16[h, 64 * h:64 * h + 64] = 1.0
    c["e16"] = e16.astype(NBF)
    psel = np.zeros((16, 1024), np.float32)
    for p in range(8):
        psel[2 * p, 128 * p:128 * p + 64] = 1.0
        psel[2 * p + 1, 128 * p + 64:128 * p + 128] = 1.0
    c["pairsel"] = psel
    i64 = np.zeros((64, 1024), np.float32)
    for h in range(16):
        i64[:, 64 * h:64 * h + 64] = np.eye(64)
    c["i64rep"] = i64.astype(NBF)
    c["fconst"] = np.repeat(np.array([[0.0, 1.0, -1.0, 0.125, 1.0 / 64, 1.0 / 1024, 1e-5,
                                       1e-6, 1e-12, 0.25, 0.5, 2.0, math.sqrt(2.0), 0.0,
                                       0.0, 0.0]], np.float32), 128, axis=0)
    return c

def _tile_wT(w, mt, kt):
    wT = np.ascontiguousarray(w.T)
    return np.ascontiguousarray(wT.reshape(kt, 128, mt, 128).transpose(2, 0, 1, 3))

def _prep_inputs(inp):
    consts = _host_consts()
    x = np.ascontiguousarray(np.asarray(inp["L0_attn_a_x"], dtype=np.float32))
    Wq = np.asarray(inp["L0_attn_a_Wq"], np.float32)
    Wk = np.asarray(inp["L0_attn_a_Wk"], np.float32)
    Wv = np.asarray(inp["L0_attn_a_Wv"], np.float32)
    Wo = np.asarray(inp["L0_attn_a_Wo"], np.float32)
    rel = np.asarray(inp["L0_attn_a_rel_emb"], np.float32)
    sh = dict(consts)
    sh["wbq"] = np.asarray(inp["L0_attn_b_Wq"], np.float32)
    sh["wbk"] = np.asarray(inp["L0_attn_b_Wk"], np.float32)
    sh["wbv"] = np.asarray(inp["L0_attn_b_Wv"], np.float32)
    sh["wfb"] = np.ascontiguousarray(
        np.concatenate([np.asarray(inp["L0_attn_b_Wf"], np.float32),
                        np.asarray(inp["L0_attn_b_Wbet"], np.float32)], axis=1))
    sh["wbo"] = np.asarray(inp["L0_attn_b_Wo"], np.float32)
    sh["gw1"] = np.asarray(inp["L0_attn_b_gate_w1"], np.float32)
    sh["gw2"] = np.asarray(inp["L0_attn_b_gate_w2"], np.float32)
    sh["delta"] = np.asarray(inp["L0_attn_b_delta"], np.float32).reshape(1, 16)
    sh["lambp"] = np.ascontiguousarray(
        np.asarray(inp["L0_attn_b_lamb_params"], np.float32).reshape(8, 128).T)
    sh["nrmw"] = np.ascontiguousarray(
        np.asarray(inp["L0_attn_b_norm_weight"], np.float32).reshape(8, 128).T)
    sh["w1t"] = _tile_wT(np.asarray(inp["L0_ffn_w1_w"], np.float32), 32, 8)
    sh["w2t"] = _tile_wT(np.asarray(inp["L0_ffn_w2_w"], np.float32), 32, 8)
    sh["h1t"] = _tile_wT(np.asarray(inp["L0_ffn_h1_w"], np.float32), 32, 32)
    sh["h2t"] = _tile_wT(np.asarray(inp["L0_ffn_h2_w"], np.float32), 32, 32)
    sh["w3t"] = _tile_wT(np.asarray(inp["L0_ffn_w3_w"], np.float32), 8, 32)
    for nm, src in [("s1c", "L0_ffn_w1_s"), ("s2c", "L0_ffn_w2_s"),
                    ("sh1c", "L0_ffn_h1_s"), ("sh2c", "L0_ffn_h2_s")]:
        sh[nm] = np.ascontiguousarray(np.asarray(inp[src], np.float32).reshape(32, 128).T)
    sh["s3c"] = np.ascontiguousarray(np.asarray(inp["L0_ffn_w3_s"], np.float32).reshape(8, 128).T)
    maps = []
    for c in range(8):
        b, r = c // 2, c % 2
        m = dict(sh)
        m["x_full"] = x[b]
        m["x_own"] = np.ascontiguousarray(x[b, 2048 * r:2048 * (r + 1)])
        m["wqm"] = Wq
        m["wkm"] = Wk
        m["wvm"] = Wv
        m["wom"] = Wo
        m["rel16"] = rel
        m["gam"] = np.array([[1.0 - r, float(r)]], np.float32)
        gs = np.zeros((1, 4), np.float32)
        if r == 1:
            gs[0, b] = 1.0
        m["gsel"] = gs
        maps.append(m)
    return maps


ENGINES = ("sync", "vector", "scalar", "tensor", "gpsimd")
SY, VE, SC, TE, GP = ENGINES[0], ENGINES[1], ENGINES[2], ENGINES[3], ENGINES[4]

class Sched:
    def __init__(self, nc, es):
        self.nc = nc
        self.es = es
        self.ops = {e: [] for e in ENGINES}
        self.sems = {}
        self.counts = {}
        self.pending = {e: None for e in ENGINES}
        self.phase = ""
        self.serial = False
        self.skip_same = False
        self.last_eng = None
        self.last_tok = None
        self.bank_rd = [None] * 8

    def _sem(self, name):
        if name not in self.sems:
            self.sems[name] = self.es.enter_context(self.nc.semaphore(name))
            self.counts[name] = 0
        return self.sems[name]

    def do(self, eng, fn, waits=(), sig=None, n=1):
        import traceback
        _st = traceback.extract_stack()
        _lbl = ";".join(f"{f.lineno}" for f in _st[-4:-1])
        if sig is not None:
            if sig in ("ts", "vs", "ss", "ds", "dg", "gs"):
                sig = sig + self.phase
            self._sem(sig)
        ws = [w for w in waits if w is not None]
        ws = [(s, self.counts[s]) if s.startswith(("ds", "dg")) else (s, v) for (s, v) in ws]
        if self.pending[eng] is not None:
            ws.append(self.pending[eng])
            self.pending[eng] = None
        if self.serial and self.last_tok is not None and (
                (not self.skip_same) or eng != self.last_eng
                or eng in ("sync", "gpsimd")):
            ws.append(self.last_tok)
        for w in ws:
            self._sem(w[0])
        best = {}
        for s, v in ws:
            best[s] = max(best.get(s, 0), v)
        tok = None
        if sig is not None:
            self.counts[sig] += n
            tok = (sig, self.counts[sig])
        self.ops[eng].append((tuple(best.items()), fn, sig, n, _lbl))
        if sig is not None:
            self.last_tok = tok
            self.last_eng = eng
        return tok

    def barrier(self):
        self._sem("bar")
        tgt = self.counts["bar"] + len(ENGINES)
        for e in ENGINES:
            self.do(e, lambda en: en.nop(), sig="bar")
        tok = ("bar", tgt)
        for e in ENGINES:
            self.pending[e] = tok
        return tok

    def finalize(self):
        with self.nc.Block() as block:
            for e in ENGINES:
                ops = self.ops[e]
                def body(eng, ops=ops):
                    for waits, fn, sig, n, lbl in ops:
                        for (s, v) in waits:
                            eng.wait_ge(self.sems[s], v)
                        try:
                            ins = fn(eng)
                        except Exception:
                            print(f"EMIT FAIL at plan lines {lbl}")
                            raise
                        if sig is not None and ins is not None:
                            ins.then_inc(self.sems[sig], n)
                getattr(block, e)(body)


def build_program(debug_taps=(), upto=None, serial=False, serial_secs=("E",)):
    nc = bass.Bass("TRN2", target_bir_lowering=False, debug=False)
    es = contextlib.ExitStack()
    sch = Sched(nc, es)
    sch.serial = serial

    inn = {}
    def din(name, shape, dtype=F32):
        inn[name] = nc.dram_tensor(name, list(shape), dtype, kind="ExternalInput").ap()
        return inn[name]

    din("x_full", [S, D]); din("x_own", [SH, D])
    din("wqm", [D, D]); din("wkm", [D, D]); din("wvm", [D, D]); din("wom", [D, D])
    din("rel16", [32, 16])
    din("wbq", [D, D]); din("wbk", [D, D]); din("wbv", [D, D]); din("wfb", [D, 32])
    din("wbo", [D, D]); din("gw1", [D, 64]); din("gw2", [64, D])
    din("delta", [1, 16]); din("lambp", [128, 8]); din("nrmw", [128, 8])
    din("gam", [1, 2]); din("gsel", [1, 4])
    din("w1t", [32, 8, 128, 128]); din("w2t", [32, 8, 128, 128])
    din("h1t", [32, 32, 128, 128]); din("h2t", [32, 32, 128, 128]); din("w3t", [8, 32, 128, 128])
    din("s1c", [128, 32]); din("s2c", [128, 32]); din("sh1c", [128, 32]); din("sh2c", [128, 32])
    din("s3c", [128, 8])
    din("ident32", [128, 128]); din("identb", [128, 128], BF16)
    din("cosb", [128, S], BF16); din("sinb", [128, S], BF16); din("rotT", [128, 128], BF16)
    din("ohcorr", [32, LB]); din("m0rep", [16, LB]); din("hrep8", [16, 2048])
    din("lt64", [64, 64]); din("upperext", [128, 512], BF16); din("hsum2", [128, 2], BF16)
    din("blocksel", [2, 128]); din("e16", [16, 1024], BF16); din("pairsel", [16, 1024])
    din("i64rep", [64, 1024], BF16)
    din("fconst", [128, 16])

    out_own = nc.dram_tensor("out_own", [SH, D], F32, kind="ExternalOutput").ap()
    taps = {}
    for tname, tshape, tdt in debug_taps:
        taps[tname] = nc.dram_tensor(tname, list(tshape), tdt, kind="ExternalOutput").ap()

    toeh = [nc.dram_tensor(f"toeh{h}", [128 * WT1], BF16) for h in range(16)]
    attnout = nc.dram_tensor("attnout", [D, S], F32)
    xresA = nc.dram_tensor("xresA", [D, SH], F32)
    xresB = nc.dram_tensor("xresB", [D, SH], F32)
    wbf = nc.dram_tensor("wbf", [48 * 1024 * 1024], BF16)
    pkg_send = nc.dram_tensor("pkg_send", [128, 1024], F32)
    pkg_recv = nc.dram_tensor("pkg_recv", [1024, 1024], F32)
    woff = {}
    off = 0
    for nm, ntile in [("w1t", 32 * 8), ("w2t", 32 * 8), ("h1t", 32 * 32), ("h2t", 32 * 32),
                      ("w3t", 8 * 32)]:
        woff[nm] = off
        off += ntile * 128 * 128

    sb = {}
    def sball(name, shape, dtype=F32):
        sb[name] = nc.alloc_sbuf_tensor("s_" + name, list(shape), dtype)
        return sb[name]

    for nm, shp, dt in [("ident32", [128, 128], F32), ("identb", [128, 128], BF16),
                        ("rotT", [128, 128], BF16), ("lt64", [64, 64], F32),
                        ("upperext", [128, 512], BF16), ("hsum2", [128, 2], BF16),
                        ("blocksel", [2, 128], F32), ("e16", [16, 1024], BF16),
                        ("pairsel", [16, 1024], F32), ("fconst", [128, 16], F32),
                        ("delta", [1, 16], F32), ("gam", [1, 2], F32), ("gsel", [1, 4], F32),
                        ("lambp", [128, 8], F32), ("nrmw", [128, 8], F32),
                        ("s1c", [128, 32], F32), ("s2c", [128, 32], F32),
                        ("sh1c", [128, 32], F32), ("sh2c", [128, 32], F32),
                        ("s3c", [128, 8], F32), ("rel8s", [32, 16], F32)]:
        sball(nm, shp, dt)
    sball("onesb", [1, 128], BF16)
    sball("onesrowf", [1, 128], F32)
    sball("onescolf", [128, 1], F32)
    sball("lambcol", [128, 8], F32)
    sball("deltarep", [128, 16], F32)
    sball("gamrep", [128, 8], F32)
    sball("statc", [128, 16], F32)
    sball("rowf", [2, 1024], F32)

    fv = [0.0, 1.0, -1.0, 0.125, 1.0 / 64, 1.0 / 1024, 1e-5, 1e-6, 1e-12, 0.25, 0.5, 2.0,
          math.sqrt(2.0)]
    for i, v in enumerate(fv):
        nc.const_aps.register(F32, float(v), sb["fconst"][:, i:i + 1])

    pb = [nc.alloc_psum_tensor(f"pb{i}", [128, 512], F32) for i in range(7)]
    pbf = nc.alloc_psum_tensor("pbf", [128, 1024], BF16)

    ar1 = [nc.alloc_sbuf_tensor(f"ar1_{i}", [128, 4096], BF16) for i in range(8)]
    ar2 = [nc.alloc_sbuf_tensor(f"ar2_{i}", [128, 4096], BF16) for i in range(4)]
    ar3 = [nc.alloc_sbuf_tensor(f"ar3_{i}", [128, 4352], BF16) for i in range(4)]
    ar4 = [nc.alloc_sbuf_tensor(f"ar4_{i}", [128, 4096], BF16) for i in range(2)]
    qg = nc.alloc_sbuf_tensor("qg", [128, 1024], BF16)
    qg2 = nc.alloc_sbuf_tensor("qg2", [128, 512], BF16)
    ctxp = nc.alloc_sbuf_tensor("ctxp", [128, 4096], BF16)
    wstr = nc.alloc_sbuf_tensor("wstr", [128, 4096], BF16)
    wstr2 = nc.alloc_sbuf_tensor("wstr2", [128, 4096], BF16)
    tmpa = nc.alloc_sbuf_tensor("tmpa", [128, 1024], F32)
    tmpb = nc.alloc_sbuf_tensor("tmpb", [128, 1024], F32)
    tmpc = nc.alloc_sbuf_tensor("tmpc", [128, 1024], F32)
    tmpd = nc.alloc_sbuf_tensor("tmpd", [128, 1536], BF16)
    lnt = tmpc[:].bitcast(BF16)[:, 0:1024]
    pax0 = nc.alloc_sbuf_tensor("pax0", [128, 512], BF16)
    pax1 = nc.alloc_sbuf_tensor("pax1", [128, 512], BF16)
    costile = nc.alloc_sbuf_tensor("costile", [128, 512], BF16)
    sintile = nc.alloc_sbuf_tensor("sintile", [128, 512], BF16)

    ohc = ar3[0].ap().bitcast(F32)[0:32, 0:LB]
    m0s = ar3[1].ap().bitcast(F32)[0:16, 0:LB]
    hreps = ar3[2].ap().bitcast(F32)[0:16, 0:2048]
    bvals = ar3[3].ap().bitcast(F32)[0:16, 0:LB]

    def dma(queue, out, in_, waits=(), sig="ds", **kw):
        return sch.do(queue, lambda e: e.dma_start(out=out, in_=in_, **kw), waits, sig, 16)

    def mm(out, lhsT, rhs, start, stop, waits=(), sig="ts", tp=None, bank=None):
        w = list(waits)
        if bank is not None and start and sch.bank_rd[bank] is not None:
            w.append(sch.bank_rd[bank])
        return sch.do(TE, lambda e: e.matmul(out, lhsT, rhs, start=start, stop=stop,
                                             tile_position=tp), w, sig, 1)

    def rdbank(bank, tok):
        sch.bank_rd[bank] = tok

    def cp(eng, out, in_, waits=(), sig=None):
        if sig is None:
            sig = {"vector": "vs", "scalar": "ss", "gpsimd": "gs"}[eng]
        return sch.do(eng, lambda e: e.tensor_copy(out=out, in_=in_), waits, sig, 1)

    def act(out, in_, func, waits=(), sig="ss", bias=0.0, scale=1.0, accum=None):
        return sch.do(SC, lambda e: e.activation(out, in_, func, bias=bias, scale=scale,
                                                 accum_out=accum), waits, sig, 1)

    def tt(out, a, b_, op, waits=(), sig="vs"):
        return sch.do(VE, lambda e: e.tensor_tensor(out=out, in0=a, in1=b_, op=op),
                      waits, sig, 1)

    def ts_(out, a, s1, s2, op0, op1=ALU.bypass, waits=(), sig="vs", accum=None):
        return sch.do(VE, lambda e: e.tensor_scalar(out=out, in0=a, scalar1=s1, scalar2=s2,
                                                    op0=op0, op1=op1, accum_out=accum),
                      waits, sig, 1)

    def stt(out, in0, scalar, in1, op0, op1, waits=(), sig="vs", accum=None):
        return sch.do(VE, lambda e: e.scalar_tensor_tensor(out=out, in0=in0, scalar=scalar,
                                                           in1=in1, op0=op0, op1=op1,
                                                           accum_out=accum), waits, sig, 1)

    def transpose(psl, src, ident, waits=(), bank=None):
        w = list(waits)
        if bank is not None and sch.bank_rd[bank] is not None:
            w.append(sch.bank_rd[bank])
        return sch.do(TE, lambda e: e.transpose(psl, src, ident), w, "ts", 1)


    def _finish_early():
        for it in range(16):
            dma(SY, out_own[128 * it:128 * (it + 1), :], tmpa[:].bitcast(F32)[:, 0:1024]
                if False else tmpa[:], waits=())
        et = sch.barrier()
        sch.do(SY, lambda e: e.nop(), waits=(("ds", sch.counts.get("ds", 0)),
                                             ("dg", sch.counts.get("dg", 0)), et))
        sch.finalize()
        return nc
    for nm in ["ident32", "identb", "rotT", "lt64", "upperext", "hsum2", "blocksel", "e16",
               "pairsel", "fconst", "delta", "lambp", "nrmw", "s1c", "s2c", "sh1c", "sh2c",
               "s3c", "gam", "gsel", "rel8s"]:
        src = inn["rel16"] if nm == "rel8s" else inn[nm]
        tok_con = dma(SY, sb[nm][:], src)
    tok_oh = dma(SY, ohc, inn["ohcorr"])
    tok_m0 = dma(SY, m0s, inn["m0rep"])
    tok_hr = dma(SY, hreps, inn["hrep8"])

    t1 = sch.do(VE, lambda e: e.memset(sb["onesb"][:], 1.0), sig="vs")
    t2 = sch.do(VE, lambda e: e.memset(sb["onesrowf"][:], 1.0), sig="vs")
    sch.do(VE, lambda e: e.memset(sb["onescolf"][:], 1.0), sig="vs")
    td = mm(pb[6][:, 4:20], sb["onesrowf"][:], sb["delta"][:], True, True,
            waits=(tok_con, t2), bank=6)
    tdr = cp(VE, sb["deltarep"][:], pb[6][:, 4:20], waits=(td,))
    rdbank(6, tdr)
    lsc = qg2[:].bitcast(F32)
    ta1 = act(lsc[:, 0:8], sb["lambp"][:], AF.Abs, waits=(tok_con,))
    ta2 = act(lsc[:, 8:16], lsc[:, 0:8], AF.Exp, scale=-1.0, waits=(ta1,))
    ta3 = act(lsc[:, 16:24], lsc[:, 8:16], AF.Ln, bias=1.0, waits=(ta2,))
    ta4 = sch.do(VE, lambda e: e.tensor_relu(out=lsc[:, 24:32], in_=sb["lambp"][:]),
                 waits=(tok_con,), sig="vs")
    ta5 = tt(lsc[:, 32:40], lsc[:, 16:24], lsc[:, 24:32], ALU.add, waits=(ta3, ta4))
    tok_lamb = ts_(sb["lambcol"][:], lsc[:, 32:40], 0.25, None, ALU.add, waits=(ta5,))

    bv_last = None
    for o in range(0, LB, 512):
        n = min(512, LB - o)
        tmm = mm(pb[5][0:16, 0:n], sb["rel8s"][:], ohc[:, o:o + n], True, True,
                 waits=(tok_con, tok_oh), bank=5)
        tex = act(qg[:].bitcast(F32)[0:16, 0:n], pb[5][0:16, 0:n], AF.Exp, waits=(tmm,))
        rdbank(5, tex)
        bv_last = tt(bvals[0:16, o:o + n], qg[:].bitcast(F32)[0:16, 0:n],
                     m0s[0:16, o:o + n], ALU.mult, waits=(tex, tok_m0))
    if "t_bv" in taps:
        dma(SY, taps["t_bv"], bvals, waits=(bv_last,))
    toe_last_dma = None
    toe_mm_last = None
    for h in range(16):
        reps = []
        for o in range(0, LB, 512):
            n = min(512, LB - o)
            tmm = mm(pb[5][:, 0:n], hreps[:, 128 * h:128 * h + 128], bvals[0:16, o:o + n],
                     True, True, waits=(bv_last, tok_hr), bank=5)
            toe_mm_last = tmm
            tc = cp(VE, tmpd[:, o:o + n], pb[5][:, 0:n], waits=(tmm,))
            rdbank(5, tc)
            reps.append(tc)
        dst = dataclasses.replace(toeh[h].ap(), ap=[[WT1, 128], [1, LB]], offset=0)
        toe_last_dma = dma(GP, dst, tmpd[:, 0:LB], waits=tuple(reps), sig="dg")
        if h == 0 and "t_tmpd" in taps:
            dma(SY, taps["t_tmpd"], tmpd[:, 0:LB], waits=tuple(reps))
        if h == 0 and "t_toe0" in taps:
            srcq = dataclasses.replace(toeh[0].ap(), ap=[[1, 1], [1, 4096]], offset=0)
            dma(SY, taps["t_toe0"], srcq, waits=(toe_last_dma,))
        sch.pending[VE] = toe_last_dma

    if upto == "consts":
        return _finish_early()
    prev = {"iter": None, "tp": None}
    for it in range(32):
        buf = tmpa if it % 2 == 0 else tmpb
        w = [tok_con]
        if prev["iter"] is not None:
            w.append(prev["iter"])
        tl = dma(SY, buf[:], inn["x_full"][128 * it:128 * (it + 1), :], waits=tuple(w))
        tm = sch.do(VE, lambda e, b_=buf: e.reduce_sum(out=sb["statc"][:, 0:1], in_=b_[:],
                                                       axis=mybir.AxisListType.X),
                    waits=(tl,), sig="vs")
        tsq = act(lnt, buf[:], AF.Square, accum=sb["statc"][:, 1:2], waits=(tl,))
        tm2 = ts_(sb["statc"][:, 2:3], sb["statc"][:, 0:1], 1.0 / 1024, None, ALU.mult,
                  waits=(tm,))
        tv1 = ts_(sb["statc"][:, 3:4], sb["statc"][:, 2:3], sb["statc"][:, 2:3], None, ALU.mult,
                  waits=(tm2,))
        tv2 = ts_(sb["statc"][:, 4:5], sb["statc"][:, 1:2], 1.0 / 1024, None, ALU.mult,
                  waits=(tsq,))
        tv3 = tt(sb["statc"][:, 5:6], sb["statc"][:, 4:5], sb["statc"][:, 3:4], ALU.subtract,
                 waits=(tv1, tv2))
        tv4 = ts_(sb["statc"][:, 5:6], sb["statc"][:, 5:6], 1e-5, None, ALU.add, waits=(tv3,))
        tv5 = act(sb["statc"][:, 6:7], sb["statc"][:, 5:6], AF.Sqrt, waits=(tv4,))
        tv6 = sch.do(VE, lambda e: e.reciprocal(out=sb["statc"][:, 7:8],
                                                in_=sb["statc"][:, 6:7]),
                     waits=(tv5,), sig="vs")
        tn_ = ts_(lnt, buf[:], sb["statc"][:, 2:3], sb["statc"][:, 7:8],
                  ALU.subtract, ALU.mult, waits=(tm2, tv6))
        last_ev = None
        for dt_ in range(8):
            psl = pbf[:, 128 * (dt_ % 8):128 * (dt_ % 8) + 128]
            w2 = [tn_, tok_con]
            if sch.bank_rd[7] is not None:
                w2.append(sch.bank_rd[7])
            ttp = sch.do(TE, lambda e, ps=psl, d=dt_: e.transpose(
                ps, lnt[:, 128 * d:128 * (d + 1)], sb["identb"][:]), tuple(w2), "ts", 1)
            tev = cp(VE, ar1[dt_][:, 128 * it:128 * (it + 1)], psl, waits=(ttp,))
            rdbank(7, tev)
            last_ev = tev
        prev["iter"] = last_ev
    tok_ln1 = sch.barrier()

    if upto == "ln1":
        return _finish_early()
    def load_wcast(dst, name, din_, m, mlo=0, rlo=0, waits=()):
        k = din_ // 128
        src_ = inn[name][rlo:rlo + din_, :].rearrange("(k p) m -> p k m", p=128)[:, :, mlo:mlo + m]
        return dma(GP, dst[:, 0:k * m].rearrange("p (k m) -> p k m", k=k), src_,
                   waits=waits, sig="dg")

    DELTAS = list(range(-384, 385, 128))
    ffn_conv_chunks = []
    for nm in ["w1t", "w2t", "h1t", "h2t", "w3t"]:
        nel = int(np.prod(inn[nm].shape))
        flat_in = inn[nm].rearrange("a b c d -> (a b c d)")
        CH = 4 * 1024 * 1024
        for o in range(0, nel, CH):
            n = min(CH, nel - o)
            ffn_conv_chunks.append((wbf.ap()[woff[nm] + o: woff[nm] + o + n],
                                    flat_in[o:o + n]))
    conv_i = 0
    conv_toks = []
    prev_wo_dma = None
    ctxp_guard = None
    kT = ar2

    for hg in range(2):
        tok_wk = load_wcast(wstr, "wkm", 1024, 512, mlo=512 * hg, waits=(tok_ln1,))
        sch.serial = serial or ("E" in serial_secs)
        kcarry = {"rope": None, "rot": None}
        for pair in range(4):
            for nch in range(8):
                dst = pb[5][:, 0:512]
                tk = None
                for kt_ in range(8):
                    tk = mm(dst, wstr[:, 512 * kt_ + 128 * pair:512 * kt_ + 128 * pair + 128],
                            ar1[kt_][:, 512 * nch:512 * (nch + 1)], kt_ == 0, kt_ == 7,
                            waits=(tok_wk,) if kt_ == 0 else (), bank=5)
                tcs = dma(GP, costile[:], inn["cosb"][:, 512 * nch:512 * (nch + 1)],
                          waits=(kcarry["rope"],), sig="dg")
                tsn = dma(GP, sintile[:], inn["sinb"][:, 512 * nch:512 * (nch + 1)],
                          waits=(kcarry["rope"],), sig="dg")
                w_e1 = [tk, kcarry["rot"], kcarry["rope"]]
                te1 = cp(VE, tmpd[:, 0:512], dst, waits=tuple(w_e1))
                rdbank(5, te1)
                trm = mm(pb[4][:, 0:512], sb["rotT"][:], tmpd[:, 0:512], True, True,
                         waits=(te1,), bank=4)
                tt1 = tt(tmpa[:, 0:512], tmpd[:, 0:512], costile[:], ALU.mult,
                         waits=(te1, tcs))
                te2 = act(tmpd[:, 512:1024], pb[4][:, 0:512], AF.Copy, waits=(trm,))
                rdbank(4, te2)
                kcarry["rot"] = te2
                tt2 = tt(tmpb[:, 0:512], tmpd[:, 512:1024], sintile[:], ALU.mult,
                         waits=(te2, tsn))
                kcarry["rope"] = tt(kT[pair][:, 512 * nch:512 * (nch + 1)], tmpa[:, 0:512],
                                    tmpb[:, 0:512], ALU.add, waits=(tt1, tt2))
        sch.serial = serial
        tok_kT = sch.barrier()
        if "F" in serial_secs:
            sch.serial = True

        tok_wv = load_wcast(wstr, "wvm", 1024, 512, mlo=512 * hg, waits=(tok_kT,))
        vt = []
        for i in range(32):
            a = ar3[i // 8]
            o = (i % 8) * 544
            vt.append(a[:, o:o + 520])
        first_v = (toe_mm_last, tok_kT) if hg == 0 else (tok_kT,)
        for ti in range(32):
            dst = pb[5][:, 0:512]
            tk = None
            for kt_ in range(8):
                w = (tok_wv,) + first_v if kt_ == 0 else ()
                tk = mm(dst, ar1[kt_][:, 128 * ti:128 * (ti + 1)],
                        wstr[:, 512 * kt_:512 * (kt_ + 1)], kt_ == 0, kt_ == 7,
                        waits=w, bank=5)
            dstv = dataclasses.replace(vt[ti], ap=[vt[ti].ap[0], [65, 8], [1, 64]])
            tv = sch.do(VE, lambda e, d=dstv, s=dst: e.tensor_copy(
                out=d, in_=s.rearrange("p (h x) -> p h x", h=8)), waits=(tk,), sig="vs")
            rdbank(5, tv)
            onesv = dataclasses.replace(vt[ti], ap=[vt[ti].ap[0], [65, 8]],
                                        offset=vt[ti].offset + 64)
            sch.do(VE, lambda e, d=onesv: e.memset(d, 1.0), waits=(tv,), sig="vs")
        sch.serial = serial
        tok_v = sch.barrier()

        tok_wq = load_wcast(wstr, "wqm", 1024, 512, mlo=512 * hg, waits=(tok_v,))
        tok_wo = load_wcast(wstr2, "wom", 512, 1024, rlo=512 * hg, waits=(tok_v,))
        for pair in range(4):
            bc = [ar4[0], ar4[1]]
            btoks = []
            for di, dl in enumerate(DELTAS):
                c0 = dl + 512
                for hh in range(2):
                    h = 8 * hg + 2 * pair + hh
                    srcb = dataclasses.replace(toeh[h].ap(), ap=[[LB, 128], [1, 512]],
                                               offset=c0)
                    btoks.append(dma(GP, bc[hh][:, 512 * di:512 * (di + 1)], srcb,
                                     waits=(toe_last_dma, tok_v), sig="dg"))
            tok_b = btoks[-1]
            if hg == 0 and pair == 0 and "t_bc" in taps:
                dma(SY, taps["t_bc"], bc[0][:, 0:3584], waits=(tok_b,))
            ctx_last = None
            last_S = None
            prev_tqr = None
            for g in range(8):
                sch.serial = serial or ("C" in serial_secs)
                qdst = pb[5][:, 0:512]
                tk = None
                for kt_ in range(8):
                    tk = mm(qdst, wstr[:, 512 * kt_ + 128 * pair:512 * kt_ + 128 * pair + 128],
                            ar1[kt_][:, 512 * g:512 * (g + 1)], kt_ == 0, kt_ == 7,
                            waits=(tok_wq,) if kt_ == 0 else (), bank=5)
                tcs = dma(GP, costile[:], inn["cosb"][:, 512 * g:512 * (g + 1)],
                          waits=(prev_tqr,), sig="dg")
                tsn = dma(GP, sintile[:], inn["sinb"][:, 512 * g:512 * (g + 1)],
                          waits=(prev_tqr,), sig="dg")
                te1 = act(qg[:, 0:512], qdst, AF.Copy, scale=0.125,
                          waits=(tk, last_S, prev_tqr))
                rdbank(5, te1)
                trm = mm(pb[4][:, 0:512], sb["rotT"][:], qg[:, 0:512], True, True,
                         waits=(te1,), bank=4)
                te2 = act(qg[:, 512:1024], pb[4][:, 0:512], AF.Copy, waits=(trm, prev_tqr))
                rdbank(4, te2)
                tt1 = tt(tmpa[:, 0:512], qg[:, 0:512], costile[:], ALU.mult,
                         waits=(te1, tcs))
                tt2 = tt(tmpb[:, 0:512], qg[:, 512:1024], sintile[:], ALU.mult,
                         waits=(te2, tsn))
                tqr = tt(qg2[:, 0:512], tmpa[:, 0:512], tmpb[:, 0:512], ALU.add,
                         waits=(tt1, tt2, last_S))
                prev_tqr = tqr
                sch.serial = serial or ("A" in serial_secs)
                nkt = 4 * (g + 1)
                pa = [tmpd[:, 0:512], tmpd[:, 512:1024]]
                olast = [None, None]
                last_tP = [None, None]
                for kt_ in range(nkt):
                    dl = 512 * g - 128 * kt_
                    near = -384 <= dl <= 384
                    for hh in range(2):
                        srow = slice(64 * hh, 64 * hh + 64)
                        sps = pb[hh][:, 0:512]
                        tS = mm(sps, kT[pair][srow, 128 * kt_:128 * (kt_ + 1)],
                                qg2[srow, 0:512], True, True, waits=(tqr,), sig="ts",
                                tp=(64 * hh, 0), bank=hh)
                        last_S = tS
                        tE = act(pa[hh], sps, AF.Exp, waits=(tS, last_tP[hh], olast[hh]))
                        rdbank(hh, tE)
                        tP = tE
                        avsrc = pa[hh]
                        if near:
                            di = DELTAS.index(dl)
                            pax = pax0 if hh == 0 else pax1
                            tP = tt(pax[:, 0:512], pa[hh], bc[hh][:, 512 * di:512 * (di + 1)],
                                    ALU.mult, waits=(tE, tok_b, last_tP[hh]))
                            last_tP[hh] = tP
                            avsrc = pax[:, 0:512]
                        hl = 2 * pair + hh
                        tAV = mm(pb[2 + hh][0:65, 0:512], vt[kt_][:, 65 * hl:65 * hl + 65],
                                 avsrc, kt_ == 0, kt_ == nkt - 1, waits=(tP,),
                                 bank=(2 + hh) if kt_ == 0 else None)
                        olast[hh] = tAV
                sch.serial = serial or ("B" in serial_secs)
                for hh in range(2):
                    ob = pb[2 + hh]
                    trc = sch.do(VE, lambda e, ob_=ob: e.reciprocal(
                        out=sb["rowf"][0:1, 0:512], in_=ob_[64:65, 0:512]),
                        waits=(olast[hh], ctx_last), sig="vs")
                    tlr = mm(pb[4][0:64, 0:512], sb["onesrowf"][0:1, 0:64],
                             sb["rowf"][0:1, 0:512], True, True, waits=(trc,), bank=4)
                    tos = act(tmpc[0:64, 512 * hh:512 * hh + 512], ob[0:64, 0:512], AF.Copy,
                              waits=(olast[hh], trc, prev_wo_dma))
                    tcx = tt(ctxp[64 * hh:64 * hh + 64, 512 * g:512 * (g + 1)],
                             tmpc[0:64, 512 * hh:512 * hh + 512], pb[4][0:64, 0:512],
                             ALU.mult, waits=(tlr, tos, ctxp_guard))
                    rdbank(2 + hh, tcx)
                    rdbank(4, tcx)
                    ctx_last = tcx
                sch.serial = serial
            if "t_ctx8" in taps:
                ctxp_guard = dma(SY,
                                 taps["t_ctx8"][128 * (4 * hg + pair):128 * (4 * hg + pair) + 128, :],
                                 ctxp[:], waits=(ctx_last,))
            sch.serial = serial or ("D" in serial_secs)
            evac_carry = [None, None]
            wo_dma_last = None
            for mt in range(8):
                for nch in range(8):
                    dst = pb[6][:, 0:512]
                    lhs = wstr2[:, 1024 * pair + 128 * mt:1024 * pair + 128 * mt + 128]
                    tmm = mm(dst, lhs, ctxp[:, 512 * nch:512 * (nch + 1)], True, True,
                             waits=(tok_wo, ctx_last, prev_wo_dma), bank=6)
                    sl = (mt * 8 + nch) % 2
                    tev = cp(VE, tmpc[:, 512 * sl:512 * sl + 512], dst,
                             waits=(tmm, evac_carry[sl]))
                    rdbank(6, tev)
                    dstp = attnout.ap()[128 * mt:128 * (mt + 1), 512 * nch:512 * (nch + 1)]
                    kw = {} if (hg == 0 and pair == 0) else {"accum_op": ALU.add}
                    tdm = dma(GP, dstp, tmpc[:, 512 * sl:512 * sl + 512], waits=(tev,),
                              sig="dg", **kw)
                    evac_carry[sl] = tdm
                    wo_dma_last = tdm
            prev_wo_dma = wo_dma_last
            sch.serial = serial
            n_emit = (len(ffn_conv_chunks) + 7) // 8
            for _ in range(n_emit):
                if conv_i < len(ffn_conv_chunks):
                    dsty, srcy = ffn_conv_chunks[conv_i]
                    conv_toks.append(dma(GP, dsty, srcy, sig="dg"))
                    conv_i += 1
            sch.barrier()
    while conv_i < len(ffn_conv_chunks):
        dsty, srcy = ffn_conv_chunks[conv_i]
        conv_toks.append(dma(GP, dsty, srcy, sig="dg"))
        conv_i += 1
    tok_conv = conv_toks[-1]
    tok_attn = sch.barrier()
    if upto == "attn":
        return _finish_early()

    tgm = mm(pb[6][:, 0:2], sb["onesrowf"][:], sb["gam"][:], True, True,
             waits=(tok_attn,), bank=6)
    tgm2 = mm(pb[6][:, 2:6], sb["onesrowf"][:], sb["gsel"][:], True, True,
              waits=(tok_attn,), bank=None)
    tok_gam = cp(VE, sb["gamrep"][:, 0:6], pb[6][:, 0:6], waits=(tgm, tgm2))
    rdbank(6, tok_gam)

    carry = {"a": None, "b": None, "l": None, "c": None}
    for it in range(16):
        tl = dma(SY, tmpa[:], inn["x_own"][128 * it:128 * (it + 1), :],
                 waits=(tok_attn, carry["l"]))
        for dt_ in range(8):
            bk = dt_ % 2
            psl = pb[bk][:, 0:128]
            ttp = transpose(psl, tmpa[:, 128 * dt_:128 * (dt_ + 1)], sb["ident32"][:],
                            waits=(tl, tok_con), bank=bk)
            tr0 = dma(GP, tmpb[:, 0:128],
                      attnout.ap()[128 * dt_:128 * (dt_ + 1), 128 * it:128 * it + 128],
                      waits=(tok_attn, prev_wo_dma, carry["a"]), sig="dg")
            tr1 = dma(GP, tmpb[:, 128:256],
                      attnout.ap()[128 * dt_:128 * (dt_ + 1),
                                   2048 + 128 * it:2048 + 128 * it + 128],
                      waits=(tok_attn, prev_wo_dma, carry["a"]), sig="dg")
            tg1 = ts_(tmpb[:, 256:384], tmpb[:, 0:128], sb["gamrep"][:, 0:1], None,
                      ALU.mult, waits=(tr0, tok_gam, carry["c"]))
            tg2 = stt(tmpb[:, 384:512], tmpb[:, 128:256], sb["gamrep"][:, 1:2],
                      tmpb[:, 256:384], ALU.mult, ALU.add, waits=(tr1, tg1))
            tad = tt(tmpc[:, 0:128], psl, tmpb[:, 384:512], ALU.add,
                     waits=(ttp, tg2, carry["b"]))
            rdbank(bk, tad)
            carry["a"] = tg2
            carry["c"] = tad
            tdm = dma(GP, xresA.ap()[128 * dt_:128 * (dt_ + 1), 128 * it:128 * it + 128],
                      tmpc[:, 0:128], waits=(tad,), sig="dg")
            carry["b"] = tdm
            if dt_ == 7:
                carry["l"] = tad
    tok_x2 = sch.barrier()

    if "t_ln1" in taps:
        for dt_ in range(8):
            dma(SY, taps["t_ln1"][128 * dt_:128 * (dt_ + 1), :], ar1[dt_][:],
                waits=(tok_ln1,))
    if "t_ao" in taps:
        dma(SY, taps["t_ao"], attnout.ap(), waits=(tok_attn, prev_wo_dma))
    if "t_toe" in taps:
        dma(SY, taps["t_toe"], toeh[0].ap()[0:8192].rearrange("(a b) -> a b", a=2),
            waits=(tok_attn,))
    if "t_btile" in taps:
        for di, dl in enumerate([0, -128, 384]):
            srcb = dataclasses.replace(toeh[0].ap(), ap=[[LB, 128], [1, 512]],
                                       offset=dl + 512)
            dma(SY, taps["t_btile"][128 * di:128 * (di + 1), :], srcb, waits=(tok_attn,))
    if "t_x2" in taps:
        for dt_ in range(8):
            dma(SY, taps["t_x2"][128 * dt_:128 * (dt_ + 1), :],
                xresA.ap()[128 * dt_:128 * (dt_ + 1), :], waits=(tok_x2,))

    def phase_fence():
        ws = []
        for pre in ("ds", "dg"):
            nm = pre + sch.phase
            if sch.counts.get(nm):
                ws.append((nm, sch.counts[nm]))
        if sch.last_tok is not None:
            ws.append(sch.last_tok)
        sch.do(SY, lambda e: e.nop(), waits=tuple(ws))
        return sch.barrier()

    phase_fence()
    sch.serial = serial

    def ffn_pass(xin, xout):
        TB = 512
        xb = lambda dt: ar1[dt // 4][:].bitcast(F32)[:, 512 * (dt % 4):512 * (dt % 4) + TB]
        lnb = lambda dt: ar1[2][:, TB * dt:TB * dt + TB]
        hbuf = lambda k: ar2[k // 8][:, TB * (k % 8):TB * (k % 8) + TB]
        gbuf = lambda k: ar3[k // 8][:, TB * (k % 8):TB * (k % 8) + TB]
        slab = [wstr, wstr2]
        for tb in range(4):
            if tb:
                sch.barrier()
            slab_rd = [None, None]
            col0 = TB * tb
            dx = None
            for dt in range(8):
                dx = dma(GP if dt < 4 else SY, xb(dt),
                         xin.ap()[128 * dt:128 * (dt + 1), col0:col0 + TB],
                         sig="dg" if dt < 4 else "ds")
            m4last = None
            for dt in range(8):
                m4last = mm(pb[4][0:1, 0:TB], sb["onescolf"][:], xb(dt), dt == 0, dt == 7,
                            waits=(dx,) if dt == 0 else (), bank=4 if dt == 0 else None)
            m5last = None
            for dt in range(8):
                tq = act(tmpa[:, 0:TB], xb(dt), AF.Square,
                         waits=((dx,) if dt == 0 else ()) +
                               ((m5last,) if m5last is not None else ()))
                m5last = mm(pb[5][0:1, 0:TB], sb["onescolf"][:], tmpa[:, 0:TB],
                            dt == 0, dt == 7, waits=(tq,), bank=5 if dt == 0 else None)
            tm = ts_(sb["rowf"][0:1, 0:TB], pb[4][0:1, 0:TB], 1.0 / 1024, None, ALU.mult,
                     waits=(m4last,))
            rdbank(4, tm)
            te = ts_(tmpa[0:1, 0:TB], pb[5][0:1, 0:TB], 1.0 / 1024, None, ALU.mult,
                     waits=(m5last,))
            rdbank(5, te)
            tv = tt(tmpa[0:1, TB:2 * TB], sb["rowf"][0:1, 0:TB], sb["rowf"][0:1, 0:TB],
                    ALU.mult)
            tv2 = tt(tmpa[0:1, TB:2 * TB], tmpa[0:1, 0:TB], tmpa[0:1, TB:2 * TB],
                     ALU.subtract)
            tv3 = ts_(tmpa[0:1, TB:2 * TB], tmpa[0:1, TB:2 * TB], 1e-5, None, ALU.add)
            tv4 = act(tmpc[0:1, 0:TB], tmpa[0:1, TB:2 * TB], AF.Sqrt, waits=(tv3,))
            tv5 = sch.do(VE, lambda e: e.reciprocal(out=tmpa[0:1, TB:2 * TB],
                                                    in_=tmpc[0:1, 0:TB]),
                         waits=(tv4,), sig="vs")
            tmm4 = mm(pb[4][:, 0:TB], sb["onesrowf"][:], sb["rowf"][0:1, 0:TB], True, True,
                      waits=(tm,), bank=4)
            tmm5 = mm(pb[5][:, 0:TB], sb["onesrowf"][:], tmpa[0:1, TB:2 * TB], True, True,
                      waits=(tv5,), bank=5)
            t2last = None
            for dt in range(8):
                t1 = tt(tmpb[:, 0:TB], xb(dt), pb[4][:, 0:TB], ALU.subtract,
                        waits=(tmm4, tmm5) if dt == 0 else ())
                t2last = tt(lnb(dt), tmpb[:, 0:TB], pb[5][:, 0:TB], ALU.mult)
                if dt == 7:
                    rdbank(4, t2last)
                    rdbank(5, t2last)
            tt_prev = None
            for mt in range(32):
                si = mt % 2
                sl = slab[si]
                war = (slab_rd[si],) if slab_rd[si] is not None else ()
                d1 = dma(GP, sl[:, 0:1024].rearrange("p (k m) -> p k m", k=8),
                         inn["w1t"][mt].rearrange("k a b -> a k b"), waits=war, sig="dg")
                d2 = dma(GP, sl[:, 1024:2048].rearrange("p (k m) -> p k m", k=8),
                         inn["w2t"][mt].rearrange("k a b -> a k b"), waits=war, sig="dg")
                m1 = None
                for k in range(8):
                    m1 = mm(pb[0][:, 0:TB], sl[:, 128 * k:128 * k + 128], lnb(k),
                            k == 0, k == 7,
                            waits=((d1, d2, t2last) if mt == 0 else (d1, d2))
                            if k == 0 else (), bank=0)
                m2 = None
                for k in range(8):
                    m2 = mm(pb[1][:, 0:TB], sl[:, 1024 + 128 * k:1024 + 128 * k + 128],
                            lnb(k), k == 0, k == 7, bank=1)
                slab_rd[si] = m2
                a1 = act(tmpd[:, 0:TB], pb[0][:, 0:TB], AF.Silu,
                         scale=sb["s1c"][:, mt:mt + 1],
                         waits=(m1,) + ((tt_prev,) if tt_prev is not None else ()))
                rdbank(0, a1)
                a2 = act(tmpd[:, TB:2 * TB], pb[1][:, 0:TB], AF.Copy,
                         scale=sb["s2c"][:, mt:mt + 1], waits=(m2,))
                rdbank(1, a2)
                tt_prev = tt(hbuf(mt), tmpd[:, 0:TB], tmpd[:, TB:2 * TB], ALU.mult,
                             waits=(a1, a2))
            src_ready = tt_prev
            for layer, wnm, scnm, srcf, dstf in (("h1", "h1t", "sh1c", hbuf, gbuf),
                                                 ("h2", "h2t", "sh2c", gbuf, hbuf)):
                last_act = None
                for mt in range(32):
                    si = mt % 2
                    sl = slab[si]
                    war = (slab_rd[si],) if slab_rd[si] is not None else ()
                    d1 = dma(GP, sl[:, 0:4096].rearrange("p (k m) -> p k m", k=32),
                             inn[wnm][mt].rearrange("k a b -> a k b"), waits=war,
                             sig="dg")
                    m1 = None
                    for k in range(32):
                        m1 = mm(pb[0][:, 0:TB], sl[:, 128 * k:128 * k + 128], srcf(k),
                                k == 0, k == 31,
                                waits=((d1, src_ready) if mt == 0 else (d1,))
                                if k == 0 else (), bank=0)
                    slab_rd[si] = m1
                    last_act = act(dstf(mt), pb[0][:, 0:TB], AF.Gelu,
                                   scale=sb[scnm][:, mt:mt + 1], waits=(m1,))
                    rdbank(0, last_act)
                src_ready = last_act
            prev_odma = None
            for mo in range(8):
                si = mo % 2
                sl = slab[si]
                war = (slab_rd[si],) if slab_rd[si] is not None else ()
                d1 = dma(GP, sl[:, 0:4096].rearrange("p (k m) -> p k m", k=32),
                         inn["w3t"][mo].rearrange("k a b -> a k b"), waits=war, sig="dg")
                m1 = None
                for k in range(32):
                    m1 = mm(pb[0][:, 0:TB], sl[:, 128 * k:128 * k + 128], hbuf(k),
                            k == 0, k == 31,
                            waits=((d1, src_ready) if mo == 0 else (d1,))
                            if k == 0 else (), bank=0)
                slab_rd[si] = m1
                s1 = stt(tmpc[:, 0:TB], pb[0][:, 0:TB], sb["s3c"][:, mo:mo + 1], xb(mo),
                         ALU.mult, ALU.add,
                         waits=(m1,) + ((prev_odma,) if prev_odma is not None else ()))
                rdbank(0, s1)
                prev_odma = dma(SY, xout.ap()[128 * mo:128 * (mo + 1), col0:col0 + TB],
                                tmpc[:, 0:TB], waits=(s1,), sig="ds")
        return phase_fence()

    sch.phase = "f1"
    tok_ffn1 = ffn_pass(xresA, xresB)
    if upto == "ffn1":
        tap_x3 = taps.get("t_x3")
        if tap_x3 is not None:
            for dt_ in range(8):
                dma(GP, tap_x3[128 * dt_:128 * (dt_ + 1), :],
                    xresB.ap()[128 * dt_:128 * (dt_ + 1), :], sig="dg")
        return _finish_early()

    lnb3 = lambda dt: ar1[dt // 4][:, 1024 * (dt % 4):1024 * (dt % 4) + 1024]
    kTb = lambda p: ar1[2 + p // 4][:, 1024 * (p % 4):1024 * (p % 4) + 1024]
    qTb = lambda p: ar1[4 + p // 4][:, 1024 * (p % 4):1024 * (p % 4) + 1024]
    oTb = lambda p: ar1[6 + p // 4][:, 1024 * (p % 4):1024 * (p % 4) + 1024]
    ktok = lambda t: ar2[t // 4][:, 1024 * (t % 4):1024 * (t % 4) + 1024]
    vtok = lambda t: ar2[2 + (t // 4)][:, 1024 * (t % 4):1024 * (t % 4) + 1024]
    gateb = lambda p: ar3[p // 4][:, 1024 * (p % 4):1024 * (p % 4) + 1024]
    fmisc = ar3[2][:].bitcast(F32)
    ftok = lambda t: fmisc[:, 32 * t:32 * t + 16]
    lbtok = lambda t: fmisc[:, 32 * t + 16:32 * t + 32]
    lnbT = fmisc[0:16, 256:1280]
    Fch = fmisc[0:64, 1280:1296]
    FTch = fmisc[0:16, 1296:1360]
    GTch = fmisc[0:16, 1360:1424]
    wendT = ar3[2][0:16, 2848:2912]
    dTcol = fmisc[0:16, 1488:1489]
    EGch = ar3[2][0:16, 3008:3072]
    eFtok = fmisc[0:64, 1568:1584]
    eFT2 = fmisc[0:2, 1584:1648]
    sss = ar4[0][:].bitcast(F32)
    kwb = ar4[1][0:64, 0:1024]
    vwb = ar4[1][0:64, 1024:2048]
    ktil = ar4[1][0:64, 2048:3072]
    lambBIG = ar4[1][:, 3584:4096]
    cfBIG = ar3[3][:, 0:512]
    vtil = ar3[3][0:64, 512:1536]
    zqs = ar3[3][:, 1536:2048]
    kch = ar3[3][0:64, 2048:3072]
    vch = ar3[3][0:64, 3072:4096]
    fch0 = fmisc[0:64, 1600:1616]
    lhcg = ctxp[:, 0:1024]
    lhO = ctxp[:, 1024:2048]
    lhK = ctxp[:, 2048:3072]
    lhV = ctxp[:, 3072:4096]
    cgAp = tmpd[:, 1024:1536]
    cgX = pax0[:, 0:512]
    cgR = pax1[:, 0:512]
    cgP = qg2[:, 0:512]
    Gext = qg[:, 0:512]
    gscr = qg[:, 512:1024]
    zsq = tmpa[:].bitcast(BF16)[:, 0:512]
    tmp1 = tmpa[:].bitcast(BF16)[:, 512:1024]
    tmp2 = tmpa[:].bitcast(BF16)[:, 1024:1536]
    tmp3 = tmpa[:].bitcast(BF16)[:, 1536:2048]
    rwrs = tmpb[0:2, 0:512]
    rwpap = tmpb[0:2, 512:1024]
    rwa = tmpc[0:2, 0:512]
    rwb = tmpc[0:2, 512:1024]
    rwt = fmisc[0:2, 1616:2128]

    def tp_pe(dst, srcap, f32=False, waits=(), waits_cp=()):
        P, Fd = srcap.shape[0], srcap.shape[1]
        if f32:
            tq = transpose(pb[6][0:Fd, 0:P], srcap, sb["ident32"][0:P, 0:P],
                           waits=waits, bank=6)
            tc = cp(VE, dst, pb[6][0:Fd, 0:P], waits=(tq,) + tuple(waits_cp))
            rdbank(6, tc)
        else:
            tq = transpose(pbf[0:Fd, 0:P], srcap, sb["identb"][0:P, 0:P],
                           waits=waits, bank=7)
            tc = cp(VE, dst, pbf[0:Fd, 0:P], waits=(tq,) + tuple(waits_cp))
            rdbank(7, tc)
        return tc

    sch.do(VE, lambda e: e.memset(sss[:, 0:1024], 0.0), sig="vs")
    tob = sch.do(VE, lambda e: e.memset(tmpd[:, 0:64], 1.0), sig="vs")
    for p in range(8):
        ts_(lambBIG[:, 64 * p:64 * p + 64], tmpd[:, 0:64], sb["lambcol"][:, p:p + 1],
            None, ALU.mult, waits=(tob,))
    dma(GP, lhK[64:128, :], inn["i64rep"], sig="dg")
    dma(GP, lhV[64:128, :], inn["i64rep"], sig="dg")
    sch.do(VE, lambda e: e.memset(tmpd[64:128, 0:512], 1.0), sig="vs")

    def ln3_half(half):
        prev_ra = None
        prev_rc = None
        t2 = None
        for sub in range(2):
            col0 = 1024 * half + 512 * sub
            m4 = m5 = None
            for dt in range(8):
                wl = () if prev_ra is None else (prev_ra,)
                tl = dma(GP, tmpa[:, 0:512],
                         xresB.ap()[128 * dt:128 * (dt + 1), col0:col0 + 512],
                         waits=wl, sig="dg")
                m4 = mm(pb[4][0:1, 0:512], sb["onescolf"][:], tmpa[:, 0:512],
                        dt == 0, dt == 7, waits=(tl,), bank=4 if dt == 0 else None)
                wq = (m5,) if m5 is not None else ((prev_rc,) if prev_rc else ())
                tq = act(tmpc[:, 0:512], tmpa[:, 0:512], AF.Square, waits=(tl,) + wq)
                prev_ra = tq
                m5 = mm(pb[5][0:1, 0:512], sb["onescolf"][:], tmpc[:, 0:512],
                        dt == 0, dt == 7, waits=(tq,), bank=5 if dt == 0 else None)
                prev_ra = m5
            tm = ts_(sb["rowf"][0:1, 0:512], pb[4][0:1, 0:512], 1.0 / 1024, None,
                     ALU.mult, waits=(m4,))
            rdbank(4, tm)
            te = ts_(tmpc[0:1, 0:512], pb[5][0:1, 0:512], 1.0 / 1024, None, ALU.mult,
                     waits=(m5,))
            rdbank(5, te)
            tt(tmpc[0:1, 512:1024], sb["rowf"][0:1, 0:512], sb["rowf"][0:1, 0:512],
               ALU.mult)
            tt(tmpc[0:1, 512:1024], tmpc[0:1, 0:512], tmpc[0:1, 512:1024], ALU.subtract)
            tv3 = ts_(tmpc[0:1, 512:1024], tmpc[0:1, 512:1024], 1e-5, None, ALU.add)
            tv4 = act(tmpc[0:1, 0:512], tmpc[0:1, 512:1024], AF.Sqrt, waits=(tv3,))
            tv5 = sch.do(VE, lambda e: e.reciprocal(out=tmpc[0:1, 512:1024],
                                                    in_=tmpc[0:1, 0:512]),
                         waits=(tv4,), sig="vs")
            tmm4 = mm(pb[4][:, 0:512], sb["onesrowf"][:], sb["rowf"][0:1, 0:512],
                      True, True, waits=(tm,), bank=4)
            tmm5 = mm(pb[5][:, 0:512], sb["onesrowf"][:], tmpc[0:1, 512:1024], True, True,
                      waits=(tv5,), bank=5)
            for dt in range(8):
                tl = dma(GP, tmpa[:, 0:512],
                         xresB.ap()[128 * dt:128 * (dt + 1), col0:col0 + 512],
                         waits=(prev_ra,), sig="dg")
                t1 = tt(tmpc[:, 0:512], tmpa[:, 0:512], pb[4][:, 0:512], ALU.subtract,
                        waits=(tl, tmm4, tmm5) if dt == 0 else (tl,))
                prev_ra = t1
                t2 = tt(lnb3(dt)[:, 512 * sub:512 * sub + 512], tmpc[:, 0:512],
                        pb[5][:, 0:512], ALU.mult)
                if dt == 7:
                    rdbank(4, t2)
                    rdbank(5, t2)
            prev_rc = t2
        return t2

    wsrd = {"a": None, "b": None}

    def wload(wname):
        wa = (wsrd["a"],) if wsrd["a"] is not None else ()
        wb = (wsrd["b"],) if wsrd["b"] is not None else ()
        da = dma(GP, wstr[:, 0:4096].rearrange("p (k m) -> p k m", k=8),
                 inn[wname].rearrange("(k p) m -> p k m", p=128)[:, :, 0:512],
                 waits=wa, sig="dg")
        db = dma(GP, wstr2[:, 0:4096].rearrange("p (k m) -> p k m", k=8),
                 inn[wname].rearrange("(k p) m -> p k m", p=128)[:, :, 512:1024],
                 waits=wb, sig="dg")
        return da, db

    def proj_feat(wname, dst_fn, l2n, lnb_ready=None):
        da, db = wload(wname)
        prev_t4 = None
        t6 = None
        for p in range(8):
            wv = wstr if p < 4 else wstr2
            mc = 128 * (p % 4)
            for nch in range(2):
                mlast = None
                w0 = (da, db) + ((lnb_ready,) if (p == 0 and nch == 0 and
                                                  lnb_ready is not None) else ())
                for k in range(8):
                    mlast = mm(pb[0][:, 0:512], wv[:, 512 * k + mc:512 * k + mc + 128],
                               lnb3(k)[:, 512 * nch:512 * nch + 512], k == 0, k == 7,
                               waits=w0 if k == 0 else (), bank=0)
                wsrd["a" if p < 4 else "b"] = mlast
                if l2n:
                    tq = act(tmp1, pb[0][:, 0:512], AF.Square,
                             waits=(mlast,) + ((t6,) if t6 is not None else ()))
                    tn = mm(pb[1][0:2, 0:512], sb["hsum2"][:], tmp1, True, True,
                            waits=(tq,), bank=1)
                    t1 = ts_(sb["rowf"][0:2, 0:512], pb[1][0:2, 0:512], 1e-6, None,
                             ALU.add, waits=(tn,) + ((prev_t4,) if prev_t4 else ()))
                    rdbank(1, t1)
                    t2 = act(tmpc[0:2, 0:512], sb["rowf"][0:2, 0:512], AF.Sqrt,
                             waits=(t1,))
                    t3 = sch.do(VE, lambda e: e.reciprocal(
                        out=sb["rowf"][0:2, 0:512], in_=tmpc[0:2, 0:512]),
                        waits=(t2,), sig="vs")
                    t4 = mm(pb[2][:, 0:512], sb["blocksel"][:], sb["rowf"][0:2, 0:512],
                            True, True, waits=(t3,), bank=2)
                    prev_t4 = t4
                    t5 = act(tmp1, pb[0][:, 0:512], AF.Copy)
                    rdbank(0, t5)
                    t6 = tt(dst_fn(p)[:, 512 * nch:512 * nch + 512], tmp1, pb[2][:, 0:512],
                            ALU.mult, waits=(t5, t4))
                    rdbank(2, t6)
                else:
                    t6 = act(dst_fn(p)[:, 512 * nch:512 * nch + 512], pb[0][:, 0:512],
                             AF.Copy, waits=(mlast,))
                    rdbank(0, t6)
        return t6

    def attnb_half_setup(half):
        t2l = ln3_half(half)
        t6k = proj_feat("wbk", kTb, True, lnb_ready=t2l)
        proj_feat("wbq", qTb, True)
        da, db = wload("wbv")
        m7 = None
        for t in range(8):
            for nh, wv in ((0, wstr), (1, wstr2)):
                for k in range(8):
                    m7 = mm(pb[0][:, 0:512], lnb3(k)[:, 128 * t:128 * t + 128],
                            wv[:, 512 * k:512 * k + 512], k == 0, k == 7,
                            waits=(da, db) if k == 0 else (), bank=0)
                wsrd["a" if nh == 0 else "b"] = m7
                t5 = act(vtok(t)[:, 512 * nh:512 * nh + 512], pb[0][:, 0:512], AF.Copy,
                         waits=(m7,))
                rdbank(0, t5)
        first = True
        for p in range(8):
            for j in range(8):
                tp_pe(ktok(j)[:, 128 * p:128 * p + 128], kTb(p)[:, 128 * j:128 * j + 128],
                      waits=(t6k,) if first else ())
                first = False
        dwf = dma(GP, wstr[:, 0:256].rearrange("p (k m) -> p k m", k=8),
                  inn["wfb"].rearrange("(k p) m -> p k m", p=128),
                  waits=(wsrd["a"],) if wsrd["a"] is not None else (), sig="dg")
        tz1p = None
        tz5 = None
        for t in range(8):
            mf = None
            for k in range(8):
                w0 = (dwf,) + ((tz1p,) if tz1p is not None else ())
                mf = mm(pb[0][:, 0:32], lnb3(k)[:, 128 * t:128 * t + 128],
                        wstr[:, 32 * k:32 * k + 32], k == 0, k == 7,
                        waits=w0 if k == 0 else (), bank=0)
            wsrd["a"] = mf
            tz1 = tt(tmpc[:, 0:16], pb[0][:, 0:16], sb["deltarep"][:], ALU.add,
                     waits=(mf,) + ((tz5,) if tz5 is not None else ()))
            tz1p = tz1
            tz2 = act(tmpc[:, 16:32], tmpc[:, 0:16], AF.Sigmoid, waits=(tz1,))
            tz3 = act(ftok(t), tmpc[:, 16:32], AF.Ln)
            tz4 = act(tmpc[:, 32:48], pb[0][:, 16:32], AF.Sigmoid)
            rdbank(0, tz4)
            tz5 = act(lbtok(t), tmpc[:, 32:48], AF.Ln)
        first = True
        for t in range(8):
            tp_pe(lnbT[0:16, 128 * t:128 * t + 128], lbtok(t), f32=True,
                  waits=(tz5,) if first else ())
            first = False

    def chunk_F(ci):
        t8, r0 = ci // 2, 64 * (ci % 2)
        ccol = slice(64 * ci, 64 * ci + 64)
        cp(VE, kch[:, 0:1024], ktok(t8)[r0:r0 + 64, 0:1024])
        cp(VE, vch[:, 0:1024], vtok(t8)[r0:r0 + 64, 0:1024])
        tfc = cp(VE, fch0[:, 0:16], ftok(t8)[r0:r0 + 64, 0:16])
        tF = mm(pb[3][0:64, 0:16], sb["lt64"][:], fch0[:, 0:16], True, True,
                waits=(tfc,), bank=3)
        tFe = cp(VE, Fch, pb[3][0:64, 0:16], waits=(tF,))
        rdbank(3, tFe)
        tFT = tp_pe(FTch, Fch, f32=True, waits=(tFe,))
        tGT = tt(GTch, lnbT[0:16, ccol], FTch, ALU.subtract, waits=(tFT,))
        twe = act(wendT, GTch, AF.Exp, bias=FTch[0:16, 63:64], waits=(tGT,))
        tdc = act(dTcol, FTch[0:16, 63:64], AF.Exp, waits=(tFT,))
        teg = act(EGch, GTch, AF.Exp, waits=(tGT,))
        tef = act(eFtok, Fch, AF.Exp, waits=(tFe,))
        tw1 = mm(pb[3][0:64, 0:512], wendT, sb["e16"][0:16, 0:512], True, True,
                 waits=(twe,), bank=3)
        tw2 = mm(pb[2][0:64, 0:512], wendT, sb["e16"][0:16, 512:1024], True, True,
                 waits=(twe,), bank=2)
        tk1 = tt(kwb[:, 0:512], kch[:, 0:512], pb[3][0:64, 0:512], ALU.mult,
                 waits=(tw1,))
        tk2 = tt(kwb[:, 512:1024], kch[:, 512:1024], pb[2][0:64, 0:512],
                 ALU.mult, waits=(tw2,))
        tv1 = tt(vwb[:, 0:512], vch[:, 0:512], pb[3][0:64, 0:512], ALU.mult,
                 waits=(tw1,))
        tv2 = tt(vwb[:, 512:1024], vch[:, 512:1024], pb[2][0:64, 0:512],
                 ALU.mult, waits=(tw2,))
        rdbank(3, tv1)
        rdbank(2, tv2)
        return t8, r0, (tdc, tef, teg, tk2, tv2)

    def chunk_state(t8, r0, toks):
        tdc = toks[0]
        mp = None
        for p in range(8):
            mp = mm(pb[4][:, p:p + 1], sb["pairsel"][:, 128 * p:128 * p + 128], dTcol,
                    p == 0, p == 7, waits=(tdc,), bank=4 if p == 0 else None)
        tdr = cp(VE, sb["statc"][:, 0:8], pb[4][:, 0:8], waits=(mp,))
        rdbank(4, tdr)
        msl = None
        for p in range(8):
            pbank, bk = (pb[5], 5) if p < 4 else (pb[6], 6)
            pcol = 128 * (p % 4)
            for hh in range(2):
                h = 2 * p + hh
                col = slice(64 * h, 64 * h + 64)
                mm(pbank[64 * hh:64 * hh + 64, pcol:pcol + 64],
                   kch[:, col], kwb[:, col],
                   True, True, tp=(0, 64 * hh),
                   waits=(toks[4],) if (p == 0 and hh == 0) else (),
                   bank=bk if (p % 4 == 0 and hh == 0) else None)
                msl = mm(pbank[64 * hh:64 * hh + 64, pcol + 64:pcol + 128],
                         kch[:, col], vwb[:, col],
                         True, True, tp=(0, 64 * hh))
        tup = None
        for p in range(8):
            pbank = pb[5] if p < 4 else pb[6]
            pcol = 128 * (p % 4)
            tup = stt(sss[:, 128 * p:128 * p + 128], sss[:, 128 * p:128 * p + 128],
                      sb["statc"][:, p:p + 1], pbank[:, pcol:pcol + 128], ALU.mult, ALU.add,
                      waits=(tdr, msl) if p == 0 else ())
        rdbank(5, tup)
        rdbank(6, tup)
        return tup

    def chunk_cg_o(ci, t8, r0, toks):
        ccol = slice(64 * ci, 64 * ci + 64)
        tef, teg = toks[1], toks[2]
        te1 = mm(pb[3][0:64, 0:512], EGch, sb["e16"][0:16, 0:512], True, True,
                 waits=(teg,), bank=3)
        te2 = mm(pb[2][0:64, 0:512], EGch, sb["e16"][0:16, 512:1024], True, True,
                 waits=(teg,), bank=2)
        tt(ktil[:, 0:512], kch[:, 0:512], pb[3][0:64, 0:512], ALU.mult, waits=(te1,))
        tk = tt(ktil[:, 512:1024], kch[:, 512:1024], pb[2][0:64, 0:512],
                ALU.mult, waits=(te2,))
        tt(vtil[:, 0:512], vch[:, 0:512], pb[3][0:64, 0:512], ALU.mult)
        tv = tt(vtil[:, 512:1024], vch[:, 512:1024], pb[2][0:64, 0:512],
                ALU.mult)
        rdbank(3, tv)
        rdbank(2, tv)
        tcf = None
        for p in range(8):
            tpe = tp_pe(eFT2, eFtok[0:64, 2 * p:2 * p + 2], f32=True, waits=(tef,),
                        waits_cp=(tcf,) if tcf is not None else ())
            tcf = mm(pb[4][:, 64 * p:64 * p + 64], sb["blocksel"][:], eFT2, True, True,
                     waits=(tpe, tef), bank=4 if p == 0 else None)
        tcfe = cp(VE, cfBIG, pb[4][:, 0:512], waits=(tcf,))
        rdbank(4, tcfe)
        a1 = cp(VE, lhK[0:64, :], ktil[:, 0:1024], waits=(tk,))
        a2 = cp(VE, lhV[0:64, :], vtil[:, 0:1024], waits=(tv,))
        for p in range(8):
            cp(VE, lhcg[:, 128 * p:128 * p + 64], kTb(p)[:, ccol])
            cp(VE, lhcg[:, 128 * p + 64:128 * p + 128], sss[:, 128 * p:128 * p + 64])
            cp(VE, lhO[:, 128 * p:128 * p + 64], kTb(p)[:, ccol])
            a3 = cp(VE, lhO[:, 128 * p + 64:128 * p + 128],
                    sss[:, 128 * p + 64:128 * p + 128])
        sch.do(VE, lambda e: e.memset(cgX[:], 0.0), sig="vs")
        for p in range(8):
            cp(VE, cgR[:, 64 * p:64 * p + 64], qTb(p)[:, ccol])
        ti = cp(VE, cgP[:], cgR[:])
        tq = act(zsq, cgR[:], AF.Square, waits=(ti,))
        tn = mm(pb[3][0:2, 0:512], sb["hsum2"][:], zsq, True, True, waits=(tq,), bank=3)
        tr0 = cp(VE, rwrs, pb[3][0:2, 0:512], waits=(tn,))
        rdbank(3, tr0)

        def Amul(pvec, dst_psumA, dst_psumB, dst_ap, lhs1, lhsk, ready=()):
            ma = mb = None
            for p in range(8):
                ma = mm(dst_psumA[:, 64 * p:64 * p + 64], lhs1[0:64, 128 * p:128 * p + 128],
                        pvec[0:64, 64 * p:64 * p + 64], True, True, tp=(0, 0),
                        waits=ready if p == 0 else (), bank=0 if p == 0 else None)
                mb = mm(dst_psumB[:, 64 * p:64 * p + 64], lhs1[64:128, 128 * p:128 * p + 128],
                        pvec[64:128, 64 * p:64 * p + 64], True, True, tp=(64, 0),
                        bank=1 if p == 0 else None)
            g1 = tt(Gext, sb["upperext"][:], dst_psumA[:, 0:512], ALU.mult, waits=(ma,))
            rdbank(0, g1)
            g2 = tt(gscr, sb["upperext"][:], dst_psumB[:, 0:512], ALU.mult, waits=(mb,))
            rdbank(1, g2)
            mk = None
            for p in range(8):
                for hh in range(2):
                    h = 2 * p + hh
                    gsrc = Gext if hh == 0 else gscr
                    mk = mm(pb[2][64 * hh:64 * hh + 64, 64 * p:64 * p + 64],
                            lhsk[:, 64 * h:64 * h + 64],
                            gsrc[:, 64 * p:64 * p + 64], True, True, tp=(0, 64 * hh),
                            waits=(g1, g2) if (p == 0 and hh == 0) else (),
                            bank=2 if (p == 0 and hh == 0) else None)
            m1 = tt(tmp1, cfBIG, pb[2][:, 0:512], ALU.mult, waits=(mk,))
            rdbank(2, m1)
            m2 = tt(tmp2, pvec[:], lambBIG, ALU.mult)
            return tt(dst_ap, tmp1, tmp2, ALU.add, waits=(m1, m2))

        u2 = u6 = None
        prev_tn = tn
        for it in range(30):
            tap_ = Amul(cgP, pb[0], pb[1], cgAp, lhcg, lhK,
                        ready=(ti,) if it == 0 else (u6,))
            z1 = tt(zsq, cgP[:], cgAp, ALU.mult, waits=(tap_, prev_tn))
            tn1 = mm(pb[3][0:2, 0:512], sb["hsum2"][:], zsq, True, True, waits=(z1,),
                     bank=3)
            tp1 = ts_(rwpap, pb[3][0:2, 0:512], 1e-12, None, ALU.add, waits=(tn1,))
            rdbank(3, tp1)
            ta0 = sch.do(VE, lambda e: e.reciprocal(out=rwt, in_=rwpap),
                         waits=(tp1,), sig="vs")
            ta1 = tt(rwa, rwrs, rwt, ALU.mult, waits=(ta0,))
            tb1 = mm(pb[4][:, 0:512], sb["blocksel"][:], rwa, True, True, waits=(ta1,),
                     bank=4)
            u1 = tt(tmp1, cgP[:], pb[4][:, 0:512], ALU.mult, waits=(tb1,))
            u2 = tt(cgX[:], cgX[:], tmp1, ALU.add, waits=(u1,))
            u3 = tt(tmp2, cgAp, pb[4][:, 0:512], ALU.mult)
            rdbank(4, u3)
            u4 = tt(cgR[:], cgR[:], tmp2, ALU.subtract, waits=(u3,))
            q1 = act(zsq, cgR[:], AF.Square, waits=(u4,))
            tn2 = mm(pb[3][0:2, 0:512], sb["hsum2"][:], zsq, True, True, waits=(q1,),
                     bank=3)
            prev_tn = tn2
            tp2 = ts_(rwt, rwrs, 1e-12, None, ALU.add)
            tz2 = sch.do(VE, lambda e: e.reciprocal(out=rwa, in_=rwt),
                         waits=(tp2, tb1), sig="vs")
            tb2 = tt(rwb, pb[3][0:2, 0:512], rwa, ALU.mult, waits=(tz2, tn2))
            tr2 = cp(VE, rwrs, pb[3][0:2, 0:512], waits=(tb2,))
            rdbank(3, tr2)
            tb3 = mm(pb[4][:, 0:512], sb["blocksel"][:], rwb, True, True, waits=(tb2,),
                     bank=4)
            u5 = tt(tmp3, cgP[:], pb[4][:, 0:512], ALU.mult, waits=(tb3,))
            rdbank(4, u5)
            u6 = tt(cgP[:], cgR[:], tmp3, ALU.add, waits=(u5,))
        to = Amul(cgX, pb[0], pb[1], cgAp, lhO, lhV, ready=(u2,))
        o1 = tt(zsq, cfBIG, pb[2][:, 0:512], ALU.mult)
        rdbank(2, o1)
        for p in range(8):
            oc = cp(VE, oTb(p)[:, ccol], zsq[:, 64 * p:64 * p + 64], waits=(o1,))
        return oc

    def attnb_gate_out(half):
        dg1 = dma(GP, wstr[:, 0:512].rearrange("p (k m) -> p k m", k=8),
                  inn["gw1"].rearrange("(k p) m -> p k m", p=128),
                  waits=(wsrd["a"],) if wsrd["a"] is not None else (), sig="dg")
        tsg = None
        for nch in range(2):
            mg = None
            for k in range(8):
                mg = mm(pb[0][0:64, 0:512], wstr[:, 64 * k:64 * k + 64],
                        lnb3(k)[:, 512 * nch:512 * nch + 512], k == 0, k == 7,
                        waits=(dg1,) if k == 0 else (), bank=0)
            wsrd["a"] = mg
            tg = act(tmpd[0:64, 0:512], pb[0][0:64, 0:512], AF.Copy, waits=(mg,))
            rdbank(0, tg)
            dg2 = dma(GP, wstr2[0:64, 0:1024].rearrange("p (k m) -> p k m", k=1),
                      inn["gw2"].rearrange("(k p) m -> p k m", p=64),
                      waits=(wsrd["b"],) if wsrd["b"] is not None else (), sig="dg")
            for p in range(8):
                mg2 = mm(pb[1][:, 0:512], wstr2[0:64, 128 * p:128 * p + 128],
                         tmpd[0:64, 0:512], True, True, waits=(dg2, tg), bank=1)
                wsrd["b"] = mg2
                tsg = act(gateb(p)[:, 512 * nch:512 * nch + 512], pb[1][:, 0:512],
                          AF.Sigmoid, waits=(mg2,))
                rdbank(1, tsg)
        t6 = t7 = None
        prev_t5 = None
        for p in range(8):
            for nch in range(2):
                sl = slice(512 * nch, 512 * nch + 512)
                t1 = tt(tmp1, oTb(p)[:, sl], gateb(p)[:, sl], ALU.mult,
                        waits=(tsg,) if (p == 0 and nch == 0) else ())
                tq = act(tmp2, tmp1, AF.Square, waits=(t1,))
                tn = mm(pb[1][0:2, 0:512], sb["hsum2"][:], tmp2, True, True, waits=(tq,),
                        bank=1)
                t2 = ts_(sb["rowf"][0:2, 0:512], pb[1][0:2, 0:512], 1.0 / 64, 1e-5,
                         ALU.mult, ALU.add,
                         waits=(tn,) + ((prev_t5,) if prev_t5 is not None else ()))
                rdbank(1, t2)
                t3 = act(tmpc[0:2, 0:512], sb["rowf"][0:2, 0:512], AF.Sqrt, waits=(t2,))
                t4 = sch.do(VE, lambda e: e.reciprocal(out=sb["rowf"][0:2, 0:512],
                                                       in_=tmpc[0:2, 0:512]),
                            waits=(t3,), sig="vs")
                t5 = mm(pb[2][:, 0:512], sb["blocksel"][:], sb["rowf"][0:2, 0:512],
                        True, True, waits=(t4,), bank=2)
                prev_t5 = t5
                t6 = tt(tmp3, tmp1, pb[2][:, 0:512], ALU.mult, waits=(t5,))
                rdbank(2, t6)
                t7 = ts_(oTb(p)[:, sl], tmp3, sb["nrmw"][:, p:p + 1], None, ALU.mult,
                         waits=(t6,))
        da, db = wload("wbo")
        prev_t1 = None
        prev_odma = None
        for mo in range(8):
            for nch in range(2):
                mw = None
                for k in range(8):
                    wv = wstr if mo < 4 else wstr2
                    mc = 128 * (mo % 4)
                    w0 = (da, db) + ((t7,) if (mo == 0 and nch == 0) else ())
                    mw = mm(pb[0][:, 0:512], wv[:, 512 * k + mc:512 * k + mc + 128],
                            oTb(k)[:, 512 * nch:512 * nch + 512], k == 0, k == 7,
                            waits=w0 if k == 0 else (), bank=0)
                wsrd["a" if mo < 4 else "b"] = mw
                col0 = 1024 * half + 512 * nch
                wl = (t6,) if prev_t1 is None else (prev_t1,)
                tl = dma(GP, tmpa[:, 0:512],
                         xresB.ap()[128 * mo:128 * (mo + 1), col0:col0 + 512],
                         waits=wl, sig="dg")
                t1 = tt(tmpc[:, 0:512], pb[0][:, 0:512], tmpa[:, 0:512], ALU.add,
                        waits=(tl, mw) + ((prev_odma,) if prev_odma is not None else ()))
                prev_t1 = t1
                rdbank(0, t1)
                prev_odma = dma(GP, xresA.ap()[128 * mo:128 * (mo + 1), col0:col0 + 512],
                                tmpc[:, 0:512], waits=(t1,), sig="dg")

    sch.serial = True
    sch.skip_same = True
    for half in range(2):
        sch.phase = f"p3a{half}"
        sch.barrier()
        attnb_half_setup(half)
        for ci in range(16):
            sch.barrier()
            t8, r0, toks = chunk_F(ci)
            chunk_state(t8, r0, toks)
        if half == 1:
            sch.barrier()
            cpv = cp(VE, tmpc[:, 0:1024], sss[:, 0:1024])
            dsend = dma(GP, pkg_send.ap(), tmpc[:, 0:1024], waits=(cpv,), sig="dg")
            def cc_ag(e):
                return e.collective_compute("AllGather", ALU.bypass,
                                            replica_groups=[list(range(8))],
                                            ins=[pkg_send.ap()], outs=[pkg_recv.ap()])
            tcc = sch.do(GP, cc_ag, waits=(dsend,), sig="cc", n=1)
            tz = sch.do(VE, lambda e: e.memset(sss[:, 0:1024], 0.0), waits=(tcc,), sig="vs")
            for j in range(4):
                tl = dma(GP, tmpc[:, 0:1024],
                         pkg_recv.ap()[128 * 2 * j:128 * 2 * j + 128, :],
                         waits=(tcc,), sig="dg")
                tz = stt(sss[:, 0:1024], tmpc[:, 0:1024], sb["gamrep"][:, 2 + j:3 + j],
                         sss[:, 0:1024], ALU.mult, ALU.add, waits=(tl, tz))
    sch.serial = True
    for half in range(2):
        sch.phase = f"p3b{half}"
        sch.barrier()
        attnb_half_setup(half)
        for ci in range(16):
            sch.barrier()
            t8, r0, toks = chunk_F(ci)
            chunk_cg_o(ci, t8, r0, toks)
            chunk_state(t8, r0, toks)
        sch.barrier()
        attnb_gate_out(half)
    tok_ab = sch.barrier()
    if upto == "attnb":
        tap_x4 = taps.get("t_x4")
        if tap_x4 is not None:
            for dt_ in range(8):
                dma(GP, tap_x4[128 * dt_:128 * (dt_ + 1), :],
                    xresA.ap()[128 * dt_:128 * (dt_ + 1), :], sig="dg")
        return _finish_early()

    phase_fence()
    sch.serial = serial
    sch.skip_same = False
    sch.phase = "f2"
    tok_ffn2 = ffn_pass(xresA, xresB)

    sch.phase = "o"
    carry = {"l": None, "e": None}
    for it in range(16):
        for dt_ in range(8):
            bk = 2 + (dt_ % 2)
            tl = dma(GP, tmpa[:, 0:128],
                     xresB.ap()[128 * dt_:128 * (dt_ + 1), 128 * it:128 * it + 128],
                     waits=(carry["l"],), sig="dg")
            psl = pb[bk][:, 0:128]
            ttp = transpose(psl, tmpa[:, 0:128], sb["ident32"][:], waits=(tl,), bank=bk)
            carry["l"] = ttp
            tev = cp(VE, tmpb[:, 0:128], psl, waits=(ttp, carry["e"]))
            rdbank(bk, tev)
            tdm = dma(GP, out_own[128 * it:128 * (it + 1), 128 * dt_:128 * (dt_ + 1)],
                      tmpb[:, 0:128], waits=(tev,), sig="dg")
            carry["e"] = tdm

    end_tok = sch.barrier()
    final_waits = [(k, v) for k, v in sch.counts.items()
                   if k.startswith(("ds", "dg"))] + [end_tok]
    sch.do(SY, lambda e: e.nop(), waits=tuple(final_waits))
    print('SEM COUNTS:', {k: v for k, v in sch.counts.items()})
    sch.finalize()
    return nc


def kernel(**inputs):
    maps = _prep_inputs(inputs)
    nc = build_program()
    res = run_bass_kernel_spmd(nc, maps, core_ids=list(range(8)))
    out = np.zeros((B, S, D), np.float32)
    for c in range(8):
        b, r = c // 2, c % 2
        out[b, 2048 * r:2048 * (r + 1)] = np.asarray(res.results[c]["out_own"])
    return out
